# version 31 (final)
# baseline (speedup 1.0000x reference)
import numpy as np
import ml_dtypes
import contextlib

DEBUG_C = False
DEBUG_IO = False
import concourse.bass as bass
import concourse.mybir as mybir
from concourse.bass_utils import run_bass_kernel_spmd

F32 = mybir.dt.float32
BF16 = mybir.dt.bfloat16
AF = mybir.ActivationFunctionType
ALU = mybir.AluOpType
AX = mybir.AxisListType

L = 4096
D = 1024
NT = 32
NG = 8
GT = 4
H = 16
DH = 64
EPS = 1e-5

ENGS = ("sync", "scalar", "vector", "tensor", "gpsimd")


class Chan:
    __slots__ = ("sem", "n", "name")

    def __init__(self, sem, name):
        self.sem = sem
        self.n = 0
        self.name = name


class PBank:
    def __init__(self, t):
        self.t = t
        self.last = None

    def acquire(self):
        return [self.last] if self.last else []

    def consumed(self, chan, val):
        self.last = (chan, val)


class Sched:
    def __init__(self, nc, stack):
        self.nc = nc
        self.stack = stack
        self.ops = {e: [] for e in ENGS}
        self.hwm = {e: {} for e in ENGS}

    def chan(self, name):
        sem = self.stack.enter_context(self.nc.semaphore(name))
        return Chan(sem, name)

    def op(self, eng, fn, waits=(), inc=None, inc_n=1):
        rw = []
        for w in waits:
            if w is None:
                continue
            ch, v = w
            if v is None or v <= 0:
                continue
            if self.hwm[eng].get(id(ch), 0) >= v:
                continue
            self.hwm[eng][id(ch)] = v
            rw.append((ch, v))
        tgt = None
        if inc is not None:
            inc.n += inc_n
            tgt = inc.n
        self.ops[eng].append((rw, fn, inc, inc_n))
        return tgt

    def emit(self):
        with self.nc.Block() as block:
            for name in ENGS:
                ops = self.ops[name]

                def mk(ops=ops):
                    def run(e):
                        for (rw, fn, inc, inc_n) in ops:
                            for ch, v in rw:
                                e.wait_ge(ch.sem, v)
                            inst = fn(e)
                            if inc is not None:
                                inst.then_inc(inc.sem, inc_n)
                    return run
                getattr(block, name)(mk())


def build_program():
    nc = bass.Bass("TRN2", target_bir_lowering=False)
    stack = contextlib.ExitStack()

    x_d = nc.dram_tensor("x", [L, D], F32, kind="ExternalInput").ap()
    wq_d = nc.dram_tensor("wq", [D, D], F32, kind="ExternalInput").ap()
    wk_d = nc.dram_tensor("wk", [D, D], F32, kind="ExternalInput").ap()
    wv_d = nc.dram_tensor("wv", [D, D], F32, kind="ExternalInput").ap()
    wo_d = nc.dram_tensor("wo", [D, D], F32, kind="ExternalInput").ap()
    wr_d = nc.dram_tensor("wr", [D, D], F32, kind="ExternalInput").ap()
    wfk_d = nc.dram_tensor("wfk", [D, D], F32, kind="ExternalInput").ap()
    wfv_d = nc.dram_tensor("wfv", [D, D], F32, kind="ExternalInput").ap()
    gam_d = nc.dram_tensor("gam", [D], F32, kind="ExternalInput").ap()
    bet_d = nc.dram_tensor("bet", [D], F32, kind="ExternalInput").ap()
    bq_d = nc.dram_tensor("bq", [1, D], F32, kind="ExternalInput").ap()
    bk_d = nc.dram_tensor("bk", [1, D], F32, kind="ExternalInput").ap()
    bv_d = nc.dram_tensor("bv", [1, D], F32, kind="ExternalInput").ap()
    bo_d = nc.dram_tensor("bo", [1, D], F32, kind="ExternalInput").ap()
    mur_d = nc.dram_tensor("mur", [D], F32, kind="ExternalInput").ap()
    muk_d = nc.dram_tensor("muk", [D], F32, kind="ExternalInput").ap()
    tab_d = nc.dram_tensor("tab", [L, 1536], F32, kind="ExternalInput").ap()
    id_d = nc.dram_tensor("ident", [128, 128], BF16, kind="ExternalInput").ap()
    mask_d = nc.dram_tensor("maskt", [128, 896], BF16, kind="ExternalInput").ap()
    out_d = nc.dram_tensor("out", [L, D], F32, kind="ExternalOutput").ap()

    _sk = "ExternalOutput" if DEBUG_IO else "Internal"
    qtr_d = nc.dram_tensor("qtr_s", [128, 8 * L], BF16, kind=_sk).ap()
    ktr_d = nc.dram_tensor("ktr_s", [128, 8 * L], BF16, kind=_sk).ap()
    vau_d = nc.dram_tensor("vau_s", [128, NT * 1040], BF16, kind=_sk).ap()
    atn_d = nc.dram_tensor("atn_s", [128, 8 * L], BF16, kind=_sk).ap()
    wrT_d = nc.dram_tensor("wrT_s", [128, 8192], BF16, kind="Internal").ap()
    ydbg_d = nc.dram_tensor("ydbg_s", [128, NT * 1024], F32, kind="Internal").ap()
    xndbg_d = nc.dram_tensor("xndbg_s", [128, NT * 1024], BF16,
                             kind=("ExternalOutput" if DEBUG_IO else "Internal")).ap()
    rsdbg_d = nc.dram_tensor("rsdbg_s", [128, NT * 4], F32,
                             kind=("ExternalOutput" if DEBUG_IO else "Internal")).ap()
    zdbg_d = nc.dram_tensor("zdbg_s", [128, NT * 1024], F32, kind="Internal").ap()
    wkT_d = nc.dram_tensor("wkT_s", [128, 8192], BF16, kind="Internal").ap()
    wvT_d = nc.dram_tensor("wvT_s", [128, 8192], BF16, kind="Internal").ap()

    DT_SZ = {F32: 4, BF16: 2}

    def mk_region(base):
        cur = [base]

        def alloc(name, shape, dtype):
            off = cur[0]
            nbytes = shape[1] * DT_SZ[dtype]
            cur[0] = (off + nbytes + 31) // 32 * 32
            return nc.alloc_sbuf_tensor_at(name, shape, dtype, offset=off)
        return alloc, cur

    SBUF_LO = 16512
    SBUF_HI = 229344
    sh, sh_cur = mk_region(SBUF_LO)
    id_sb = sh("id_sb", [128, 128], BF16)
    mask_sb = sh("mask_sb", [128, 896], BF16)
    negc_sb = sh("negc_sb", [128, 1], F32)
    gam_sb = sh("gam_sb", [128, 8], F32)
    bet_sb = sh("bet_sb", [128, 8], F32)
    betbf_sb = sh("betbf_sb", [128, 8], BF16)
    mur_sb = sh("mur_sb", [128, 8], F32)
    muk_sb = sh("muk_sb", [128, 8], F32)
    omur_sb = sh("omur_sb", [128, 8], F32)
    omuk_sb = sh("omuk_sb", [128, 8], F32)
    onec_sb = sh("onec_sb", [128, 8], F32)
    ones_sb = sh("ones_sb", [128, 128], BF16)
    qkmax_sb = sh("qkmax_sb", [128, 32], F32)
    nrm1_sb = sh("nrm1_sb", [128, 32], F32)
    nrm2_sb = sh("nrm2_sb", [128, 128], BF16)
    nrm3_sb = sh("nrm3_sb", [128, 4], F32)
    mzc_sb = sh("mzc_sb", [128, 2], BF16)
    eps_sb = sh("eps_sb", [128, 1], F32)
    boh_row = sh("boh_row", [1, 1024], BF16)
    bol_row = sh("bol_row", [1, 1024], BF16)
    bqh_row = sh("bqh_row", [1, 1024], BF16)
    bql_row = sh("bql_row", [1, 1024], BF16)
    bkh_row = sh("bkh_row", [1, 1024], BF16)
    bkl_row = sh("bkl_row", [1, 1024], BF16)
    bvh_row = sh("bvh_row", [1, 1024], BF16)
    bvl_row = sh("bvl_row", [1, 1024], BF16)
    st6a2 = [sh(f"st6a_{i}", [128, 8], F32) for i in range(2)]
    st6b2 = [sh(f"st6b_{i}", [128, 8], F32) for i in range(2)]
    mv1_2 = [sh(f"mv1_{i}", [128, 2], F32) for i in range(2)]
    mv2_2 = [sh(f"mv2_{i}", [128, 2], F32) for i in range(2)]
    mvc_2 = [sh(f"mvc_{i}", [128, 2], F32) for i in range(2)]
    lntmp_2 = [sh(f"lntmp_{i}", [128, 2], F32) for i in range(2)]
    rs_2 = [sh(f"rs_{i}", [128, 1], F32) for i in range(2)]
    row_a = sh("row_a", [1, 1024], F32)
    row_b = sh("row_b", [1, 1024], F32)
    row_c = sh("row_c", [1, 1024], F32)
    R0 = (sh_cur[0] + 31) // 32 * 32

    pa, pa_cur = mk_region(R0)
    wq_sb = pa("wq_sb", [128, 8192], BF16)
    wk_sb = pa("wk_sb", [128, 8192], BF16)
    wv_sb = pa("wv_sb", [128, 8192], BF16)
    wpm_sb = pa("wpm_sb", [128, 8192], BF16)
    xbuf = [pa(f"xbuf{i}", [128, 1024], F32) for i in range(2)]
    tabbuf = [pa(f"tabbuf{i}", [128, 1536], F32) for i in range(2)]
    xnbf = [pa(f"xnbf{i}", [128, 1024], BF16) for i in range(2)]
    xnT = [pa(f"xnT{i}", [128, 1024], BF16) for i in range(2)]
    qkf_q = pa("qkf_q", [128, 1024], F32)
    qkf_k = pa("qkf_k", [128, 1024], F32)
    ropet = {('q', i): pa(f"ropetq{i}", [128, 512], F32) for i in range(4)}
    ropet.update({('k', i): pa(f"ropetk{i}", [128, 512], F32) for i in range(4)})
    qbf_sb = pa("qbf_sb", [128, 1024], BF16)
    kbf_sb = pa("kbf_sb", [128, 1024], BF16)
    sq_sb = pa("sq_sb", [128, 1024], F32)
    sq2_sb = pa("sq2_sb", [128, 1024], F32)
    trev = [pa(f"trev{i}", [128, 1024], BF16) for i in range(2)]
    vab = [pa(f"vab{i}", [128, 1040], BF16) for i in range(2)]
    wtmp_sb = pa("wtmp_sb", [128, 8192], BF16)
    hi_row = pa("hi_row", [1, 1024], BF16)
    lo_row = pa("lo_row", [1, 1024], BF16)
    hif_row = pa("hif_row", [1, 1024], F32)
    assert pa_cur[0] <= SBUF_HI, pa_cur[0]

    pb, pb_cur = mk_region(R0)
    ktr_sb = pb("ktr_sb", [128, 32768], BF16)
    vau_sb = pb("vau_sb", [128, 33280], BF16)
    qtr_sb = [pb(f"qtr_sb{i}", [128, 4096], BF16) for i in range(2)]
    ptb_bufs = [pb(f"ptb{i}", [128, 1024], BF16) for i in range(4)]
    repev_sb = pb("repev_sb", [128, 512], F32)
    recip_sb = pb("recip_sb", [128, 512], F32)
    rcbf_sb = pb("rcbf_sb", [128, 512], BF16)
    atn_sb = pb("atn_sb", [128, 8192], BF16)
    assert pb_cur[0] <= SBUF_HI, pb_cur[0]

    pc, pc_cur = mk_region(R0)
    wo_sb = pc("wo_sb", [128, 8192], BF16)
    wrT_sb = pc("wrT_sb", [128, 8192], BF16)
    wkT_sb = pc("wkT_sb", [128, 8192], BF16)
    wvT_sb = pc("wvT_sb", [128, 8192], BF16)
    atc_sb = [pc(f"atc_sb{i}", [128, 4096], BF16) for i in range(2)]
    xcb = [pc(f"xcb{i}", [128, 1024], F32) for i in range(2)]
    ysb = pc("ysb", [128, 1024], F32)
    z1_sb = pc("z1_sb", [128, 4096], F32)
    z1bf_off = pc_cur[0]
    z1bf_sb = pc("z1bf_sb", [128, 1024], BF16)
    z1bf2_off = pc_cur[0]
    z1bf_b2 = pc("z1bf_b2", [128, 1024], BF16)
    z1T_sb = pc("z1T_sb", [128, 4104], BF16)
    wkTm_sb = pc("wkTm_sb", [128, 8192], BF16)
    wrTm_sb = pc("wrTm_sb", [128, 8192], BF16)
    d1_sb = pc("d1_sb", [128, 4096], BF16)
    kactT_sb = pc("kactT_sb", [128, 4096], BF16)
    kact_t = nc.alloc_sbuf_tensor_at("kact_t", [128, 512], F32, offset=z1bf_off)
    sgt_sb = nc.alloc_sbuf_tensor_at("sgt_sb", [128, 512], F32, offset=z1bf2_off)
    r_sb = pc("r_sb", [128, 1024], F32)
    y2_sb = ysb
    outb0 = pc("outb0", [128, 1024], F32)
    outb = [outb0, outb0]
    assert pc_cur[0] <= SBUF_HI, pc_cur[0]

    ps_s = [nc.alloc_psum_tensor("ps_s0", [128, 1024], F32),
            nc.alloc_psum_tensor("ps_s1", [128, 1024], F32)]
    ps_a = [nc.alloc_psum_tensor("ps_a0", [128, 512], F32),
            nc.alloc_psum_tensor("ps_a1", [128, 512], F32)]
    ps_rp = nc.alloc_psum_tensor("ps_rp", [128, 512], F32)
    ps_tr = nc.alloc_psum_tensor("ps_tr", [128, 1024], BF16)

    sch = Sched(nc, stack)
    cSc = sch.chan("cSc")
    cV = sch.chan("cV")
    cT = sch.chan("cT")
    cSyStart = sch.chan("cSyStart")
    cBo = sch.chan("cBo")
    cWq = sch.chan("cWq"); cWk = sch.chan("cWk"); cWv = sch.chan("cWv")
    cWtmp = sch.chan("cWtmp")
    cWTw = sch.chan("cWTw")
    cX = [sch.chan("cX0"), sch.chan("cX1")]
    cTab = [sch.chan("cTab0"), sch.chan("cTab1")]
    cQw = sch.chan("cQw"); cKw = sch.chan("cKw")
    cVw = [sch.chan("cVw0"), sch.chan("cVw1")]
    cKV = sch.chan("cKV")
    cQtr = [sch.chan("cQtr0"), sch.chan("cQtr1")]
    cAtnW = sch.chan("cAtnW")
    cWoG = sch.chan("cWoG")
    cWc = sch.chan("cWc")
    cAtc = [sch.chan("cAtc0"), sch.chan("cAtc1")]
    cXc = [sch.chan("cXc0"), sch.chan("cXc1")]
    cOut = [sch.chan("cOut0"), sch.chan("cOut1")]
    cP = sch.chan("cP")

    bank_s = [PBank(ps_s[0]), PBank(ps_s[1])]
    bank_a = [PBank(ps_a[0]), PBank(ps_a[1])]
    bank_r = PBank(ps_rp)
    bank_t = PBank(ps_tr)
    ps_tr_f32 = ps_tr[:].bitcast(F32)

    def vself():
        return sch.op("vector", lambda e: e.engine_nop(), inc=cV)

    def vw(val):
        return (cV, val) if val else None

    ld_q = sch.op("gpsimd", lambda e: e.dma_start(
        out=wpm_sb[:].rearrange("p (ci d) -> p ci d", d=1024), in_=wq_d.rearrange("(ci p) d -> p ci d", p=128)), inc=cWq, inc_n=16)
    ld_k = sch.op("gpsimd", lambda e: e.dma_start(
        out=wtmp_sb[:].rearrange("p (ci d) -> p ci d", d=1024), in_=wk_d.rearrange("(ci p) d -> p ci d", p=128)),
        inc=cWk, inc_n=16)
    sch.op("gpsimd", lambda e: e.dma_start(
        out=wv_sb[:].rearrange("p (ci d) -> p ci d", d=1024), in_=wv_d.rearrange("(ci p) d -> p ci d", p=128)), inc=cWv, inc_n=16)
    pq_vals = []
    for ci in range(8):
        pq_vals.append(sch.op("scalar", lambda e, ci=ci: e.copy(
            wq_sb[:, 1024 * ci:1024 * ci + 1024].rearrange(
                "p (h z dd) -> p h z dd", z=2, dd=32),
            wpm_sb[:, 1024 * ci:1024 * ci + 1024].rearrange(
                "p (h dd z) -> p h z dd", z=2, dd=32)),
            waits=[(cWq, ld_q)], inc=cSc))
    perm_q = pq_vals[-1]
    pk_vals = []
    for ci in range(8):
        pk_vals.append(sch.op("scalar", lambda e, ci=ci: e.copy(
            wk_sb[:, 1024 * ci:1024 * ci + 1024].rearrange(
                "p (h z dd) -> p h z dd", z=2, dd=32),
            wtmp_sb[:, 1024 * ci:1024 * ci + 1024].rearrange(
                "p (h dd z) -> p h z dd", z=2, dd=32)),
            waits=[(cWk, ld_k)], inc=cSc))
    perm_k = pk_vals[-1]

    sch.op("sync", lambda e: e.dma_start(out=id_sb[:], in_=id_d), inc=cSyStart, inc_n=16)
    sch.op("sync", lambda e: e.dma_start(out=mask_sb[:], in_=mask_d), inc=cSyStart, inc_n=16)
    for k in range(8):
        for dd, dst in ((gam_d, gam_sb), (bet_d, bet_sb), (mur_d, mur_sb), (muk_d, muk_sb)):
            sch.op("scalar", lambda e, k=k, dd=dd, dst=dst: e.dma_start(
                out=dst[:, k:k + 1], in_=dd[128 * k:128 * (k + 1)].unsqueeze(1)),
                inc=cSyStart, inc_n=16)
    for dd, dst in ((bq_d, row_a), (bk_d, row_b), (bv_d, row_c)):
        sch.op("sync", lambda e, dd=dd, dst=dst: e.dma_start(out=dst[:], in_=dd),
               inc=cSyStart, inc_n=16)
    startup_sy = cSyStart.n

    sch.op("vector", lambda e: e.memset(onec_sb[:], 1.0))
    sch.op("vector", lambda e: e.memset(ones_sb[:], 1.0))
    sch.op("vector", lambda e: e.memzero(rcbf_sb[64:128, :]))
    sch.op("vector", lambda e: e.memzero(mzc_sb[:]))
    sch.op("vector", lambda e: e.memset(eps_sb[:], EPS))
    sch.op("gpsimd", lambda e: e.memzero(qkmax_sb[:]))
    sch.op("vector", lambda e: e.tensor_sub(omur_sb[:], onec_sb[:], mur_sb[:]),
           waits=[(cSyStart, startup_sy)])
    sch.op("vector", lambda e: e.tensor_sub(omuk_sb[:], onec_sb[:], muk_sb[:]))
    v_betbf = sch.op("vector", lambda e: e.tensor_copy(betbf_sb[:], bet_sb[:]), inc=cV)

    prow = None
    for rr in (row_a, row_b):
        p1 = sch.op("vector", lambda e, rr=rr: e.tensor_copy(hif_row[:], rr[:]),
                    waits=[(cSyStart, startup_sy), vw(prow)], inc=cV)
        prow = sch.op("vector", lambda e, rr=rr: e.tensor_copy(
            rr[:].rearrange("o (h z dd) -> o h z dd", z=2, dd=32),
            hif_row[:].rearrange("o (h dd z) -> o h z dd", z=2, dd=32)),
            waits=[(cV, p1)], inc=cV)

    bmm_last = {}
    beta_done = {}
    for mi, (w_sb, dst_row) in enumerate(((wq_sb, row_a), (wk_sb, row_b), (wv_sb, row_c))):
        for half in range(2):
            bank = bank_a[half]

            def f(e, w_sb=w_sb, half=half, bank=bank):
                last = None
                for ci in range(8):
                    last = e.matmul(bank.t[0:1, :], betbf_sb[:, ci:ci + 1],
                                    w_sb[:, 1024 * ci + 512 * half:1024 * ci + 512 * half + 512],
                                    start=(ci == 0), stop=(ci == 7), skip_group_check=True)
                return last
            wld = ((cSc, perm_q), (cSc, perm_k), (cWv, 16))[mi]
            t = sch.op("tensor", f, waits=[wld, vw(v_betbf)] + bank.acquire(),
                       inc=cT)
            bmm_last[mi] = t
            vv = sch.op("vector", lambda e, dst_row=dst_row, half=half, bank=bank: e.tensor_add(
                dst_row[:, 512 * half:512 * half + 512], dst_row[:, 512 * half:512 * half + 512],
                bank.t[0:1, 0:512]), waits=[(cT, t), (cSyStart, startup_sy), vw(prow)], inc=cV)
            bank.consumed(cV, vv)
            beta_done[mi] = vv

    gscale_done = None
    for mi, w_sb in enumerate((wq_sb, wk_sb, wv_sb)):
        for ci in range(8):
            gscale_done = sch.op("vector", lambda e, w_sb=w_sb, ci=ci: e.tensor_scalar(
                w_sb[:, 1024 * ci:1024 * ci + 1024], w_sb[:, 1024 * ci:1024 * ci + 1024],
                gam_sb[:, ci:ci + 1], None, ALU.mult),
                waits=[(cT, bmm_last[mi])], inc=cV)

    def split_row(src_row, dst_hi, dst_lo, extra_wait=None, prev=None):
        v1 = sch.op("vector", lambda e: e.tensor_copy(dst_hi[:], src_row[:]),
                    waits=[extra_wait, vw(prev)], inc=cV)
        v2 = sch.op("vector", lambda e: e.tensor_copy(hif_row[:], dst_hi[:]),
                    waits=[(cV, v1)], inc=cV)
        v3 = sch.op("vector", lambda e: e.tensor_sub(hif_row[:], src_row[:], hif_row[:]),
                    waits=[(cV, v2)], inc=cV)
        v4 = sch.op("vector", lambda e: e.tensor_copy(dst_lo[:], hif_row[:]),
                    waits=[(cV, v3)], inc=cV)
        return v4

    vq_rep = split_row(row_a, bqh_row, bql_row, extra_wait=(cV, beta_done[0]))
    vk_rep = split_row(row_b, bkh_row, bkl_row, extra_wait=(cV, beta_done[1]), prev=vq_rep)
    vv_rep = split_row(row_c, bvh_row, bvl_row, extra_wait=(cV, beta_done[2]), prev=vk_rep)
    bo_ld = sch.op("sync", lambda e: e.dma_start(out=row_a[:], in_=bo_d),
                   waits=[(cV, vq_rep)], inc=cBo, inc_n=16)
    vbo_rep = split_row(row_a, boh_row, bol_row, extra_wait=(cBo, bo_ld), prev=vv_rep)

    wtr_prev_t = None
    last_wT_dma = None
    for mi, (wd, wTd) in enumerate(((wr_d, wrT_d), (wfk_d, wkT_d), (wfv_d, wvT_d))):
        g_ld = sch.op("gpsimd", lambda e, wd=wd: e.dma_start(
            out=wtmp_sb[:].rearrange("p (di c) -> p di c", c=1024), in_=wd.rearrange("(di p) c -> p di c", p=128)),
            waits=[((cSc, perm_k) if wtr_prev_t is None else (cT, wtr_prev_t))], inc=cWtmp, inc_n=16)
        for ci in range(8):
            def f(e, ci=ci):
                last = None
                for di in range(8):
                    last = e.transpose(bank_t.t[:, 128 * di:128 * di + 128],
                                       wtmp_sb[:, 1024 * di + 128 * ci:1024 * di + 128 * ci + 128],
                                       id_sb[:])
                return last
            t = sch.op("tensor", f, waits=[(cWtmp, g_ld), (cSyStart, 16)] + bank_t.acquire(), inc=cT)
            wtr_prev_t = t
            s = sch.op("scalar", lambda e: e.copy(trev[0][:], bank_t.t[:]),
                       waits=[(cT, t), (None if last_wT_dma is None else (cWTw, last_wT_dma))],
                       inc=cSc)
            bank_t.consumed(cSc, s)
            last_wT_dma = sch.op("gpsimd", lambda e, wTd=wTd, ci=ci: e.dma_start(
                out=wTd[:, 1024 * ci:1024 * ci + 1024], in_=trev[0][:]),
                waits=[(cSc, s)], inc=cWTw, inc_n=16)
    wT_writes_done = cWTw.n

    ln_prev_s1 = [None, None]
    ln_prev_xn = [None, None]

    def emit_ln(sl, src, extra_wait=None):
        vst1 = sch.op("vector", lambda e, sl=sl, src=src: e.bn_stats(
            st6a2[sl][:, 0:6], src[:, 0:512]), waits=[extra_wait], inc=cV)
        vst2 = sch.op("vector", lambda e, sl=sl, src=src: e.bn_stats(
            st6b2[sl][:, 0:6], src[:, 512:1024]), inc=cV)
        sch.op("vector", lambda e, sl=sl: e.bn_aggr(mv1_2[sl][:], st6a2[sl][:, 0:6]),
               waits=[(cV, vst1)])
        vag = sch.op("vector", lambda e, sl=sl: e.bn_aggr(mv2_2[sl][:], st6b2[sl][:, 0:6]),
                     waits=[(cV, vst2)], inc=cV)
        i1 = sch.op("vector", lambda e, sl=sl: e.tensor_add(
            mvc_2[sl][:], mv1_2[sl][:], mv2_2[sl][:]),
            waits=[(cV, vag),
                   (None if ln_prev_s1[sl] is None else (cSc, ln_prev_s1[sl]))], inc=cV)
        i2 = sch.op("vector", lambda e, sl=sl: e.tensor_scalar(
            mvc_2[sl][:], mvc_2[sl][:], 0.5, None, ALU.mult), waits=[(cV, i1)], inc=cV)
        i3 = sch.op("vector", lambda e, sl=sl: e.tensor_sub(
            lntmp_2[sl][:, 0:1], mv1_2[sl][:, 0:1], mv2_2[sl][:, 0:1]), inc=cV)
        i4 = sch.op("vector", lambda e, sl=sl: e.tensor_mul(
            lntmp_2[sl][:, 0:1], lntmp_2[sl][:, 0:1], lntmp_2[sl][:, 0:1]),
            waits=[(cV, i3)], inc=cV)
        i5 = sch.op("vector", lambda e, sl=sl: e.tensor_scalar(
            lntmp_2[sl][:, 0:1], lntmp_2[sl][:, 0:1], 0.25, None, ALU.mult),
            waits=[(cV, i4)], inc=cV)
        vstat = sch.op("vector", lambda e, sl=sl: e.tensor_add(
            mvc_2[sl][:, 1:2], mvc_2[sl][:, 1:2], lntmp_2[sl][:, 0:1]),
            waits=[(cV, i5), (cV, i2)], inc=cV)
        s1 = sch.op("scalar", lambda e, sl=sl: e.activation(
            rs_2[sl][:], mvc_2[sl][:, 1:2], AF.Ln, bias=eps_sb[:]),
            waits=[(cV, vstat),
                   (None if ln_prev_xn[sl] is None else (cV, ln_prev_xn[sl]))], inc=cSc)
        s2 = sch.op("scalar", lambda e, sl=sl: e.activation(
            rs_2[sl][:], rs_2[sl][:], AF.Exp, scale=-0.5), inc=cSc)
        ln_prev_s1[sl] = s1
        return s1, s2

    def ln_used(sl, xn_v):
        ln_prev_xn[sl] = xn_v

    last_xn_v = {}
    last_tab_use = {}
    last_tr = {}
    tx_vals = {}
    trev_cons = {}
    vab_cons = {}
    xnT_cons = {}
    rope_cons = {"v": None}
    sq_done = {}
    red_done = {}
    qk_vv_t = {}
    tab_t = {}
    x_t = {}
    rope_qkf_done = {}

    def evens(ap):
        return ap.rearrange("p (h z d) -> p h z d", z=2, d=32)[:, :, 0, :]

    def odds(ap):
        return ap.rearrange("p (h z d) -> p h z d", z=2, d=32)[:, :, 1, :]

    rope_p = {}
    rope_vr = {}

    def emit_rope_muls(t):
        sl = t % 2
        tabin = tab_t[t]
        tabc = tabbuf[sl][:, 0:512].rearrange("p (h d) -> p h d", d=32)
        tabs = tabbuf[sl][:, 512:1024].rearrange("p (h d) -> p h d", d=32)
        tabns = tabbuf[sl][:, 1024:1536].rearrange("p (h d) -> p h d", d=32)
        for nm in ('q', 'k'):
            qkf = qkf_q if nm == 'q' else qkf_k
            x1 = qkf[:].rearrange("p (h z dd) -> p h z dd", z=2, dd=32)[:, :, 0, :]
            x2 = qkf[:].rearrange("p (h z dd) -> p h z dd", z=2, dd=32)[:, :, 1, :]
            rset = [ropet[(nm, i)][:].rearrange("p (h d) -> p h d", d=32) for i in range(4)]
            w_ws = rope_vr.get((t - 2, nm))
            sch.op("gpsimd", lambda e, x1=x1, tabc=tabc, r0=rset[0]: e.tensor_mul(
                r0, x1, tabc),
                waits=[(cTab[sl], tabin), (cSc, qk_vv_t[(t, nm)]), vw(w_ws)])
            p2 = sch.op("gpsimd", lambda e, x2=x2, tabns=tabns, r1=rset[1]: e.tensor_mul(
                r1, x2, tabns), inc=cP)
            sch.op("gpsimd", lambda e, x1=x1, tabs=tabs, r2=rset[2]: e.tensor_mul(
                r2, x1, tabs))
            p4 = sch.op("gpsimd", lambda e, x2=x2, tabc=tabc, r3=rset[3]: e.tensor_mul(
                r3, x2, tabc), inc=cP)
            rope_p[(t, nm)] = (p2, p4)
            rope_qkf_done[(t, nm)] = p4
            if nm == 'k':
                last_tab_use[t] = p4

    def emit_rope_dve(t, nm):
        dst_bf = qbf_sb if nm == 'q' else kbf_sb
        p2, p4 = rope_p[(t, nm)]
        rset = [ropet[(nm, i)][:].rearrange("p (h d) -> p h d", d=32) for i in range(4)]
        sch.op("vector", lambda e, dst_bf=dst_bf, r0=rset[0], r1=rset[1]: e.tensor_add(
            evens(dst_bf), r0, r1),
            waits=[(cP, p2),
                   (None if (t - 1, nm) not in last_tr else (cT, last_tr[(t - 1, nm)])),
                   (None if (t - 1, nm) not in sq_done else (cSc, sq_done[(t - 1, nm)]))])
        vr = sch.op("vector", lambda e, dst_bf=dst_bf, r2=rset[2], r3=rset[3]: e.tensor_add(
            odds(dst_bf), r2, r3), waits=[(cP, p4)], inc=cV)
        rope_vr[(t, nm)] = vr
        noff = 0 if nm == 'q' else 16
        sqbuf = sq_sb if nm == 'q' else sq2_sb
        sqv = sch.op("scalar", lambda e, dst_bf=dst_bf, sqbuf=sqbuf: e.activation(
            sqbuf[:], dst_bf[:], AF.Square),
            waits=[(cV, vr),
                   (None if (t - 1, nm) not in red_done else (cV, red_done[(t - 1, nm)]))],
            inc=cSc)
        sq_done[(t, nm)] = sqv
        prd = sch.op("vector", lambda e, noff=noff, sqbuf=sqbuf: e.reduce_sum(
            nrm1_sb[:, noff:noff + 16], sqbuf[:].rearrange("p (h d) -> p h d", d=64),
            axis=AX.X), waits=[(cSc, sqv)], inc=cV)
        red_done[(t, nm)] = sch.op("vector", lambda e, noff=noff: e.tensor_max(
            qkmax_sb[:, noff:noff + 16], qkmax_sb[:, noff:noff + 16],
            nrm1_sb[:, noff:noff + 16]), waits=[(cV, prd)], inc=cV)

    def emit_rope_tr(t, nm):
        dst_bf = qbf_sb if nm == 'q' else kbf_sb
        dram = qtr_d if nm == 'q' else ktr_d
        slot = 0 if nm == 'q' else 1
        vr = rope_vr[(t, nm)]

        def f_tr(e, dst_bf=dst_bf):
            last = None
            for j in range(8):
                last = e.transpose(bank_t.t[:, 128 * j:128 * j + 128],
                                   dst_bf[:, 128 * j:128 * j + 128], id_sb[:])
            return last
        tt = sch.op("tensor", f_tr, waits=[vw(vr)] + bank_t.acquire(), inc=cT)
        last_tr[(t, nm)] = tt
        w_slot = trev_cons.get((t - 1, slot))
        seed_w = (cWTw, last_wT_dma) if (t == 0 and slot == 0) else None
        se = sch.op("scalar", lambda e, slot=slot: e.copy(trev[slot][:], bank_t.t[:]),
                    waits=[(cT, tt), seed_w,
                           (None if w_slot is None else ((cQw if slot == 0 else cKw), w_slot))],
                    inc=cSc)
        bank_t.consumed(cSc, se)
        trev_cons[(t, slot)] = sch.op("sync", lambda e, dram=dram, slot=slot, t=t: e.dma_start(
            out=dram.rearrange("p (di c) -> p di c", c=L)[:, :, 128 * t:128 * t + 128],
            in_=trev[slot][:].rearrange("p (di j) -> p di j", j=128)),
            waits=[(cSc, se)], inc=(cQw if slot == 0 else cKw), inc_n=16)

    qk_aps = [ps_s[0][:][:, 0:512], ps_s[0][:][:, 512:1024],
              ps_s[1][:][:, 0:512], ps_s[1][:][:, 512:1024]]

    def emit_xdma_x(t):
        sl = t % 2
        x_t[t] = sch.op("sync", lambda e, t=t, sl=sl: e.dma_start(
            out=xbuf[sl][:], in_=x_d[128 * t:128 * (t + 1), :]),
            waits=[vw(last_xn_v.get(t - 2))], inc=cX[sl], inc_n=16)

    def emit_xdma_tab(t):
        sl = t % 2
        tab_t[t] = sch.op("sync", lambda e, t=t, sl=sl: e.dma_start(
            out=tabbuf[sl][:], in_=tab_d[128 * t:128 * (t + 1), :]),
            waits=[(None if (t - 2) not in last_tab_use else (cP, last_tab_use[t - 2]))],
            inc=cTab[sl], inc_n=16)

    def emit_ln_norm(t):
        sl = t % 2
        s1, s2 = emit_ln(sl, xbuf[sl][:], extra_wait=(cX[sl], x_t[t]))
        last_xn_v[t] = sch.op("vector", lambda e, t=t, sl=sl: e.tensor_scalar(
            xnbf[sl][:], xbuf[sl][:], mvc_2[sl][:, 0:1], rs_2[sl][:], ALU.subtract, ALU.mult),
            waits=[(cSc, s2), (None if (t - 2) not in tx_vals else (cT, tx_vals[t - 2]))],
            inc=cV)
        ln_used(sl, last_xn_v[t])

    emit_xdma_x(0)
    emit_xdma_x(1)
    emit_xdma_tab(0)
    for t in range(NT):
        sl = t % 2
        if t == 0:
            emit_ln_norm(0)
        if t >= 1:
            emit_rope_muls(t - 1)
        if t + 1 < NT:
            emit_xdma_tab(t + 1)
        if t + 2 < NT:
            emit_xdma_x(t + 2)

        if DEBUG_IO:
            dxn = sch.op("gpsimd", lambda e, sl=sl, t=t: e.dma_start(
                out=xndbg_d[:, 1024 * t:1024 * (t + 1)], in_=xnbf[sl][:]),
                waits=[(cV, last_xn_v[t])], inc=cVw[0], inc_n=16)
            sch.op("vector", lambda e: e.engine_nop(), waits=[(cVw[0], dxn)])

        def f_trxn(e, sl=sl):
            last = None
            for j in range(8):
                last = e.transpose(bank_t.t[:, 128 * j:128 * j + 128],
                                   xnbf[sl][:, 128 * j:128 * j + 128], id_sb[:])
            return last
        tx = sch.op("tensor", f_trxn, waits=[vw(last_xn_v[t])] + bank_t.acquire(), inc=cT)
        tx_vals[t] = tx
        sxn = sch.op("scalar", lambda e, sl=sl: e.copy(xnT[sl][:], bank_t.t[:]),
                     waits=[(cT, tx),
                            (None if (t - 2) not in xnT_cons else (cT, xnT_cons[t - 2]))],
                     inc=cSc)
        bank_t.consumed(cSc, sxn)

        if t >= 1:
            emit_rope_dve(t - 1, 'q')
            emit_rope_dve(t - 1, 'k')

        for nm, w_sb, bs, aps, bh, bl in (
                ('q', wq_sb, bank_s[0], (qk_aps[0], qk_aps[1]), bqh_row, bql_row),
                ('k', wk_sb, bank_s[1], (qk_aps[2], qk_aps[3]), bkh_row, bkl_row)):
            def f_p(e, sl=sl, w_sb=w_sb, aps=aps, bh=bh, bl=bl):
                last = None
                for ci in range(8):
                    for half in range(2):
                        e.matmul(aps[half], xnT[sl][:, 128 * ci:128 * ci + 128],
                                 w_sb[:, 1024 * ci + 512 * half:1024 * ci + 512 * half + 512],
                                 start=(ci == 0), stop=False, skip_group_check=True)
                for half in range(2):
                    e.matmul(aps[half], ones_sb[0:1, :],
                             bh[:, 512 * half:512 * half + 512],
                             start=False, stop=False, skip_group_check=True)
                    last = e.matmul(aps[half], ones_sb[0:1, :],
                                    bl[:, 512 * half:512 * half + 512],
                                    start=False, stop=True, skip_group_check=True)
                return last
            waits = [(cSc, sxn)] + bs.acquire()
            if t == 0:
                waits += [vw(gscale_done), vw(vbo_rep)]
            tp_ = sch.op("tensor", f_p, waits=waits, inc=cT)
            dst_f = qkf_q if nm == 'q' else qkf_k
            w_rp = rope_qkf_done.get((t - 1, nm))
            for half in range(2):
                vv = sch.op("scalar", lambda e, half=half, dst_f=dst_f,
                            ap=aps[half]: e.copy(
                    dst_f[:, 512 * half:512 * half + 512], ap),
                    waits=[(cT, tp_),
                           (None if w_rp is None else (cP, w_rp))], inc=cSc)
                if half == 1:
                    qk_vv_t[(t, nm)] = vv
                    bs.consumed(cSc, vv)
        if t + 1 < NT:
            emit_ln_norm(t + 1)
        if t >= 1:
            emit_rope_tr(t - 1, 'q')
            emit_rope_tr(t - 1, 'k')

        w_slot_v = vab_cons.get(t - 2)

        def f_v(e, sl=sl):
            last = None
            for ci in range(8):
                for half in range(2):
                    e.matmul(bank_a[half].t[:], xnT[sl][:, 128 * ci:128 * ci + 128],
                             wv_sb[:, 1024 * ci + 512 * half:1024 * ci + 512 * half + 512],
                             start=(ci == 0), stop=False, skip_group_check=True)
            for half in range(2):
                e.matmul(bank_a[half].t[:], ones_sb[0:1, :],
                         bvh_row[:, 512 * half:512 * half + 512],
                         start=False, stop=False, skip_group_check=True)
                last = e.matmul(bank_a[half].t[:], ones_sb[0:1, :],
                                bvl_row[:, 512 * half:512 * half + 512],
                                start=False, stop=True, skip_group_check=True)
            return last
        tv = sch.op("tensor", f_v,
                    waits=[(cSc, sxn)] + bank_a[0].acquire() + bank_a[1].acquire(), inc=cT)
        xnT_cons[t] = tv
        for half in range(2):
            vv = sch.op("scalar", lambda e, half=half, sl=sl: e.copy(
                vab[sl][:].rearrange("p (h c) -> p h c", c=65)[:, 8 * half:8 * half + 8, 0:64],
                bank_a[half].t[:].rearrange("p (h d) -> p h d", d=64)),
                waits=[(cT, tv),
                       (None if w_slot_v is None else (cVw[sl], w_slot_v))], inc=cSc)
            bank_a[half].consumed(cSc, vv)
        vab_ev = vv
        if t < 2:
            vones = sch.op("vector", lambda e, sl=sl: e.memset(
                vab[sl][:].rearrange("p (h c) -> p h c", c=65)[:, :, 64:65], 1.0), inc=cV)
            vab_cons[t] = sch.op("sync", lambda e, sl=sl, t=t: e.dma_start(
                out=vau_d[:, 1040 * t:1040 * (t + 1)], in_=vab[sl][:]),
                waits=[(cV, vones), (cSc, vab_ev)], inc=cVw[sl], inc_n=16)
        else:
            vab_cons[t] = sch.op("sync", lambda e, sl=sl, t=t: e.dma_start(
                out=vau_d[:, 1040 * t:1040 * (t + 1)], in_=vab[sl][:]),
                waits=[(cSc, vab_ev)], inc=cVw[sl], inc_n=16)

    emit_rope_muls(NT - 1)
    emit_rope_dve(NT - 1, 'q')
    emit_rope_dve(NT - 1, 'k')
    emit_rope_tr(NT - 1, 'q')
    emit_rope_tr(NT - 1, 'k')

    e1 = sch.op("vector", lambda e: e.reduce_max(nrm3_sb[:, 0:1], qkmax_sb[:], axis=AX.X),
                waits=[(cV, red_done[(NT - 1, 'k')])], inc=cV)
    e2 = sch.op("vector", lambda e: e.memzero(nrm2_sb[:]), waits=[(cV, e1)], inc=cV)
    e3 = sch.op("vector", lambda e: e.tensor_copy(nrm2_sb[:, 0:1], nrm3_sb[:, 0:1]),
                waits=[(cV, e2), (cV, e1)], inc=cV)
    te = sch.op("tensor", lambda e: e.transpose(bank_t.t[:, 0:128], nrm2_sb[:], id_sb[:]),
                waits=[vw(e3)] + bank_t.acquire(), inc=cT)
    se = sch.op("scalar", lambda e: e.copy(nrm2_sb[:], bank_t.t[:, 0:128]),
                waits=[(cT, te)], inc=cSc)
    bank_t.consumed(cSc, se)
    e4 = sch.op("vector", lambda e: e.tensor_copy(sq_sb[0:1, 0:128], nrm2_sb[0:1, :]),
                waits=[(cSc, se)], inc=cV)
    e5 = sch.op("vector", lambda e: e.reduce_max(nrm3_sb[0:1, 1:2], sq_sb[0:1, 0:128], axis=AX.X),
                waits=[(cV, e4)], inc=cV)
    e6 = sch.op("vector", lambda e: e.tensor_scalar(nrm3_sb[0:1, 2:3], nrm3_sb[0:1, 1:2],
                                                    -0.1275, None, ALU.mult),
                waits=[(cV, e5)], inc=cV)
    e7 = sch.op("vector", lambda e: e.tensor_copy(mzc_sb[0:1, 0:1], nrm3_sb[0:1, 2:3]),
                waits=[(cV, e6)], inc=cV)
    tneg = sch.op("tensor", lambda e: e.matmul(
        ps_rp[:, 0:1], ones_sb[0:64, 0:128], mzc_sb[0:64, 0:1], start=True, stop=True,
        skip_group_check=True),
        waits=[vw(e7)] + bank_r.acquire(), inc=cT)
    negc_done = sch.op("vector", lambda e: e.tensor_copy(negc_sb[:], ps_rp[:, 0:1]),
                       waits=[(cT, tneg)], inc=cV)
    bank_r.consumed(cV, negc_done)
    if DEBUG_IO:
        dbg_d = nc.dram_tensor("dbg_s", [128, 40], F32, kind="ExternalOutput").ap()
        sch.op("vector", lambda e: e.tensor_copy(sq_sb[:, 200:201], negc_sb[:]),
               waits=[(cV, negc_done)])
        dv = sch.op("vector", lambda e: e.tensor_copy(sq_sb[:, 201:233], qkmax_sb[:]), inc=cV)
        sch.op("gpsimd", lambda e: e.dma_start(out=dbg_d[:, 0:1], in_=sq_sb[:, 200:201], allow_slow_non_contiguous=True),
               waits=[(cV, dv)], inc=cVw[0], inc_n=16)
        sch.op("gpsimd", lambda e: e.dma_start(out=dbg_d[:, 1:33], in_=sq_sb[:, 201:233], allow_slow_non_contiguous=True),
               inc=cVw[0], inc_n=16)

    dma_chansA = [cSyStart, cBo, cWq, cWk, cWv, cWtmp, cWTw, cX[0], cX[1],
                  cTab[0], cTab[1], cQw, cKw, cVw[0], cVw[1]]
    barA = [(c0, c0.n) for c0 in dma_chansA] + [(cSc, cSc.n), (cV, cV.n), (cT, cT.n)]

    cVau = sch.chan("cVau")
    for i in range(4):
        sch.op("sync", lambda e, i=i: e.dma_start(
            out=ktr_sb[:, 8192 * i:8192 * (i + 1)], in_=ktr_d[:, 8192 * i:8192 * (i + 1)]),
            waits=barA, inc=cKV, inc_n=16)
    for i in range(4):
        sch.op("scalar", lambda e, i=i: e.dma_start(
            out=vau_sb[:, 8320 * i:8320 * (i + 1)], in_=vau_d[:, 8320 * i:8320 * (i + 1)]),
            waits=(barA if i == 0 else []), inc=cVau, inc_n=16)
    kv_done = cKV.n

    qtr_ld = {}
    S_last_of_group = {}
    div_vals = []
    atw_vals = {}
    u = 0
    av_pair = []
    av_banks = [bank_a[0], bank_a[1], bank_t]
    av_aps = [ps_a[0][:], ps_a[1][:], ps_tr_f32]
    pend_epi = []
    epi_state = {"rc_last_cons": None, "rcbf_last_T": None}

    def emit_qtr_ld(g):
        w_slot = (cT, S_last_of_group[g - 2]) if g - 2 in S_last_of_group else None
        qtr_ld[g] = sch.op("sync", lambda e, g=g: e.dma_start(
            out=qtr_sb[g % 2][:].rearrange("p (di i) -> p di i", i=512),
            in_=qtr_d.rearrange("p (di c) -> p di c", c=L)[:, :, 512 * g:512 * g + 512]),
            waits=[w_slot], inc=cQtr[g % 2], inc_n=16)

    for g in range(NG):
        if g == 0:
            emit_qtr_ld(0)
        if g + 1 < NG:
            emit_qtr_ld(g + 1)

        nkt = 4 * g + 4
        kt_order = list(range(4 * g, 4 * g + 4)) + list(range(0, 4 * g))
        for hp in range(8):
            pend_av = []
            for ki, kt in enumerate(kt_order):
                sb = bank_s[u % 2]

                def f_s(e, hp=hp, kt=kt, g=g, sb=sb):
                    e.matmul(sb.t[:, 0:512],
                             ktr_sb[0:64, 4096 * hp + 128 * kt:4096 * hp + 128 * kt + 128],
                             qtr_sb[g % 2][0:64, 512 * hp:512 * hp + 512],
                             start=True, stop=True, skip_group_check=True, tile_position=(0, 0))
                    return e.matmul(sb.t[:, 512:1024],
                                    ktr_sb[64:128, 4096 * hp + 128 * kt:4096 * hp + 128 * kt + 128],
                                    qtr_sb[g % 2][64:128, 512 * hp:512 * hp + 512],
                                    start=True, stop=True, skip_group_check=True,
                                    tile_position=(64, 0))
                waits = [(cQtr[g % 2], qtr_ld[g]), (cKV, 16 * (hp // 2 + 1))] + sb.acquire()
                if u == 0:
                    waits += [vw(negc_done)]
                ts_ = sch.op("tensor", f_s, waits=waits, inc=cT)
                if ki == nkt - 1 and hp == 7:
                    S_last_of_group[g] = ts_

                pslot = u % 4
                w_pt = (cT, av_pair[u - 4][1]) if u >= 4 else None
                ex = sch.op("scalar", lambda e, sb=sb, pslot=pslot: e.activation(
                    ptb_bufs[pslot][:], sb.t[:], AF.Exp, bias=negc_sb[:, 0:1], scale=0.125),
                    waits=[(cT, ts_), w_pt], inc=cSc)
                sb.consumed(cSc, ex)

                m = kt - 4 * g
                use_w_a = use_w_b = (cSc, ex)
                if 0 <= m <= 3:
                    mo = 384 - 128 * m
                    vma = sch.op("vector", lambda e, pslot=pslot, mo=mo: e.tensor_mul(
                        ptb_bufs[pslot][:, 0:512], ptb_bufs[pslot][:, 0:512],
                        mask_sb[:, mo:mo + 512]), waits=[(cSc, ex)], inc=cV)
                    vmb = sch.op("vector", lambda e, pslot=pslot, mo=mo: e.tensor_mul(
                        ptb_bufs[pslot][:, 512:1024], ptb_bufs[pslot][:, 512:1024],
                        mask_sb[:, mo:mo + 512]), waits=[(cSc, ex)], inc=cV)
                    use_w_a = (cV, vma)
                    use_w_b = (cV, vmb)

                def emit_av(ki2, kt2, pslot2, uwa, uwb, hp=hp, g=g, nkt=nkt):
                    tas = []
                    for a in range(2):
                        h = 2 * hp + a
                        ab = av_banks[h % 3]
                        abt = av_aps[h % 3]

                        def f_av(e, kt2=kt2, h=h, pslot2=pslot2, a=a, abt=abt,
                                 ki2=ki2, nkt=nkt):
                            return e.matmul(abt[0:65, :],
                                            vau_sb[:, 1040 * kt2 + 65 * h:1040 * kt2 + 65 * h + 65],
                                            ptb_bufs[pslot2][:, 512 * a:512 * a + 512],
                                            start=(ki2 == 0), stop=(ki2 == nkt - 1),
                                            skip_group_check=True)
                        aw = [uwa if a == 0 else uwb, (cVau, 16 * (kt2 // 8 + 1))]
                        if ki2 == 0:
                            hg = 16 * g + h
                            if hg >= 3:
                                aw.append((cV, div_vals[hg - 3]))
                            aw += ab.acquire()
                        ta = sch.op("tensor", f_av, waits=aw, inc=cT)
                        tas.append(ta)
                    av_pair.append(tuple(tas))

                if ki == 2 and pend_epi:
                    for f_ in pend_epi:
                        f_()
                    pend_epi.clear()
                if len(pend_av) == 2:
                    emit_av(*pend_av.pop(0))
                pend_av.append((ki, kt, pslot, use_w_a, use_w_b))
                u += 1
            for p_ in pend_av:
                emit_av(*p_)
            pend_av = []

            for a in range(2):
                h = 2 * hp + a
                hg = 16 * g + h
                ab = av_banks[h % 3]
                abt = av_aps[h % 3]
                av_done = av_pair[-1][a]

                def epi(a=a, h=h, hg=hg, ab=ab, abt=abt, av_done=av_done, g=g):
                    sr1 = sch.op("scalar", lambda e, abt=abt: e.activation(
                        recip_sb[64:65, :], abt[64:65, :], AF.Ln),
                        waits=[(cT, av_done), (None if epi_state["rc_last_cons"] is None
                                               else (cSc, epi_state["rc_last_cons"]))], inc=cSc)
                    sr2 = sch.op("scalar", lambda e: e.activation(
                        rcbf_sb[64:65, :], recip_sb[64:65, :], AF.Exp, scale=-1.0),
                        waits=[(None if epi_state["rcbf_last_T"] is None
                                else (cT, epi_state["rcbf_last_T"]))], inc=cSc)
                    epi_state["rc_last_cons"] = sr2
                    tr_ = sch.op("tensor", lambda e: e.matmul(
                        bank_r.t[0:64, :], ones_sb[64:128, 0:64], rcbf_sb[64:128, :],
                        start=True, stop=True, skip_group_check=True),
                        waits=[(cSc, sr2)] + bank_r.acquire(), inc=cT)
                    epi_state["rcbf_last_T"] = tr_
                    ve = sch.op("vector", lambda e: e.tensor_copy(
                        repev_sb[0:64, :], bank_r.t[0:64, :]),
                        waits=[(cT, tr_)], inc=cV)
                    bank_r.consumed(cV, ve)
                    w_at = (cAtnW, atw_vals[g - 1]) if (h == 0 and g - 1 in atw_vals) else None
                    vd_ = sch.op("vector", lambda e, h=h, abt=abt: e.tensor_mul(
                        atn_sb[0:64, 512 * h:512 * h + 512], abt[0:64, :], repev_sb[0:64, :]),
                        waits=[w_at], inc=cV)
                    ab.consumed(cV, vd_)
                    div_vals.append(vd_)
                pend_epi.append(epi)
            if hp == 7:
                for f_ in pend_epi:
                    f_()
                pend_epi.clear()

        sch.op("gpsimd", lambda e, g=g: e.dma_start(
            out=atn_d.rearrange("(par p) (g2 h2 i) -> p par g2 h2 i",
                                par=2, h2=8, i=512)[:, 0, g, :, :],
            in_=atn_sb[0:64, :].rearrange("p (h2 par i) -> p par h2 i",
                                          par=2, i=512)[:, 0, :, :]),
            waits=[(cV, div_vals[16 * g + 15])], inc=cAtnW, inc_n=16)
        atw_vals[g] = sch.op("gpsimd", lambda e, g=g: e.dma_start(
            out=atn_d.rearrange("(par p) (g2 h2 i) -> p par g2 h2 i",
                                par=2, h2=8, i=512)[:, 1, g, :, :],
            in_=atn_sb[0:64, :].rearrange("p (h2 par i) -> p par h2 i",
                                          par=2, i=512)[:, 1, :, :]),
            inc=cAtnW, inc_n=16)

    dma_chansB = dma_chansA + [cKV, cVau, cQtr[0], cQtr[1], cAtnW]
    barB = [(c0, c0.n) for c0 in dma_chansB] + [(cSc, cSc.n), (cV, cV.n), (cT, cT.n)]

    sch.op("gpsimd", lambda e: e.dma_start(
        out=wo_sb[:].rearrange("p (ci d) -> p ci d", d=1024), in_=wo_d.rearrange("(ci p) d -> p ci d", p=128)),
        waits=barB, inc=cWoG, inc_n=16)
    for wsb, wdd in ((wrT_sb, wrT_d), (wkT_sb, wkT_d), (wvT_sb, wvT_d)):
        sch.op("sync", lambda e, wsb=wsb, wdd=wdd: e.dma_start(out=wsb[:], in_=wdd),
               waits=barB, inc=cWc, inc_n=16)
    wC_sy = cWc.n
    wC_g = cWoG.n

    wscale_done = None
    for wsrc, wdst, musb in ((wkT_sb, wkTm_sb, muk_sb), (wrT_sb, wrTm_sb, mur_sb)):
        for ci2 in range(8):
            wscale_done = sch.op("vector", lambda e, wsrc=wsrc, wdst=wdst, musb=musb,
                                 ci2=ci2: e.tensor_scalar(
                wdst[:, 1024 * ci2:1024 * ci2 + 1024],
                wsrc[:, 1024 * ci2:1024 * ci2 + 1024],
                musb[:, ci2:ci2 + 1], None, ALU.mult),
                waits=[(cWc, wC_sy)] + (barB if ci2 == 0 else []), inc=cV)

    atc_ld = {}
    xc_ld = {}
    xcb_cons = {}
    bank_rC = [PBank(None), PBank(None)]
    bank_kvC = [PBank(None), PBank(None)]
    pend_trz = []
    trz_T = {}
    z1bf2 = [z1bf_sb, z1bf_b2]
    bank_rC[0].last = bank_s[0].last
    bank_rC[1].last = bank_s[0].last
    bank_kvC[0].last = bank_s[1].last
    bank_kvC[1].last = bank_s[1].last
    r_aps = [ps_s[0][:][:, 0:512], ps_s[0][:][:, 512:1024]]
    kv_aps = [ps_s[1][:][:, 0:512], ps_s[1][:][:, 512:1024]]
    out_st = {}
    y_done = {}
    z1T_evac = {}
    wo_last = {}
    grp_end_T = {}
    r_cons = {}
    kact_cons = None
    z1padd = {}
    d1_done = {}

    def emit_atc_ld(g):
        w_slot = (cT, wo_last[g - 2]) if g - 2 in wo_last else None
        atc_ld[g] = sch.op("sync", lambda e, g=g: e.dma_start(
            out=atc_sb[g % 2][:], in_=atn_d[:, 4096 * g:4096 * (g + 1)]),
            waits=[w_slot, (cWc, wC_sy)], inc=cAtc[g % 2], inc_n=16)

    for g in range(NG):
        if g == 0:
            emit_atc_ld(0)
        if g + 1 < NG:
            emit_atc_ld(g + 1)

        if g == 0:
            vh = sch.op("vector", lambda e: e.memset(
                z1T_sb[:].rearrange("p (ci w) -> p ci w", w=513)[:, :, 0:1], 0.0),
                waits=barB, inc=cV)
        else:
            vh = sch.op("vector", lambda e: e.tensor_copy(
                z1T_sb[:].rearrange("p (ci w) -> p ci w", w=513)[:, :, 0:1],
                z1T_sb[:].rearrange("p (ci w) -> p ci w", w=513)[:, :, 512:513]),
                waits=[(cSc, z1T_evac[4 * g - 1]), (cT, grp_end_T[g - 1])], inc=cV)
        vh_g = vh

        def emit_d1(tt, g=g, vh_g=vh_g):
            w = [(cSc, z1T_evac[4 * g + tt])]
            if tt == 0:
                w.append(vw(vh_g))
                if g >= 1:
                    w.append((cT, grp_end_T[g - 1]))
            d1_done[(g, tt)] = sch.op("vector", lambda e, tt=tt: e.tensor_sub(
                d1_sb[:].rearrange("p (ci w) -> p ci w", w=512)[:, :, 128 * tt:128 * tt + 128],
                z1T_sb[:].rearrange("p (ci w) -> p ci w", w=513)[:, :, 1 + 128 * tt:129 + 128 * tt],
                z1T_sb[:].rearrange("p (ci w) -> p ci w", w=513)[:, :, 128 * tt:128 * tt + 128]),
                waits=w, inc=cV)

        for tt in range(GT):
            ct = 4 * g + tt
            xc_ld[ct] = sch.op("scalar", lambda e, ct=ct: e.dma_start(
                out=xcb[ct % 2][:], in_=x_d[128 * ct:128 * (ct + 1), :]),
                waits=[vw(xcb_cons.get(ct - 2))], inc=cXc[ct % 2], inc_n=16)
            for half in range(2):
                bank = bank_a[half]

                def f(e, g=g, tt=tt, half=half, bank=bank):
                    for ci2 in range(8):
                        e.matmul(
                            bank.t[:],
                            atc_sb[g % 2][:, 512 * ci2 + 128 * tt:512 * ci2 + 128 * tt + 128],
                            wo_sb[:, 1024 * ci2 + 512 * half:1024 * ci2 + 512 * half + 512],
                            start=(ci2 == 0), stop=False, skip_group_check=True)
                    e.matmul(bank.t[:], ones_sb[0:1, :],
                             boh_row[:, 512 * half:512 * half + 512],
                             start=False, stop=False, skip_group_check=True)
                    return e.matmul(bank.t[:], ones_sb[0:1, :],
                                    bol_row[:, 512 * half:512 * half + 512],
                                    start=False, stop=True, skip_group_check=True)
                two = sch.op("tensor", f,
                             waits=[(cAtc[g % 2], atc_ld[g]), (cWoG, wC_g)] + bank.acquire(), inc=cT)
                if half == 1:
                    wo_last[g] = two
                vvx = sch.op("vector", lambda e, half=half, ct=ct, bank=bank: e.scalar_tensor_tensor(
                    ysb[:, 512 * half:512 * half + 512],
                    xcb[ct % 2][:, 512 * half:512 * half + 512], 2.0, bank.t[:],
                    ALU.mult, ALU.add),
                    waits=[(cT, two), (cXc[ct % 2], xc_ld[ct])], inc=cV)
                if half == 1:
                    xcb_cons[ct] = vvx
                bank.consumed(cV, vvx)
            sl2 = ct % 2
            s1, s2 = emit_ln(sl2, ysb[:], extra_wait=(cV, vvx))
            vz = sch.op("vector", lambda e, tt=tt, sl2=sl2: e.tensor_scalar(
                z1_sb[:, 1024 * tt:1024 * tt + 1024], ysb[:], mvc_2[sl2][:, 0:1], rs_2[sl2][:],
                ALU.subtract, ALU.mult),
                waits=[(cSc, s2),
                       (None if tt not in z1padd else (cP, z1padd[tt]))], inc=cV)
            ln_used(sl2, vz)
            zb = sch.op("scalar", lambda e, tt=tt: e.copy(
                z1bf2[tt % 2][:], z1_sb[:, 1024 * tt:1024 * tt + 1024]),
                waits=[(cV, vz),
                       (None if (ct - 2) not in trz_T else (cT, trz_T[ct - 2]))], inc=cSc)

            def mk_trz(tt=tt, ct=ct, zb=zb):
                def doit():
                    def f_trz(e, sl2=tt % 2):
                        last = None
                        for j in range(8):
                            last = e.transpose(bank_t.t[:, 128 * j:128 * j + 128],
                                               z1bf2[sl2][:, 128 * j:128 * j + 128], id_sb[:])
                        return last
                    tz = sch.op("tensor", f_trz, waits=[(cSc, zb)] + bank_t.acquire(), inc=cT)
                    trz_T[ct] = tz
                    sz = sch.op("scalar", lambda e, tt=tt: e.copy(
                        z1T_sb[:].rearrange("p (ci w) -> p ci w", w=513)[:, :, 1 + 128 * tt:129 + 128 * tt],
                        bank_t.t[:].rearrange("p (ci j) -> p ci j", j=128)),
                        waits=[(cT, tz), vw(vh)], inc=cSc)
                    bank_t.consumed(cSc, sz)
                    z1T_evac[ct] = sz
                    emit_d1(tt)
                return doit
            pend_trz.append(mk_trz())
            if len(pend_trz) > 1:
                pend_trz.pop(0)()

        for f_ in pend_trz:
            f_()
        pend_trz.clear()
        d1_last = d1_done[(g, 3)]

        z1Tv_g = z1T_sb[:].rearrange("p (ci w) -> p ci w", w=513)
        d1v_g = d1_sb[:].rearrange("p (ci w) -> p ci w", w=512)
        sq_vals = {}
        for di in range(8):
            bank = bank_a[di % 2]

            def f(e, di=di, bank=bank):
                last = None
                for ci2 in range(8):
                    e.matmul(bank.t[:],
                             wkT_sb[:, 1024 * ci2 + 128 * di:1024 * ci2 + 128 * di + 128],
                             z1Tv_g[:, ci2, 0:512],
                             start=(ci2 == 0), stop=False, skip_group_check=True)
                    last = e.matmul(bank.t[:],
                                    wkTm_sb[:, 1024 * ci2 + 128 * di:1024 * ci2 + 128 * di + 128],
                                    d1v_g[:, ci2, 0:512],
                                    start=False, stop=(ci2 == 7), skip_group_check=True)
                return last
            kw = [(cV, d1_last)] + bank.acquire()
            if g == 0:
                kw.append((cV, wscale_done))
            tk = sch.op("tensor", f, waits=kw, inc=cT)
            sk = sch.op("scalar", lambda e, bank=bank: e.activation(kact_t[:], bank.t[:], AF.Relu),
                        waits=[(cT, tk)], inc=cSc)
            bank.consumed(cSc, sk)
            sq_vals[di] = sch.op("scalar", lambda e, di=di: e.activation(
                kactT_sb[:, 512 * di:512 * di + 512], kact_t[:], AF.Square),
                waits=[(None if g == 0 else (cT, grp_end_T[g - 1]))], inc=cSc)
        kact_done = sq_vals[7]

        for tt in range(GT):
            ct = 4 * g + tt
            for half in range(2):

                def f_r(e, tt=tt, half=half, rap=r_aps[half]):
                    last = None
                    for ci2 in range(8):
                        e.matmul(
                            rap,
                            z1Tv_g[:, ci2, 128 * tt:128 * tt + 128],
                            wrT_sb[:, 1024 * ci2 + 512 * half:1024 * ci2 + 512 * half + 512],
                            start=(ci2 == 0), stop=False, skip_group_check=True)
                        last = e.matmul(
                            rap,
                            d1v_g[:, ci2, 128 * tt:128 * tt + 128],
                            wrTm_sb[:, 1024 * ci2 + 512 * half:1024 * ci2 + 512 * half + 512],
                            start=False, stop=(ci2 == 7), skip_group_check=True)
                    return last
                rb = bank_rC[half]
                rap = r_aps[half]
                rw = [(cV, d1_done[(g, tt)])] + rb.acquire()
                if g == 0 and tt == 0:
                    rw.append((cV, wscale_done))
                tr_ = sch.op("tensor", f_r, waits=rw, inc=cT)
                sa = sch.op("scalar", lambda e, rap=rap: e.activation(
                    sgt_sb[:], rap, AF.Exp, scale=-1.0),
                    waits=[(cT, tr_), vw(r_cons.get((ct - 1, half)) if tt > 0 or g > 0 else None)],
                    inc=cSc)
                rb.consumed(cSc, sa)
                sch.op("scalar", lambda e: e.activation(sgt_sb[:], sgt_sb[:], AF.Ln, bias=onec_sb[:, 0:1]))
                sc_ = sch.op("scalar", lambda e, half=half: e.activation(
                    r_sb[:, 512 * half:512 * half + 512], sgt_sb[:], AF.Exp, scale=-1.0), inc=cSc)

                def f_kv(e, tt=tt, half=half, kap=kv_aps[half]):
                    last = None
                    for ci2 in range(8):
                        last = e.matmul(
                            kap,
                            kactT_sb[:, 512 * ci2 + 128 * tt:512 * ci2 + 128 * tt + 128],
                            wvT_sb[:, 1024 * ci2 + 512 * half:1024 * ci2 + 512 * half + 512],
                            start=(ci2 == 0), stop=(ci2 == 7), skip_group_check=True)
                    return last
                kvb = bank_kvC[half]
                tkv = sch.op("tensor", f_kv,
                             waits=[(cSc, kact_done)] + kvb.acquire(), inc=cT)
                vp = sch.op("vector", lambda e, half=half, kap=kv_aps[half]: e.tensor_mul(
                    y2_sb[:, 512 * half:512 * half + 512], kap,
                    r_sb[:, 512 * half:512 * half + 512]),
                    waits=[(cT, tkv), (cSc, sc_)], inc=cV)
                kvb.consumed(cV, vp)
                r_cons[(ct, half)] = vp
                pad = sch.op("gpsimd", lambda e, half=half, tt=tt: e.tensor_add(
                    y2_sb[:, 512 * half:512 * half + 512], y2_sb[:, 512 * half:512 * half + 512],
                    z1_sb[:, 1024 * tt + 512 * half:1024 * tt + 512 * half + 512]),
                    waits=[(cV, vp)], inc=cP)
                z1padd[tt] = pad
            sl2 = ct % 2
            s1, s2 = emit_ln(sl2, y2_sb[:], extra_wait=(cP, pad))
            w_ob = (cOut[(ct - 1) % 2], out_st[ct - 1]) if (ct - 1) in out_st else None
            vo = sch.op("vector", lambda e, ct=ct, sl2=sl2: e.tensor_scalar(
                outb[ct % 2][:], y2_sb[:], mvc_2[sl2][:, 0:1], rs_2[sl2][:],
                ALU.subtract, ALU.mult),
                waits=[(cSc, s2), w_ob], inc=cV)
            ln_used(sl2, vo)
            y_done[ct] = vo
            out_st[ct] = sch.op("sync", lambda e, ct=ct: e.dma_start(
                out=out_d[128 * ct:128 * (ct + 1), :], in_=outb[ct % 2][:]),
                waits=[(cV, vo)], inc=cOut[ct % 2], inc_n=16)
        grp_end_T[g] = cT.n
        kact_cons = r_cons[(4 * g + 3, 1)]

    sch.op("gpsimd", lambda e: e.nop(), waits=[(cOut[0], cOut[0].n), (cOut[1], cOut[1].n)])

    sch.emit()
    stack.close()
    return nc


_prog_cache = {}


def _get_prog():
    if "nc" not in _prog_cache:
        _prog_cache["nc"] = build_program()
    return _prog_cache["nc"]


def _host_consts():
    inv_freq = 1.0 / (10000.0 ** (np.arange(0, DH, 2, dtype=np.float64) / DH))
    tpos = np.arange(L, dtype=np.float64)[:, None] * inv_freq[None, :]
    cos_rep = np.tile(np.cos(tpos), (1, H)).astype(np.float32)
    sin_rep = np.tile(np.sin(tpos), (1, H)).astype(np.float32)
    tab = np.concatenate([cos_rep, sin_rep, -sin_rep], axis=1)
    ident = np.eye(128, dtype=ml_dtypes.bfloat16)
    jj = np.arange(128)[:, None]
    cc = np.arange(896)[None, :]
    maskt = (jj <= cc - 384).astype(ml_dtypes.bfloat16)
    return np.ascontiguousarray(tab), ident, np.ascontiguousarray(maskt)


def kernel(L0_attn_x, L0_attn_ln_gamma, L0_attn_ln_beta, L0_attn_Wq, L0_attn_bq,
           L0_attn_Wk, L0_attn_bk, L0_attn_Wv, L0_attn_bv, L0_attn_Wo, L0_attn_bo,
           L0_ffn_W_r, L0_ffn_W_k, L0_ffn_W_v, L0_ffn_mu_r, L0_ffn_mu_k):
    x = np.ascontiguousarray(np.asarray(L0_attn_x, dtype=np.float32))
    tab, ident, maskt = _host_consts()
    common = dict(
        wq=np.ascontiguousarray(np.asarray(L0_attn_Wq, np.float32)),
        wk=np.ascontiguousarray(np.asarray(L0_attn_Wk, np.float32)),
        wv=np.ascontiguousarray(np.asarray(L0_attn_Wv, np.float32)),
        wo=np.ascontiguousarray(np.asarray(L0_attn_Wo, np.float32)),
        wr=np.ascontiguousarray(np.asarray(L0_ffn_W_r, np.float32)),
        wfk=np.ascontiguousarray(np.asarray(L0_ffn_W_k, np.float32)),
        wfv=np.ascontiguousarray(np.asarray(L0_ffn_W_v, np.float32)),
        gam=np.ascontiguousarray(np.asarray(L0_attn_ln_gamma, np.float32)),
        bet=np.ascontiguousarray(np.asarray(L0_attn_ln_beta, np.float32)),
        bq=np.ascontiguousarray(np.asarray(L0_attn_bq, np.float32).reshape(1, D)),
        bk=np.ascontiguousarray(np.asarray(L0_attn_bk, np.float32).reshape(1, D)),
        bv=np.ascontiguousarray(np.asarray(L0_attn_bv, np.float32).reshape(1, D)),
        bo=np.ascontiguousarray(np.asarray(L0_attn_bo, np.float32).reshape(1, D)),
        mur=np.ascontiguousarray(np.asarray(L0_ffn_mu_r, np.float32)),
        muk=np.ascontiguousarray(np.asarray(L0_ffn_mu_k, np.float32)),
        tab=tab, ident=ident, maskt=maskt,
    )
    in_maps = [dict(common, x=np.ascontiguousarray(x[b])) for b in range(8)]
    nc = _get_prog()
    res = run_bass_kernel_spmd(nc, in_maps, core_ids=list(range(8)))
    return np.stack([res.results[b]["out"] for b in range(8)], axis=0)


# version 32
# speedup vs baseline: 1.1801x; 1.1801x over previous
import numpy as np
import ml_dtypes
import contextlib

DEBUG_C = False
DEBUG_IO = False
import concourse.bass as bass
import concourse.mybir as mybir
from concourse.bass_utils import run_bass_kernel_spmd

F32 = mybir.dt.float32
BF16 = mybir.dt.bfloat16
AF = mybir.ActivationFunctionType
ALU = mybir.AluOpType
AX = mybir.AxisListType

L = 4096
D = 1024
NT = 32
NG = 8
GT = 4
H = 16
DH = 64
EPS = 1e-5

ENGS = ("sync", "scalar", "vector", "tensor", "gpsimd")


class Chan:
    __slots__ = ("sem", "n", "name")

    def __init__(self, sem, name):
        self.sem = sem
        self.n = 0
        self.name = name


class PBank:
    def __init__(self, t):
        self.t = t
        self.last = None

    def acquire(self):
        return [self.last] if self.last else []

    def consumed(self, chan, val):
        self.last = (chan, val)


class Sched:
    def __init__(self, nc, stack):
        self.nc = nc
        self.stack = stack
        self.ops = {e: [] for e in ENGS}
        self.hwm = {e: {} for e in ENGS}

    def chan(self, name):
        sem = self.stack.enter_context(self.nc.semaphore(name))
        return Chan(sem, name)

    def op(self, eng, fn, waits=(), inc=None, inc_n=1):
        rw = []
        for w in waits:
            if w is None:
                continue
            ch, v = w
            if v is None or v <= 0:
                continue
            if self.hwm[eng].get(id(ch), 0) >= v:
                continue
            self.hwm[eng][id(ch)] = v
            rw.append((ch, v))
        tgt = None
        if inc is not None:
            inc.n += inc_n
            tgt = inc.n
        self.ops[eng].append((rw, fn, inc, inc_n))
        return tgt

    def emit(self):
        with self.nc.Block() as block:
            for name in ENGS:
                ops = self.ops[name]

                def mk(ops=ops):
                    def run(e):
                        for (rw, fn, inc, inc_n) in ops:
                            for ch, v in rw:
                                e.wait_ge(ch.sem, v)
                            inst = fn(e)
                            if inc is not None:
                                inst.then_inc(inc.sem, inc_n)
                    return run
                getattr(block, name)(mk())


def build_program():
    nc = bass.Bass("TRN2", target_bir_lowering=False)
    stack = contextlib.ExitStack()

    x_d = nc.dram_tensor("x", [L, D], F32, kind="ExternalInput").ap()
    wq_d = nc.dram_tensor("wq", [D, D], F32, kind="ExternalInput").ap()
    wk_d = nc.dram_tensor("wk", [D, D], F32, kind="ExternalInput").ap()
    wv_d = nc.dram_tensor("wv", [D, D], F32, kind="ExternalInput").ap()
    wo_d = nc.dram_tensor("wo", [D, D], F32, kind="ExternalInput").ap()
    wr_d = nc.dram_tensor("wr", [D, D], F32, kind="ExternalInput").ap()
    wfk_d = nc.dram_tensor("wfk", [D, D], F32, kind="ExternalInput").ap()
    wfv_d = nc.dram_tensor("wfv", [D, D], F32, kind="ExternalInput").ap()
    gam_d = nc.dram_tensor("gam", [D], F32, kind="ExternalInput").ap()
    bet_d = nc.dram_tensor("bet", [D], F32, kind="ExternalInput").ap()
    bq_d = nc.dram_tensor("bq", [1, D], F32, kind="ExternalInput").ap()
    bk_d = nc.dram_tensor("bk", [1, D], F32, kind="ExternalInput").ap()
    bv_d = nc.dram_tensor("bv", [1, D], F32, kind="ExternalInput").ap()
    bo_d = nc.dram_tensor("bo", [1, D], F32, kind="ExternalInput").ap()
    mur_d = nc.dram_tensor("mur", [D], F32, kind="ExternalInput").ap()
    muk_d = nc.dram_tensor("muk", [D], F32, kind="ExternalInput").ap()
    tab_d = nc.dram_tensor("tab", [L, 1536], F32, kind="ExternalInput").ap()
    id_d = nc.dram_tensor("ident", [128, 128], BF16, kind="ExternalInput").ap()
    mask_d = nc.dram_tensor("maskt", [128, 896], BF16, kind="ExternalInput").ap()
    out_d = nc.dram_tensor("out", [L, D], F32, kind="ExternalOutput").ap()

    _sk = "ExternalOutput" if DEBUG_IO else "Internal"
    qtr_d = nc.dram_tensor("qtr_s", [128, 8 * L], BF16, kind=_sk).ap()
    ktr_d = nc.dram_tensor("ktr_s", [128, 8 * L], BF16, kind=_sk).ap()
    vau_d = nc.dram_tensor("vau_s", [128, NT * 1040], BF16, kind=_sk).ap()
    atn_d = nc.dram_tensor("atn_s", [128, 8 * L], BF16, kind=_sk).ap()
    wrT_d = nc.dram_tensor("wrT_s", [128, 8192], BF16, kind="Internal").ap()
    ydbg_d = nc.dram_tensor("ydbg_s", [128, NT * 1024], F32, kind="Internal").ap()
    xndbg_d = nc.dram_tensor("xndbg_s", [128, NT * 1024], BF16,
                             kind=("ExternalOutput" if DEBUG_IO else "Internal")).ap()
    rsdbg_d = nc.dram_tensor("rsdbg_s", [128, NT * 4], F32,
                             kind=("ExternalOutput" if DEBUG_IO else "Internal")).ap()
    zdbg_d = nc.dram_tensor("zdbg_s", [128, NT * 1024], F32, kind="Internal").ap()
    wkT_d = nc.dram_tensor("wkT_s", [128, 8192], BF16, kind="Internal").ap()
    wvT_d = nc.dram_tensor("wvT_s", [128, 8192], BF16, kind="Internal").ap()

    DT_SZ = {F32: 4, BF16: 2}

    def mk_region(base):
        cur = [base]

        def alloc(name, shape, dtype):
            off = cur[0]
            nbytes = shape[1] * DT_SZ[dtype]
            cur[0] = (off + nbytes + 31) // 32 * 32
            return nc.alloc_sbuf_tensor_at(name, shape, dtype, offset=off)
        return alloc, cur

    SBUF_LO = 16512
    SBUF_HI = 229344
    sh, sh_cur = mk_region(SBUF_LO)
    id_sb = sh("id_sb", [128, 128], BF16)
    mask_sb = sh("mask_sb", [128, 896], BF16)
    negc_sb = sh("negc_sb", [128, 1], F32)
    gam_sb = sh("gam_sb", [128, 8], F32)
    bet_sb = sh("bet_sb", [128, 8], F32)
    betbf_sb = sh("betbf_sb", [128, 8], BF16)
    mur_sb = sh("mur_sb", [128, 8], F32)
    muk_sb = sh("muk_sb", [128, 8], F32)
    omur_sb = sh("omur_sb", [128, 8], F32)
    omuk_sb = sh("omuk_sb", [128, 8], F32)
    onec_sb = sh("onec_sb", [128, 8], F32)
    ones_sb = sh("ones_sb", [128, 128], BF16)
    qkmax_sb = sh("qkmax_sb", [128, 32], F32)
    nrm1_sb = sh("nrm1_sb", [128, 32], F32)
    nrm2_sb = sh("nrm2_sb", [128, 128], BF16)
    nrm3_sb = sh("nrm3_sb", [128, 4], F32)
    mzc_sb = sh("mzc_sb", [128, 2], BF16)
    eps_sb = sh("eps_sb", [128, 1], F32)
    boh_row = sh("boh_row", [1, 1024], BF16)
    bol_row = sh("bol_row", [1, 1024], BF16)
    bqh_row = sh("bqh_row", [1, 1024], BF16)
    bql_row = sh("bql_row", [1, 1024], BF16)
    bkh_row = sh("bkh_row", [1, 1024], BF16)
    bkl_row = sh("bkl_row", [1, 1024], BF16)
    bvh_row = sh("bvh_row", [1, 1024], BF16)
    bvl_row = sh("bvl_row", [1, 1024], BF16)
    st6a2 = [sh(f"st6a_{i}", [128, 8], F32) for i in range(2)]
    st6b2 = [sh(f"st6b_{i}", [128, 8], F32) for i in range(2)]
    mv1_2 = [sh(f"mv1_{i}", [128, 2], F32) for i in range(2)]
    mv2_2 = [sh(f"mv2_{i}", [128, 2], F32) for i in range(2)]
    mvc_2 = [sh(f"mvc_{i}", [128, 2], F32) for i in range(2)]
    lntmp_2 = [sh(f"lntmp_{i}", [128, 2], F32) for i in range(2)]
    rs_2 = [sh(f"rs_{i}", [128, 1], F32) for i in range(2)]
    row_a = sh("row_a", [1, 1024], F32)
    row_b = sh("row_b", [1, 1024], F32)
    row_c = sh("row_c", [1, 1024], F32)
    R0 = (sh_cur[0] + 31) // 32 * 32

    pa, pa_cur = mk_region(R0)
    wq_sb = pa("wq_sb", [128, 8192], BF16)
    wk_sb = pa("wk_sb", [128, 8192], BF16)
    wv_sb = pa("wv_sb", [128, 8192], BF16)
    wpm_sb = pa("wpm_sb", [128, 8192], BF16)
    xbuf = [pa(f"xbuf{i}", [128, 1024], F32) for i in range(2)]
    tabbuf = [pa(f"tabbuf{i}", [128, 1536], F32) for i in range(2)]
    xnbf = [pa(f"xnbf{i}", [128, 1024], BF16) for i in range(2)]
    xnT = [pa(f"xnT{i}", [128, 1024], BF16) for i in range(2)]
    qkf_q = pa("qkf_q", [128, 1024], F32)
    qkf_k = pa("qkf_k", [128, 1024], F32)
    ropet = {('q', i): pa(f"ropetq{i}", [128, 512], F32) for i in range(4)}
    ropet.update({('k', i): pa(f"ropetk{i}", [128, 512], F32) for i in range(4)})
    qbf_sb = pa("qbf_sb", [128, 1024], BF16)
    kbf_sb = pa("kbf_sb", [128, 1024], BF16)
    sq_sb = pa("sq_sb", [128, 1024], F32)
    sq2_sb = pa("sq2_sb", [128, 1024], F32)
    trev = [pa(f"trev{i}", [128, 1024], BF16) for i in range(2)]
    vab = [pa(f"vab{i}", [128, 1040], BF16) for i in range(2)]
    wtmp_sb = pa("wtmp_sb", [128, 8192], BF16)
    hi_row = pa("hi_row", [1, 1024], BF16)
    lo_row = pa("lo_row", [1, 1024], BF16)
    hif_row = pa("hif_row", [1, 1024], F32)
    assert pa_cur[0] <= SBUF_HI, pa_cur[0]

    pb, pb_cur = mk_region(R0)
    ktr_sb = pb("ktr_sb", [128, 32768], BF16)
    vau_sb = pb("vau_sb", [128, 33280], BF16)
    qtr_sb = [pb(f"qtr_sb{i}", [128, 4096], BF16) for i in range(2)]
    ptb_bufs = [pb(f"ptb{i}", [128, 1024], BF16) for i in range(4)]
    repev_sb = pb("repev_sb", [128, 512], F32)
    recip_sb = pb("recip_sb", [128, 512], F32)
    rcbf_sb = pb("rcbf_sb", [128, 512], BF16)
    atn_sb = pb("atn_sb", [128, 8192], BF16)
    assert pb_cur[0] <= SBUF_HI, pb_cur[0]

    pc, pc_cur = mk_region(R0)
    wo_sb = pc("wo_sb", [128, 8192], BF16)
    wrT_sb = pc("wrT_sb", [128, 8192], BF16)
    wkT_sb = pc("wkT_sb", [128, 8192], BF16)
    wvT_sb = pc("wvT_sb", [128, 8192], BF16)
    atc_sb = [pc(f"atc_sb{i}", [128, 4096], BF16) for i in range(2)]
    xcb = [pc(f"xcb{i}", [128, 1024], F32) for i in range(2)]
    ysb = pc("ysb", [128, 1024], F32)
    z1_sb = pc("z1_sb", [128, 4096], F32)
    z1bf_off = pc_cur[0]
    z1bf_sb = pc("z1bf_sb", [128, 1024], BF16)
    z1bf2_off = pc_cur[0]
    z1bf_b2 = pc("z1bf_b2", [128, 1024], BF16)
    z1T_sb = pc("z1T_sb", [128, 4104], BF16)
    wkTm_sb = pc("wkTm_sb", [128, 8192], BF16)
    wrTm_sb = pc("wrTm_sb", [128, 8192], BF16)
    d1_sb = pc("d1_sb", [128, 4096], BF16)
    kactT_sb = pc("kactT_sb", [128, 4096], BF16)
    kact_t = nc.alloc_sbuf_tensor_at("kact_t", [128, 512], F32, offset=z1bf_off)
    sgt_sb = nc.alloc_sbuf_tensor_at("sgt_sb", [128, 512], F32, offset=z1bf2_off)
    r_sb = pc("r_sb", [128, 1024], F32)
    y2_sb = ysb
    outb0 = pc("outb0", [128, 1024], F32)
    outb = [outb0, outb0]
    assert pc_cur[0] <= SBUF_HI, pc_cur[0]

    ps_s = [nc.alloc_psum_tensor("ps_s0", [128, 1024], F32),
            nc.alloc_psum_tensor("ps_s1", [128, 1024], F32)]
    ps_a = [nc.alloc_psum_tensor("ps_a0", [128, 512], F32),
            nc.alloc_psum_tensor("ps_a1", [128, 512], F32)]
    ps_rp = nc.alloc_psum_tensor("ps_rp", [128, 512], F32)
    ps_tr = nc.alloc_psum_tensor("ps_tr", [128, 1024], BF16)

    sch = Sched(nc, stack)
    cSc = sch.chan("cSc")
    cV = sch.chan("cV")
    cT = sch.chan("cT")
    cSyStart = sch.chan("cSyStart")
    cBo = sch.chan("cBo")
    cWq = sch.chan("cWq"); cWk = sch.chan("cWk"); cWv = sch.chan("cWv")
    cWtmp = sch.chan("cWtmp")
    cWTw = sch.chan("cWTw")
    cX = [sch.chan("cX0"), sch.chan("cX1")]
    cTab = [sch.chan("cTab0"), sch.chan("cTab1")]
    cQw = sch.chan("cQw"); cKw = sch.chan("cKw")
    cVw = [sch.chan("cVw0"), sch.chan("cVw1")]
    cKV = sch.chan("cKV")
    cQtr = [sch.chan("cQtr0"), sch.chan("cQtr1")]
    cAtnW = sch.chan("cAtnW")
    cWoG = sch.chan("cWoG")
    cWc = sch.chan("cWc")
    cAtc = [sch.chan("cAtc0"), sch.chan("cAtc1")]
    cXc = [sch.chan("cXc0"), sch.chan("cXc1")]
    cOut = [sch.chan("cOut0"), sch.chan("cOut1")]
    cP = sch.chan("cP")

    bank_s = [PBank(ps_s[0]), PBank(ps_s[1])]
    bank_a = [PBank(ps_a[0]), PBank(ps_a[1])]
    bank_r = PBank(ps_rp)
    bank_t = PBank(ps_tr)
    ps_tr_f32 = ps_tr[:].bitcast(F32)

    def vself():
        return sch.op("vector", lambda e: e.engine_nop(), inc=cV)

    def vw(val):
        return (cV, val) if val else None

    ld_q = sch.op("gpsimd", lambda e: e.dma_start(
        out=wpm_sb[:].rearrange("p (ci d) -> p ci d", d=1024), in_=wq_d.rearrange("(ci p) d -> p ci d", p=128)), inc=cWq, inc_n=16)
    ld_k = sch.op("gpsimd", lambda e: e.dma_start(
        out=wtmp_sb[:].rearrange("p (ci d) -> p ci d", d=1024), in_=wk_d.rearrange("(ci p) d -> p ci d", p=128)),
        inc=cWk, inc_n=16)
    sch.op("gpsimd", lambda e: e.dma_start(
        out=wv_sb[:].rearrange("p (ci d) -> p ci d", d=1024), in_=wv_d.rearrange("(ci p) d -> p ci d", p=128)), inc=cWv, inc_n=16)
    pq_vals = []
    for ci in range(8):
        pq_vals.append(sch.op("scalar", lambda e, ci=ci: e.copy(
            wq_sb[:, 1024 * ci:1024 * ci + 1024].rearrange(
                "p (h z dd) -> p h z dd", z=2, dd=32),
            wpm_sb[:, 1024 * ci:1024 * ci + 1024].rearrange(
                "p (h dd z) -> p h z dd", z=2, dd=32)),
            waits=[(cWq, ld_q)], inc=cSc))
    perm_q = pq_vals[-1]
    pk_vals = []
    for ci in range(8):
        pk_vals.append(sch.op("scalar", lambda e, ci=ci: e.copy(
            wk_sb[:, 1024 * ci:1024 * ci + 1024].rearrange(
                "p (h z dd) -> p h z dd", z=2, dd=32),
            wtmp_sb[:, 1024 * ci:1024 * ci + 1024].rearrange(
                "p (h dd z) -> p h z dd", z=2, dd=32)),
            waits=[(cWk, ld_k)], inc=cSc))
    perm_k = pk_vals[-1]

    sch.op("sync", lambda e: e.dma_start(out=id_sb[:], in_=id_d), inc=cSyStart, inc_n=16)
    sch.op("sync", lambda e: e.dma_start(out=mask_sb[:], in_=mask_d), inc=cSyStart, inc_n=16)
    for k in range(8):
        for dd, dst in ((gam_d, gam_sb), (bet_d, bet_sb), (mur_d, mur_sb), (muk_d, muk_sb)):
            sch.op("scalar", lambda e, k=k, dd=dd, dst=dst: e.dma_start(
                out=dst[:, k:k + 1], in_=dd[128 * k:128 * (k + 1)].unsqueeze(1)),
                inc=cSyStart, inc_n=16)
    for dd, dst in ((bq_d, row_a), (bk_d, row_b), (bv_d, row_c)):
        sch.op("sync", lambda e, dd=dd, dst=dst: e.dma_start(out=dst[:], in_=dd),
               inc=cSyStart, inc_n=16)
    startup_sy = cSyStart.n

    sch.op("vector", lambda e: e.memset(onec_sb[:], 1.0))
    sch.op("vector", lambda e: e.memset(ones_sb[:], 1.0))
    sch.op("vector", lambda e: e.memzero(rcbf_sb[64:128, :]))
    sch.op("vector", lambda e: e.memzero(mzc_sb[:]))
    sch.op("vector", lambda e: e.memset(eps_sb[:], EPS))
    sch.op("gpsimd", lambda e: e.memzero(qkmax_sb[:]))
    sch.op("vector", lambda e: e.tensor_sub(omur_sb[:], onec_sb[:], mur_sb[:]),
           waits=[(cSyStart, startup_sy)])
    sch.op("vector", lambda e: e.tensor_sub(omuk_sb[:], onec_sb[:], muk_sb[:]))
    v_betbf = sch.op("vector", lambda e: e.tensor_copy(betbf_sb[:], bet_sb[:]), inc=cV)

    prow = None
    for rr in (row_a, row_b):
        p1 = sch.op("vector", lambda e, rr=rr: e.tensor_copy(hif_row[:], rr[:]),
                    waits=[(cSyStart, startup_sy), vw(prow)], inc=cV)
        prow = sch.op("vector", lambda e, rr=rr: e.tensor_copy(
            rr[:].rearrange("o (h z dd) -> o h z dd", z=2, dd=32),
            hif_row[:].rearrange("o (h dd z) -> o h z dd", z=2, dd=32)),
            waits=[(cV, p1)], inc=cV)

    bmm_last = {}
    beta_done = {}
    for mi, (w_sb, dst_row) in enumerate(((wq_sb, row_a), (wk_sb, row_b), (wv_sb, row_c))):
        for half in range(2):
            bank = bank_a[half]

            def f(e, w_sb=w_sb, half=half, bank=bank):
                last = None
                for ci in range(8):
                    last = e.matmul(bank.t[0:1, :], betbf_sb[:, ci:ci + 1],
                                    w_sb[:, 1024 * ci + 512 * half:1024 * ci + 512 * half + 512],
                                    start=(ci == 0), stop=(ci == 7), skip_group_check=True)
                return last
            wld = ((cSc, perm_q), (cSc, perm_k), (cWv, 16))[mi]
            t = sch.op("tensor", f, waits=[wld, vw(v_betbf)] + bank.acquire(),
                       inc=cT)
            bmm_last[mi] = t
            vv = sch.op("vector", lambda e, dst_row=dst_row, half=half, bank=bank: e.tensor_add(
                dst_row[:, 512 * half:512 * half + 512], dst_row[:, 512 * half:512 * half + 512],
                bank.t[0:1, 0:512]), waits=[(cT, t), (cSyStart, startup_sy), vw(prow)], inc=cV)
            bank.consumed(cV, vv)
            beta_done[mi] = vv

    gscale_done = None
    for mi, w_sb in enumerate((wq_sb, wk_sb, wv_sb)):
        for ci in range(8):
            gscale_done = sch.op("vector", lambda e, w_sb=w_sb, ci=ci: e.tensor_scalar(
                w_sb[:, 1024 * ci:1024 * ci + 1024], w_sb[:, 1024 * ci:1024 * ci + 1024],
                gam_sb[:, ci:ci + 1], None, ALU.mult),
                waits=[(cT, bmm_last[mi])], inc=cV)

    def split_row(src_row, dst_hi, dst_lo, extra_wait=None, prev=None):
        v1 = sch.op("vector", lambda e: e.tensor_copy(dst_hi[:], src_row[:]),
                    waits=[extra_wait, vw(prev)], inc=cV)
        v2 = sch.op("vector", lambda e: e.tensor_copy(hif_row[:], dst_hi[:]),
                    waits=[(cV, v1)], inc=cV)
        v3 = sch.op("vector", lambda e: e.tensor_sub(hif_row[:], src_row[:], hif_row[:]),
                    waits=[(cV, v2)], inc=cV)
        v4 = sch.op("vector", lambda e: e.tensor_copy(dst_lo[:], hif_row[:]),
                    waits=[(cV, v3)], inc=cV)
        return v4

    vq_rep = split_row(row_a, bqh_row, bql_row, extra_wait=(cV, beta_done[0]))
    vk_rep = split_row(row_b, bkh_row, bkl_row, extra_wait=(cV, beta_done[1]), prev=vq_rep)
    vv_rep = split_row(row_c, bvh_row, bvl_row, extra_wait=(cV, beta_done[2]), prev=vk_rep)
    bo_ld = sch.op("sync", lambda e: e.dma_start(out=row_a[:], in_=bo_d),
                   waits=[(cV, vq_rep)], inc=cBo, inc_n=16)
    vbo_rep = split_row(row_a, boh_row, bol_row, extra_wait=(cBo, bo_ld), prev=vv_rep)

    wtr_prev_t = None
    last_wT_dma = None
    for mi, (wd, wTd) in enumerate(((wr_d, wrT_d), (wfk_d, wkT_d), (wfv_d, wvT_d))):
        g_ld = sch.op("gpsimd", lambda e, wd=wd: e.dma_start(
            out=wtmp_sb[:].rearrange("p (di c) -> p di c", c=1024), in_=wd.rearrange("(di p) c -> p di c", p=128)),
            waits=[((cSc, perm_k) if wtr_prev_t is None else (cT, wtr_prev_t))], inc=cWtmp, inc_n=16)
        for ci in range(8):
            def f(e, ci=ci):
                last = None
                for di in range(8):
                    last = e.transpose(bank_t.t[:, 128 * di:128 * di + 128],
                                       wtmp_sb[:, 1024 * di + 128 * ci:1024 * di + 128 * ci + 128],
                                       id_sb[:])
                return last
            t = sch.op("tensor", f, waits=[(cWtmp, g_ld), (cSyStart, 16)] + bank_t.acquire(), inc=cT)
            wtr_prev_t = t
            s = sch.op("scalar", lambda e: e.copy(trev[0][:], bank_t.t[:]),
                       waits=[(cT, t), (None if last_wT_dma is None else (cWTw, last_wT_dma))],
                       inc=cSc)
            bank_t.consumed(cSc, s)
            last_wT_dma = sch.op("gpsimd", lambda e, wTd=wTd, ci=ci: e.dma_start(
                out=wTd[:, 1024 * ci:1024 * ci + 1024], in_=trev[0][:]),
                waits=[(cSc, s)], inc=cWTw, inc_n=16)
    wT_writes_done = cWTw.n

    ln_prev_s1 = [None, None]
    ln_prev_xn = [None, None]

    def emit_ln(sl, src, extra_wait=None):
        vst1 = sch.op("vector", lambda e, sl=sl, src=src: e.bn_stats(
            st6a2[sl][:, 0:6], src[:, 0:512]), waits=[extra_wait], inc=cV)
        vst2 = sch.op("vector", lambda e, sl=sl, src=src: e.bn_stats(
            st6b2[sl][:, 0:6], src[:, 512:1024]), inc=cV)
        sch.op("vector", lambda e, sl=sl: e.bn_aggr(mv1_2[sl][:], st6a2[sl][:, 0:6]),
               waits=[(cV, vst1)])
        vag = sch.op("vector", lambda e, sl=sl: e.bn_aggr(mv2_2[sl][:], st6b2[sl][:, 0:6]),
                     waits=[(cV, vst2)], inc=cV)
        i1 = sch.op("vector", lambda e, sl=sl: e.tensor_add(
            mvc_2[sl][:], mv1_2[sl][:], mv2_2[sl][:]),
            waits=[(cV, vag),
                   (None if ln_prev_s1[sl] is None else (cSc, ln_prev_s1[sl]))], inc=cV)
        i2 = sch.op("vector", lambda e, sl=sl: e.tensor_scalar(
            mvc_2[sl][:], mvc_2[sl][:], 0.5, None, ALU.mult), waits=[(cV, i1)], inc=cV)
        i3 = sch.op("vector", lambda e, sl=sl: e.tensor_sub(
            lntmp_2[sl][:, 0:1], mv1_2[sl][:, 0:1], mv2_2[sl][:, 0:1]), inc=cV)
        i4 = sch.op("vector", lambda e, sl=sl: e.tensor_mul(
            lntmp_2[sl][:, 0:1], lntmp_2[sl][:, 0:1], lntmp_2[sl][:, 0:1]),
            waits=[(cV, i3)], inc=cV)
        i5 = sch.op("vector", lambda e, sl=sl: e.tensor_scalar(
            lntmp_2[sl][:, 0:1], lntmp_2[sl][:, 0:1], 0.25, None, ALU.mult),
            waits=[(cV, i4)], inc=cV)
        vstat = sch.op("vector", lambda e, sl=sl: e.tensor_add(
            mvc_2[sl][:, 1:2], mvc_2[sl][:, 1:2], lntmp_2[sl][:, 0:1]),
            waits=[(cV, i5), (cV, i2)], inc=cV)
        s1 = sch.op("scalar", lambda e, sl=sl: e.activation(
            rs_2[sl][:], mvc_2[sl][:, 1:2], AF.Ln, bias=eps_sb[:]),
            waits=[(cV, vstat),
                   (None if ln_prev_xn[sl] is None else (cV, ln_prev_xn[sl]))], inc=cSc)
        s2 = sch.op("scalar", lambda e, sl=sl: e.activation(
            rs_2[sl][:], rs_2[sl][:], AF.Exp, scale=-0.5), inc=cSc)
        ln_prev_s1[sl] = s1
        return s1, s2

    def ln_used(sl, xn_v):
        ln_prev_xn[sl] = xn_v

    last_xn_v = {}
    last_tab_use = {}
    last_tr = {}
    tx_vals = {}
    trev_cons = {}
    vab_cons = {}
    xnT_cons = {}
    rope_cons = {"v": None}
    sq_done = {}
    red_done = {}
    qk_vv_t = {}
    tab_t = {}
    x_t = {}
    rope_qkf_done = {}

    def evens(ap):
        return ap.rearrange("p (h z d) -> p h z d", z=2, d=32)[:, :, 0, :]

    def odds(ap):
        return ap.rearrange("p (h z d) -> p h z d", z=2, d=32)[:, :, 1, :]

    rope_p = {}
    rope_vr = {}

    def emit_rope_muls(t):
        sl = t % 2
        tabin = tab_t[t]
        tabc = tabbuf[sl][:, 0:512].rearrange("p (h d) -> p h d", d=32)
        tabs = tabbuf[sl][:, 512:1024].rearrange("p (h d) -> p h d", d=32)
        tabns = tabbuf[sl][:, 1024:1536].rearrange("p (h d) -> p h d", d=32)
        for nm in ('q', 'k'):
            qkf = qkf_q if nm == 'q' else qkf_k
            x1 = qkf[:].rearrange("p (h z dd) -> p h z dd", z=2, dd=32)[:, :, 0, :]
            x2 = qkf[:].rearrange("p (h z dd) -> p h z dd", z=2, dd=32)[:, :, 1, :]
            rset = [ropet[(nm, i)][:].rearrange("p (h d) -> p h d", d=32) for i in range(4)]
            w_ws = rope_vr.get((t - 2, nm))
            sch.op("gpsimd", lambda e, x1=x1, tabc=tabc, r0=rset[0]: e.tensor_mul(
                r0, x1, tabc),
                waits=[(cTab[sl], tabin), (cSc, qk_vv_t[(t, nm)]), vw(w_ws)])
            p2 = sch.op("gpsimd", lambda e, x2=x2, tabns=tabns, r1=rset[1]: e.tensor_mul(
                r1, x2, tabns), inc=cP)
            sch.op("gpsimd", lambda e, x1=x1, tabs=tabs, r2=rset[2]: e.tensor_mul(
                r2, x1, tabs))
            p4 = sch.op("gpsimd", lambda e, x2=x2, tabc=tabc, r3=rset[3]: e.tensor_mul(
                r3, x2, tabc), inc=cP)
            rope_p[(t, nm)] = (p2, p4)
            rope_qkf_done[(t, nm)] = p4
            if nm == 'k':
                last_tab_use[t] = p4

    def emit_rope_dve(t, nm):
        dst_bf = qbf_sb if nm == 'q' else kbf_sb
        p2, p4 = rope_p[(t, nm)]
        rset = [ropet[(nm, i)][:].rearrange("p (h d) -> p h d", d=32) for i in range(4)]
        sch.op("vector", lambda e, dst_bf=dst_bf, r0=rset[0], r1=rset[1]: e.tensor_add(
            evens(dst_bf), r0, r1),
            waits=[(cP, p2),
                   (None if (t - 1, nm) not in last_tr else (cT, last_tr[(t - 1, nm)])),
                   (None if (t - 1, nm) not in sq_done else (cSc, sq_done[(t - 1, nm)]))])
        vr = sch.op("vector", lambda e, dst_bf=dst_bf, r2=rset[2], r3=rset[3]: e.tensor_add(
            odds(dst_bf), r2, r3), waits=[(cP, p4)], inc=cV)
        rope_vr[(t, nm)] = vr
        noff = 0 if nm == 'q' else 16
        sqbuf = sq_sb if nm == 'q' else sq2_sb
        sqv = sch.op("scalar", lambda e, dst_bf=dst_bf, sqbuf=sqbuf: e.activation(
            sqbuf[:], dst_bf[:], AF.Square),
            waits=[(cV, vr),
                   (None if (t - 1, nm) not in red_done else (cV, red_done[(t - 1, nm)]))],
            inc=cSc)
        sq_done[(t, nm)] = sqv
        prd = sch.op("vector", lambda e, noff=noff, sqbuf=sqbuf: e.reduce_sum(
            nrm1_sb[:, noff:noff + 16], sqbuf[:].rearrange("p (h d) -> p h d", d=64),
            axis=AX.X), waits=[(cSc, sqv)], inc=cV)
        red_done[(t, nm)] = sch.op("vector", lambda e, noff=noff: e.tensor_max(
            qkmax_sb[:, noff:noff + 16], qkmax_sb[:, noff:noff + 16],
            nrm1_sb[:, noff:noff + 16]), waits=[(cV, prd)], inc=cV)

    def emit_rope_tr(t, nm):
        dst_bf = qbf_sb if nm == 'q' else kbf_sb
        dram = qtr_d if nm == 'q' else ktr_d
        slot = 0 if nm == 'q' else 1
        vr = rope_vr[(t, nm)]

        def f_tr(e, dst_bf=dst_bf):
            last = None
            for j in range(8):
                last = e.transpose(bank_t.t[:, 128 * j:128 * j + 128],
                                   dst_bf[:, 128 * j:128 * j + 128], id_sb[:])
            return last
        tt = sch.op("tensor", f_tr, waits=[vw(vr)] + bank_t.acquire(), inc=cT)
        last_tr[(t, nm)] = tt
        w_slot = trev_cons.get((t - 1, slot))
        seed_w = (cWTw, last_wT_dma) if (t == 0 and slot == 0) else None
        se = sch.op("scalar", lambda e, slot=slot: e.copy(trev[slot][:], bank_t.t[:]),
                    waits=[(cT, tt), seed_w,
                           (None if w_slot is None else ((cQw if slot == 0 else cKw), w_slot))],
                    inc=cSc)
        bank_t.consumed(cSc, se)
        trev_cons[(t, slot)] = sch.op("sync", lambda e, dram=dram, slot=slot, t=t: e.dma_start(
            out=dram.rearrange("p (di c) -> p di c", c=L)[:, :, 128 * t:128 * t + 128],
            in_=trev[slot][:].rearrange("p (di j) -> p di j", j=128)),
            waits=[(cSc, se)], inc=(cQw if slot == 0 else cKw), inc_n=16)

    qk_aps = [ps_s[0][:][:, 0:512], ps_s[0][:][:, 512:1024],
              ps_s[1][:][:, 0:512], ps_s[1][:][:, 512:1024]]

    def emit_xdma_x(t):
        sl = t % 2
        x_t[t] = sch.op("sync", lambda e, t=t, sl=sl: e.dma_start(
            out=xbuf[sl][:], in_=x_d[128 * t:128 * (t + 1), :]),
            waits=[vw(last_xn_v.get(t - 2))], inc=cX[sl], inc_n=16)

    def emit_xdma_tab(t):
        sl = t % 2
        tab_t[t] = sch.op("sync", lambda e, t=t, sl=sl: e.dma_start(
            out=tabbuf[sl][:], in_=tab_d[128 * t:128 * (t + 1), :]),
            waits=[(None if (t - 2) not in last_tab_use else (cP, last_tab_use[t - 2]))],
            inc=cTab[sl], inc_n=16)

    def emit_ln_norm(t):
        sl = t % 2
        s1, s2 = emit_ln(sl, xbuf[sl][:], extra_wait=(cX[sl], x_t[t]))
        last_xn_v[t] = sch.op("vector", lambda e, t=t, sl=sl: e.tensor_scalar(
            xnbf[sl][:], xbuf[sl][:], mvc_2[sl][:, 0:1], rs_2[sl][:], ALU.subtract, ALU.mult),
            waits=[(cSc, s2), (None if (t - 2) not in tx_vals else (cT, tx_vals[t - 2]))],
            inc=cV)
        ln_used(sl, last_xn_v[t])

    emit_xdma_x(0)
    emit_xdma_x(1)
    emit_xdma_tab(0)
    for t in range(NT):
        sl = t % 2
        if t == 0:
            emit_ln_norm(0)
        if t >= 1:
            emit_rope_muls(t - 1)
        if t + 1 < NT:
            emit_xdma_tab(t + 1)
        if t + 2 < NT:
            emit_xdma_x(t + 2)

        if DEBUG_IO:
            dxn = sch.op("gpsimd", lambda e, sl=sl, t=t: e.dma_start(
                out=xndbg_d[:, 1024 * t:1024 * (t + 1)], in_=xnbf[sl][:]),
                waits=[(cV, last_xn_v[t])], inc=cVw[0], inc_n=16)
            sch.op("vector", lambda e: e.engine_nop(), waits=[(cVw[0], dxn)])

        def f_trxn(e, sl=sl):
            last = None
            for j in range(8):
                last = e.transpose(bank_t.t[:, 128 * j:128 * j + 128],
                                   xnbf[sl][:, 128 * j:128 * j + 128], id_sb[:])
            return last
        tx = sch.op("tensor", f_trxn, waits=[vw(last_xn_v[t])] + bank_t.acquire(), inc=cT)
        tx_vals[t] = tx
        sxn = sch.op("scalar", lambda e, sl=sl: e.copy(xnT[sl][:], bank_t.t[:]),
                     waits=[(cT, tx),
                            (None if (t - 2) not in xnT_cons else (cT, xnT_cons[t - 2]))],
                     inc=cSc)
        bank_t.consumed(cSc, sxn)

        if t >= 1:
            emit_rope_dve(t - 1, 'q')
            emit_rope_dve(t - 1, 'k')

        for nm, w_sb, bs, aps, bh, bl in (
                ('q', wq_sb, bank_s[0], (qk_aps[0], qk_aps[1]), bqh_row, bql_row),
                ('k', wk_sb, bank_s[1], (qk_aps[2], qk_aps[3]), bkh_row, bkl_row)):
            def f_p(e, sl=sl, w_sb=w_sb, aps=aps, bh=bh, bl=bl):
                last = None
                for ci in range(8):
                    for half in range(2):
                        e.matmul(aps[half], xnT[sl][:, 128 * ci:128 * ci + 128],
                                 w_sb[:, 1024 * ci + 512 * half:1024 * ci + 512 * half + 512],
                                 start=(ci == 0), stop=False, skip_group_check=True)
                for half in range(2):
                    e.matmul(aps[half], ones_sb[0:1, :],
                             bh[:, 512 * half:512 * half + 512],
                             start=False, stop=False, skip_group_check=True)
                    last = e.matmul(aps[half], ones_sb[0:1, :],
                                    bl[:, 512 * half:512 * half + 512],
                                    start=False, stop=True, skip_group_check=True)
                return last
            waits = [(cSc, sxn)] + bs.acquire()
            if t == 0:
                waits += [vw(gscale_done), vw(vbo_rep)]
            tp_ = sch.op("tensor", f_p, waits=waits, inc=cT)
            dst_f = qkf_q if nm == 'q' else qkf_k
            w_rp = rope_qkf_done.get((t - 1, nm))
            for half in range(2):
                vv = sch.op("scalar", lambda e, half=half, dst_f=dst_f,
                            ap=aps[half]: e.copy(
                    dst_f[:, 512 * half:512 * half + 512], ap),
                    waits=[(cT, tp_),
                           (None if w_rp is None else (cP, w_rp))], inc=cSc)
                if half == 1:
                    qk_vv_t[(t, nm)] = vv
                    bs.consumed(cSc, vv)
        if t + 1 < NT:
            emit_ln_norm(t + 1)
        if t >= 1:
            emit_rope_tr(t - 1, 'q')
            emit_rope_tr(t - 1, 'k')

        w_slot_v = vab_cons.get(t - 2)

        def f_v(e, sl=sl):
            last = None
            for ci in range(8):
                for half in range(2):
                    e.matmul(bank_a[half].t[:], xnT[sl][:, 128 * ci:128 * ci + 128],
                             wv_sb[:, 1024 * ci + 512 * half:1024 * ci + 512 * half + 512],
                             start=(ci == 0), stop=False, skip_group_check=True)
            for half in range(2):
                e.matmul(bank_a[half].t[:], ones_sb[0:1, :],
                         bvh_row[:, 512 * half:512 * half + 512],
                         start=False, stop=False, skip_group_check=True)
                last = e.matmul(bank_a[half].t[:], ones_sb[0:1, :],
                                bvl_row[:, 512 * half:512 * half + 512],
                                start=False, stop=True, skip_group_check=True)
            return last
        tv = sch.op("tensor", f_v,
                    waits=[(cSc, sxn)] + bank_a[0].acquire() + bank_a[1].acquire(), inc=cT)
        xnT_cons[t] = tv
        for half in range(2):
            vv = sch.op("scalar", lambda e, half=half, sl=sl: e.copy(
                vab[sl][:].rearrange("p (h c) -> p h c", c=65)[:, 8 * half:8 * half + 8, 0:64],
                bank_a[half].t[:].rearrange("p (h d) -> p h d", d=64)),
                waits=[(cT, tv),
                       (None if w_slot_v is None else (cVw[sl], w_slot_v))], inc=cSc)
            bank_a[half].consumed(cSc, vv)
        vab_ev = vv
        if t < 2:
            vones = sch.op("vector", lambda e, sl=sl: e.memset(
                vab[sl][:].rearrange("p (h c) -> p h c", c=65)[:, :, 64:65], 1.0), inc=cV)
            vab_cons[t] = sch.op("sync", lambda e, sl=sl, t=t: e.dma_start(
                out=vau_d[:, 1040 * t:1040 * (t + 1)], in_=vab[sl][:]),
                waits=[(cV, vones), (cSc, vab_ev)], inc=cVw[sl], inc_n=16)
        else:
            vab_cons[t] = sch.op("sync", lambda e, sl=sl, t=t: e.dma_start(
                out=vau_d[:, 1040 * t:1040 * (t + 1)], in_=vab[sl][:]),
                waits=[(cSc, vab_ev)], inc=cVw[sl], inc_n=16)

    emit_rope_muls(NT - 1)
    emit_rope_dve(NT - 1, 'q')
    emit_rope_dve(NT - 1, 'k')
    emit_rope_tr(NT - 1, 'q')
    emit_rope_tr(NT - 1, 'k')

    e1 = sch.op("vector", lambda e: e.reduce_max(nrm3_sb[:, 0:1], qkmax_sb[:], axis=AX.X),
                waits=[(cV, red_done[(NT - 1, 'k')])], inc=cV)
    e2 = sch.op("vector", lambda e: e.memzero(nrm2_sb[:]), waits=[(cV, e1)], inc=cV)
    e3 = sch.op("vector", lambda e: e.tensor_copy(nrm2_sb[:, 0:1], nrm3_sb[:, 0:1]),
                waits=[(cV, e2), (cV, e1)], inc=cV)
    te = sch.op("tensor", lambda e: e.transpose(bank_t.t[:, 0:128], nrm2_sb[:], id_sb[:]),
                waits=[vw(e3)] + bank_t.acquire(), inc=cT)
    se = sch.op("scalar", lambda e: e.copy(nrm2_sb[:], bank_t.t[:, 0:128]),
                waits=[(cT, te)], inc=cSc)
    bank_t.consumed(cSc, se)
    e4 = sch.op("vector", lambda e: e.tensor_copy(sq_sb[0:1, 0:128], nrm2_sb[0:1, :]),
                waits=[(cSc, se)], inc=cV)
    e5 = sch.op("vector", lambda e: e.reduce_max(nrm3_sb[0:1, 1:2], sq_sb[0:1, 0:128], axis=AX.X),
                waits=[(cV, e4)], inc=cV)
    e6 = sch.op("vector", lambda e: e.tensor_scalar(nrm3_sb[0:1, 2:3], nrm3_sb[0:1, 1:2],
                                                    -0.1275, None, ALU.mult),
                waits=[(cV, e5)], inc=cV)
    e7 = sch.op("vector", lambda e: e.tensor_copy(mzc_sb[0:1, 0:1], nrm3_sb[0:1, 2:3]),
                waits=[(cV, e6)], inc=cV)
    tneg = sch.op("tensor", lambda e: e.matmul(
        ps_rp[:, 0:1], ones_sb[0:64, 0:128], mzc_sb[0:64, 0:1], start=True, stop=True,
        skip_group_check=True),
        waits=[vw(e7)] + bank_r.acquire(), inc=cT)
    negc_done = sch.op("vector", lambda e: e.tensor_copy(negc_sb[:], ps_rp[:, 0:1]),
                       waits=[(cT, tneg)], inc=cV)
    bank_r.consumed(cV, negc_done)
    if DEBUG_IO:
        dbg_d = nc.dram_tensor("dbg_s", [128, 40], F32, kind="ExternalOutput").ap()
        sch.op("vector", lambda e: e.tensor_copy(sq_sb[:, 200:201], negc_sb[:]),
               waits=[(cV, negc_done)])
        dv = sch.op("vector", lambda e: e.tensor_copy(sq_sb[:, 201:233], qkmax_sb[:]), inc=cV)
        sch.op("gpsimd", lambda e: e.dma_start(out=dbg_d[:, 0:1], in_=sq_sb[:, 200:201], allow_slow_non_contiguous=True),
               waits=[(cV, dv)], inc=cVw[0], inc_n=16)
        sch.op("gpsimd", lambda e: e.dma_start(out=dbg_d[:, 1:33], in_=sq_sb[:, 201:233], allow_slow_non_contiguous=True),
               inc=cVw[0], inc_n=16)

    dma_chansA = [cSyStart, cBo, cWq, cWk, cWv, cWtmp, cWTw, cX[0], cX[1],
                  cTab[0], cTab[1], cQw, cKw, cVw[0], cVw[1]]
    barA = [(c0, c0.n) for c0 in dma_chansA] + [(cSc, cSc.n), (cV, cV.n), (cT, cT.n)]

    cVau = sch.chan("cVau")
    cKV2 = sch.chan("cKV2")
    for i in range(2):
        sch.op("sync", lambda e, i=i: e.dma_start(
            out=ktr_sb[:, 8192 * i:8192 * (i + 1)], in_=ktr_d[:, 8192 * i:8192 * (i + 1)]),
            waits=barA, inc=cKV, inc_n=16)
    for i in range(2, 4):
        sch.op("gpsimd", lambda e, i=i: e.dma_start(
            out=ktr_sb[:, 8192 * i:8192 * (i + 1)], in_=ktr_d[:, 8192 * i:8192 * (i + 1)]),
            waits=(barA if i == 2 else []), inc=cKV2, inc_n=16)
    for i in range(4):
        sch.op("scalar", lambda e, i=i: e.dma_start(
            out=vau_sb[:, 8320 * i:8320 * (i + 1)], in_=vau_d[:, 8320 * i:8320 * (i + 1)]),
            waits=(barA if i == 0 else []), inc=cVau, inc_n=16)
    kv_done = cKV.n

    qtr_ld = {}
    S_last_of_group = {}
    div_vals = []
    atw_vals = {}
    u = 0
    av_pair = []
    av_banks = [bank_a[0], bank_a[1], bank_t]
    av_aps = [ps_a[0][:], ps_a[1][:], ps_tr_f32]
    pend_epi = []
    epi_state = {"rc_last_cons": None, "rcbf_last_T": None}

    def emit_qtr_ld(g):
        w_slot = (cT, S_last_of_group[g - 2]) if g - 2 in S_last_of_group else None
        qtr_ld[g] = sch.op("sync", lambda e, g=g: e.dma_start(
            out=qtr_sb[g % 2][:].rearrange("p (di i) -> p di i", i=512),
            in_=qtr_d.rearrange("p (di c) -> p di c", c=L)[:, :, 512 * g:512 * g + 512]),
            waits=[w_slot], inc=cQtr[g % 2], inc_n=16)

    for g in range(NG):
        if g == 0:
            emit_qtr_ld(0)
        if g + 1 < NG:
            emit_qtr_ld(g + 1)

        nkt = 4 * g + 4
        kt_order = list(range(4 * g, 4 * g + 4)) + list(range(0, 4 * g))
        for hp in range(8):
            pend_av = []
            for ki, kt in enumerate(kt_order):
                sb = bank_s[u % 2]

                def f_s(e, hp=hp, kt=kt, g=g, sb=sb):
                    e.matmul(sb.t[:, 0:512],
                             ktr_sb[0:64, 4096 * hp + 128 * kt:4096 * hp + 128 * kt + 128],
                             qtr_sb[g % 2][0:64, 512 * hp:512 * hp + 512],
                             start=True, stop=True, skip_group_check=True, tile_position=(0, 0))
                    return e.matmul(sb.t[:, 512:1024],
                                    ktr_sb[64:128, 4096 * hp + 128 * kt:4096 * hp + 128 * kt + 128],
                                    qtr_sb[g % 2][64:128, 512 * hp:512 * hp + 512],
                                    start=True, stop=True, skip_group_check=True,
                                    tile_position=(64, 0))
                kchunk = hp // 2
                kw_ = (cKV, 16 * (kchunk + 1)) if kchunk < 2 else (cKV2, 16 * (kchunk - 1))
                waits = [(cQtr[g % 2], qtr_ld[g]), kw_] + sb.acquire()
                if u == 0:
                    waits += [vw(negc_done)]
                ts_ = sch.op("tensor", f_s, waits=waits, inc=cT)
                if ki == nkt - 1 and hp == 7:
                    S_last_of_group[g] = ts_

                pslot = u % 4
                w_pt = (cT, av_pair[u - 4][1]) if u >= 4 else None
                ex = sch.op("scalar", lambda e, sb=sb, pslot=pslot: e.activation(
                    ptb_bufs[pslot][:], sb.t[:], AF.Exp, bias=negc_sb[:, 0:1], scale=0.125),
                    waits=[(cT, ts_), w_pt], inc=cSc)
                sb.consumed(cSc, ex)

                m = kt - 4 * g
                use_w_a = use_w_b = (cSc, ex)
                if 0 <= m <= 3:
                    mo = 384 - 128 * m
                    vma = sch.op("vector", lambda e, pslot=pslot, mo=mo: e.tensor_mul(
                        ptb_bufs[pslot][:, 0:512], ptb_bufs[pslot][:, 0:512],
                        mask_sb[:, mo:mo + 512]), waits=[(cSc, ex)], inc=cV)
                    vmb = sch.op("vector", lambda e, pslot=pslot, mo=mo: e.tensor_mul(
                        ptb_bufs[pslot][:, 512:1024], ptb_bufs[pslot][:, 512:1024],
                        mask_sb[:, mo:mo + 512]), waits=[(cSc, ex)], inc=cV)
                    use_w_a = (cV, vma)
                    use_w_b = (cV, vmb)

                def emit_av(ki2, kt2, pslot2, uwa, uwb, hp=hp, g=g, nkt=nkt):
                    tas = []
                    for a in range(2):
                        h = 2 * hp + a
                        ab = av_banks[h % 3]
                        abt = av_aps[h % 3]

                        def f_av(e, kt2=kt2, h=h, pslot2=pslot2, a=a, abt=abt,
                                 ki2=ki2, nkt=nkt):
                            return e.matmul(abt[0:65, :],
                                            vau_sb[:, 1040 * kt2 + 65 * h:1040 * kt2 + 65 * h + 65],
                                            ptb_bufs[pslot2][:, 512 * a:512 * a + 512],
                                            start=(ki2 == 0), stop=(ki2 == nkt - 1),
                                            skip_group_check=True)
                        aw = [uwa if a == 0 else uwb, (cVau, 16 * (kt2 // 8 + 1))]
                        if ki2 == 0:
                            hg = 16 * g + h
                            if hg >= 3:
                                aw.append((cV, div_vals[hg - 3]))
                            aw += ab.acquire()
                        ta = sch.op("tensor", f_av, waits=aw, inc=cT)
                        tas.append(ta)
                    av_pair.append(tuple(tas))

                if ki == 2 and pend_epi:
                    for f_ in pend_epi:
                        f_()
                    pend_epi.clear()
                if len(pend_av) == 2:
                    emit_av(*pend_av.pop(0))
                pend_av.append((ki, kt, pslot, use_w_a, use_w_b))
                u += 1
            for p_ in pend_av:
                emit_av(*p_)
            pend_av = []

            for a in range(2):
                h = 2 * hp + a
                hg = 16 * g + h
                ab = av_banks[h % 3]
                abt = av_aps[h % 3]
                av_done = av_pair[-1][a]

                def epi(a=a, h=h, hg=hg, ab=ab, abt=abt, av_done=av_done, g=g):
                    sr1 = sch.op("scalar", lambda e, abt=abt: e.activation(
                        recip_sb[64:65, :], abt[64:65, :], AF.Ln),
                        waits=[(cT, av_done), (None if epi_state["rc_last_cons"] is None
                                               else (cSc, epi_state["rc_last_cons"]))], inc=cSc)
                    sr2 = sch.op("scalar", lambda e: e.activation(
                        rcbf_sb[64:65, :], recip_sb[64:65, :], AF.Exp, scale=-1.0),
                        waits=[(None if epi_state["rcbf_last_T"] is None
                                else (cT, epi_state["rcbf_last_T"]))], inc=cSc)
                    epi_state["rc_last_cons"] = sr2
                    tr_ = sch.op("tensor", lambda e: e.matmul(
                        bank_r.t[0:64, :], ones_sb[64:128, 0:64], rcbf_sb[64:128, :],
                        start=True, stop=True, skip_group_check=True),
                        waits=[(cSc, sr2)] + bank_r.acquire(), inc=cT)
                    epi_state["rcbf_last_T"] = tr_
                    ve = sch.op("vector", lambda e: e.tensor_copy(
                        repev_sb[0:64, :], bank_r.t[0:64, :]),
                        waits=[(cT, tr_)], inc=cV)
                    bank_r.consumed(cV, ve)
                    w_at = (cAtnW, atw_vals[g - 1]) if (h == 0 and g - 1 in atw_vals) else None
                    vd_ = sch.op("vector", lambda e, h=h, abt=abt: e.tensor_mul(
                        atn_sb[0:64, 512 * h:512 * h + 512], abt[0:64, :], repev_sb[0:64, :]),
                        waits=[w_at], inc=cV)
                    ab.consumed(cV, vd_)
                    div_vals.append(vd_)
                pend_epi.append(epi)
            if hp == 7:
                for f_ in pend_epi:
                    f_()
                pend_epi.clear()

        sch.op("gpsimd", lambda e, g=g: e.dma_start(
            out=atn_d.rearrange("(par p) (g2 h2 i) -> p par g2 h2 i",
                                par=2, h2=8, i=512)[:, 0, g, :, :],
            in_=atn_sb[0:64, :].rearrange("p (h2 par i) -> p par h2 i",
                                          par=2, i=512)[:, 0, :, :]),
            waits=[(cV, div_vals[16 * g + 15])], inc=cAtnW, inc_n=16)
        atw_vals[g] = sch.op("gpsimd", lambda e, g=g: e.dma_start(
            out=atn_d.rearrange("(par p) (g2 h2 i) -> p par g2 h2 i",
                                par=2, h2=8, i=512)[:, 1, g, :, :],
            in_=atn_sb[0:64, :].rearrange("p (h2 par i) -> p par h2 i",
                                          par=2, i=512)[:, 1, :, :]),
            inc=cAtnW, inc_n=16)

    dma_chansB = dma_chansA + [cKV, cKV2, cVau, cQtr[0], cQtr[1], cAtnW]
    barB = [(c0, c0.n) for c0 in dma_chansB] + [(cSc, cSc.n), (cV, cV.n), (cT, cT.n)]

    sch.op("gpsimd", lambda e: e.dma_start(
        out=wo_sb[:].rearrange("p (ci d) -> p ci d", d=1024), in_=wo_d.rearrange("(ci p) d -> p ci d", p=128)),
        waits=barB, inc=cWoG, inc_n=16)
    for wsb, wdd in ((wrT_sb, wrT_d), (wkT_sb, wkT_d), (wvT_sb, wvT_d)):
        sch.op("sync", lambda e, wsb=wsb, wdd=wdd: e.dma_start(out=wsb[:], in_=wdd),
               waits=barB, inc=cWc, inc_n=16)
    wC_sy = cWc.n
    wC_g = cWoG.n

    wscale_done = None
    for wsrc, wdst, musb in ((wkT_sb, wkTm_sb, muk_sb), (wrT_sb, wrTm_sb, mur_sb)):
        for ci2 in range(8):
            wscale_done = sch.op("vector", lambda e, wsrc=wsrc, wdst=wdst, musb=musb,
                                 ci2=ci2: e.tensor_scalar(
                wdst[:, 1024 * ci2:1024 * ci2 + 1024],
                wsrc[:, 1024 * ci2:1024 * ci2 + 1024],
                musb[:, ci2:ci2 + 1], None, ALU.mult),
                waits=[(cWc, wC_sy)] + (barB if ci2 == 0 else []), inc=cV)

    atc_ld = {}
    xc_ld = {}
    xcb_cons = {}
    bank_rC = [PBank(None), PBank(None)]
    bank_kvC = [PBank(None), PBank(None)]
    pend_trz = []
    trz_T = {}
    z1bf2 = [z1bf_sb, z1bf_b2]
    bank_rC[0].last = bank_s[0].last
    bank_rC[1].last = bank_s[0].last
    bank_kvC[0].last = bank_s[1].last
    bank_kvC[1].last = bank_s[1].last
    r_aps = [ps_s[0][:][:, 0:512], ps_s[0][:][:, 512:1024]]
    kv_aps = [ps_s[1][:][:, 0:512], ps_s[1][:][:, 512:1024]]
    out_st = {}
    y_done = {}
    z1T_evac = {}
    wo_last = {}
    grp_end_T = {}
    r_cons = {}
    kact_cons = None
    z1padd = {}
    d1_done = {}

    def emit_atc_ld(g):
        w_slot = (cT, wo_last[g - 2]) if g - 2 in wo_last else None
        atc_ld[g] = sch.op("sync", lambda e, g=g: e.dma_start(
            out=atc_sb[g % 2][:], in_=atn_d[:, 4096 * g:4096 * (g + 1)]),
            waits=[w_slot, (cWc, wC_sy)], inc=cAtc[g % 2], inc_n=16)

    for g in range(NG):
        if g == 0:
            emit_atc_ld(0)
        if g + 1 < NG:
            emit_atc_ld(g + 1)

        if g == 0:
            vh = sch.op("vector", lambda e: e.memset(
                z1T_sb[:].rearrange("p (ci w) -> p ci w", w=513)[:, :, 0:1], 0.0),
                waits=barB, inc=cV)
        else:
            vh = sch.op("vector", lambda e: e.tensor_copy(
                z1T_sb[:].rearrange("p (ci w) -> p ci w", w=513)[:, :, 0:1],
                z1T_sb[:].rearrange("p (ci w) -> p ci w", w=513)[:, :, 512:513]),
                waits=[(cSc, z1T_evac[4 * g - 1]), (cT, grp_end_T[g - 1])], inc=cV)
        vh_g = vh

        def emit_d1(tt, g=g, vh_g=vh_g):
            w = [(cSc, z1T_evac[4 * g + tt])]
            if tt == 0:
                w.append(vw(vh_g))
                if g >= 1:
                    w.append((cT, grp_end_T[g - 1]))
            d1_done[(g, tt)] = sch.op("vector", lambda e, tt=tt: e.tensor_sub(
                d1_sb[:].rearrange("p (ci w) -> p ci w", w=512)[:, :, 128 * tt:128 * tt + 128],
                z1T_sb[:].rearrange("p (ci w) -> p ci w", w=513)[:, :, 1 + 128 * tt:129 + 128 * tt],
                z1T_sb[:].rearrange("p (ci w) -> p ci w", w=513)[:, :, 128 * tt:128 * tt + 128]),
                waits=w, inc=cV)

        for tt in range(GT):
            ct = 4 * g + tt
            xc_ld[ct] = sch.op("scalar", lambda e, ct=ct: e.dma_start(
                out=xcb[ct % 2][:], in_=x_d[128 * ct:128 * (ct + 1), :]),
                waits=[vw(xcb_cons.get(ct - 2))], inc=cXc[ct % 2], inc_n=16)
            for half in range(2):
                bank = bank_a[half]

                def f(e, g=g, tt=tt, half=half, bank=bank):
                    for ci2 in range(8):
                        e.matmul(
                            bank.t[:],
                            atc_sb[g % 2][:, 512 * ci2 + 128 * tt:512 * ci2 + 128 * tt + 128],
                            wo_sb[:, 1024 * ci2 + 512 * half:1024 * ci2 + 512 * half + 512],
                            start=(ci2 == 0), stop=False, skip_group_check=True)
                    e.matmul(bank.t[:], ones_sb[0:1, :],
                             boh_row[:, 512 * half:512 * half + 512],
                             start=False, stop=False, skip_group_check=True)
                    return e.matmul(bank.t[:], ones_sb[0:1, :],
                                    bol_row[:, 512 * half:512 * half + 512],
                                    start=False, stop=True, skip_group_check=True)
                two = sch.op("tensor", f,
                             waits=[(cAtc[g % 2], atc_ld[g]), (cWoG, wC_g)] + bank.acquire(), inc=cT)
                if half == 1:
                    wo_last[g] = two
                vvx = sch.op("vector", lambda e, half=half, ct=ct, bank=bank: e.scalar_tensor_tensor(
                    ysb[:, 512 * half:512 * half + 512],
                    xcb[ct % 2][:, 512 * half:512 * half + 512], 2.0, bank.t[:],
                    ALU.mult, ALU.add),
                    waits=[(cT, two), (cXc[ct % 2], xc_ld[ct])], inc=cV)
                if half == 1:
                    xcb_cons[ct] = vvx
                bank.consumed(cV, vvx)
            sl2 = ct % 2
            s1, s2 = emit_ln(sl2, ysb[:], extra_wait=(cV, vvx))
            vz = sch.op("vector", lambda e, tt=tt, sl2=sl2: e.tensor_scalar(
                z1_sb[:, 1024 * tt:1024 * tt + 1024], ysb[:], mvc_2[sl2][:, 0:1], rs_2[sl2][:],
                ALU.subtract, ALU.mult),
                waits=[(cSc, s2),
                       (None if tt not in z1padd else (cP, z1padd[tt]))], inc=cV)
            ln_used(sl2, vz)
            zb = sch.op("scalar", lambda e, tt=tt: e.copy(
                z1bf2[tt % 2][:], z1_sb[:, 1024 * tt:1024 * tt + 1024]),
                waits=[(cV, vz),
                       (None if (ct - 2) not in trz_T else (cT, trz_T[ct - 2]))], inc=cSc)

            def mk_trz(tt=tt, ct=ct, zb=zb):
                def doit():
                    def f_trz(e, sl2=tt % 2):
                        last = None
                        for j in range(8):
                            last = e.transpose(bank_t.t[:, 128 * j:128 * j + 128],
                                               z1bf2[sl2][:, 128 * j:128 * j + 128], id_sb[:])
                        return last
                    tz = sch.op("tensor", f_trz, waits=[(cSc, zb)] + bank_t.acquire(), inc=cT)
                    trz_T[ct] = tz
                    sz = sch.op("scalar", lambda e, tt=tt: e.copy(
                        z1T_sb[:].rearrange("p (ci w) -> p ci w", w=513)[:, :, 1 + 128 * tt:129 + 128 * tt],
                        bank_t.t[:].rearrange("p (ci j) -> p ci j", j=128)),
                        waits=[(cT, tz), vw(vh)], inc=cSc)
                    bank_t.consumed(cSc, sz)
                    z1T_evac[ct] = sz
                    emit_d1(tt)
                return doit
            pend_trz.append(mk_trz())
            if len(pend_trz) > 1:
                pend_trz.pop(0)()

        for f_ in pend_trz:
            f_()
        pend_trz.clear()
        d1_last = d1_done[(g, 3)]

        z1Tv_g = z1T_sb[:].rearrange("p (ci w) -> p ci w", w=513)
        d1v_g = d1_sb[:].rearrange("p (ci w) -> p ci w", w=512)
        sq_vals = {}
        for di in range(8):
            bank = bank_a[di % 2]

            def f(e, di=di, bank=bank):
                last = None
                for ci2 in range(8):
                    e.matmul(bank.t[:],
                             wkT_sb[:, 1024 * ci2 + 128 * di:1024 * ci2 + 128 * di + 128],
                             z1Tv_g[:, ci2, 0:512],
                             start=(ci2 == 0), stop=False, skip_group_check=True)
                    last = e.matmul(bank.t[:],
                                    wkTm_sb[:, 1024 * ci2 + 128 * di:1024 * ci2 + 128 * di + 128],
                                    d1v_g[:, ci2, 0:512],
                                    start=False, stop=(ci2 == 7), skip_group_check=True)
                return last
            kw = [(cV, d1_last)] + bank.acquire()
            if g == 0:
                kw.append((cV, wscale_done))
            tk = sch.op("tensor", f, waits=kw, inc=cT)
            sk = sch.op("scalar", lambda e, bank=bank: e.activation(kact_t[:], bank.t[:], AF.Relu),
                        waits=[(cT, tk)], inc=cSc)
            bank.consumed(cSc, sk)
            sq_vals[di] = sch.op("scalar", lambda e, di=di: e.activation(
                kactT_sb[:, 512 * di:512 * di + 512], kact_t[:], AF.Square),
                waits=[(None if g == 0 else (cT, grp_end_T[g - 1]))], inc=cSc)
        kact_done = sq_vals[7]

        for tt in range(GT):
            ct = 4 * g + tt
            for half in range(2):

                def f_r(e, tt=tt, half=half, rap=r_aps[half]):
                    last = None
                    for ci2 in range(8):
                        e.matmul(
                            rap,
                            z1Tv_g[:, ci2, 128 * tt:128 * tt + 128],
                            wrT_sb[:, 1024 * ci2 + 512 * half:1024 * ci2 + 512 * half + 512],
                            start=(ci2 == 0), stop=False, skip_group_check=True)
                        last = e.matmul(
                            rap,
                            d1v_g[:, ci2, 128 * tt:128 * tt + 128],
                            wrTm_sb[:, 1024 * ci2 + 512 * half:1024 * ci2 + 512 * half + 512],
                            start=False, stop=(ci2 == 7), skip_group_check=True)
                    return last
                rb = bank_rC[half]
                rap = r_aps[half]
                rw = [(cV, d1_done[(g, tt)])] + rb.acquire()
                if g == 0 and tt == 0:
                    rw.append((cV, wscale_done))
                tr_ = sch.op("tensor", f_r, waits=rw, inc=cT)
                sa = sch.op("scalar", lambda e, rap=rap: e.activation(
                    sgt_sb[:], rap, AF.Exp, scale=-1.0),
                    waits=[(cT, tr_), vw(r_cons.get((ct - 1, half)) if tt > 0 or g > 0 else None)],
                    inc=cSc)
                rb.consumed(cSc, sa)
                sch.op("scalar", lambda e: e.activation(sgt_sb[:], sgt_sb[:], AF.Ln, bias=onec_sb[:, 0:1]))
                sc_ = sch.op("scalar", lambda e, half=half: e.activation(
                    r_sb[:, 512 * half:512 * half + 512], sgt_sb[:], AF.Exp, scale=-1.0), inc=cSc)

                def f_kv(e, tt=tt, half=half, kap=kv_aps[half]):
                    last = None
                    for ci2 in range(8):
                        last = e.matmul(
                            kap,
                            kactT_sb[:, 512 * ci2 + 128 * tt:512 * ci2 + 128 * tt + 128],
                            wvT_sb[:, 1024 * ci2 + 512 * half:1024 * ci2 + 512 * half + 512],
                            start=(ci2 == 0), stop=(ci2 == 7), skip_group_check=True)
                    return last
                kvb = bank_kvC[half]
                tkv = sch.op("tensor", f_kv,
                             waits=[(cSc, kact_done)] + kvb.acquire(), inc=cT)
                vp = sch.op("vector", lambda e, half=half, kap=kv_aps[half]: e.tensor_mul(
                    y2_sb[:, 512 * half:512 * half + 512], kap,
                    r_sb[:, 512 * half:512 * half + 512]),
                    waits=[(cT, tkv), (cSc, sc_)], inc=cV)
                kvb.consumed(cV, vp)
                r_cons[(ct, half)] = vp
                pad = sch.op("gpsimd", lambda e, half=half, tt=tt: e.tensor_add(
                    y2_sb[:, 512 * half:512 * half + 512], y2_sb[:, 512 * half:512 * half + 512],
                    z1_sb[:, 1024 * tt + 512 * half:1024 * tt + 512 * half + 512]),
                    waits=[(cV, vp)], inc=cP)
                z1padd[tt] = pad
            sl2 = ct % 2
            s1, s2 = emit_ln(sl2, y2_sb[:], extra_wait=(cP, pad))
            w_ob = (cOut[(ct - 1) % 2], out_st[ct - 1]) if (ct - 1) in out_st else None
            vo = sch.op("vector", lambda e, ct=ct, sl2=sl2: e.tensor_scalar(
                outb[ct % 2][:], y2_sb[:], mvc_2[sl2][:, 0:1], rs_2[sl2][:],
                ALU.subtract, ALU.mult),
                waits=[(cSc, s2), w_ob], inc=cV)
            ln_used(sl2, vo)
            y_done[ct] = vo
            out_st[ct] = sch.op("sync", lambda e, ct=ct: e.dma_start(
                out=out_d[128 * ct:128 * (ct + 1), :], in_=outb[ct % 2][:]),
                waits=[(cV, vo)], inc=cOut[ct % 2], inc_n=16)
        grp_end_T[g] = cT.n
        kact_cons = r_cons[(4 * g + 3, 1)]

    sch.op("gpsimd", lambda e: e.nop(), waits=[(cOut[0], cOut[0].n), (cOut[1], cOut[1].n)])

    sch.emit()
    stack.close()
    return nc


_prog_cache = {}


def _get_prog():
    if "nc" not in _prog_cache:
        _prog_cache["nc"] = build_program()
    return _prog_cache["nc"]


def _host_consts():
    inv_freq = 1.0 / (10000.0 ** (np.arange(0, DH, 2, dtype=np.float64) / DH))
    tpos = np.arange(L, dtype=np.float64)[:, None] * inv_freq[None, :]
    cos_rep = np.tile(np.cos(tpos), (1, H)).astype(np.float32)
    sin_rep = np.tile(np.sin(tpos), (1, H)).astype(np.float32)
    tab = np.concatenate([cos_rep, sin_rep, -sin_rep], axis=1)
    ident = np.eye(128, dtype=ml_dtypes.bfloat16)
    jj = np.arange(128)[:, None]
    cc = np.arange(896)[None, :]
    maskt = (jj <= cc - 384).astype(ml_dtypes.bfloat16)
    return np.ascontiguousarray(tab), ident, np.ascontiguousarray(maskt)


def kernel(L0_attn_x, L0_attn_ln_gamma, L0_attn_ln_beta, L0_attn_Wq, L0_attn_bq,
           L0_attn_Wk, L0_attn_bk, L0_attn_Wv, L0_attn_bv, L0_attn_Wo, L0_attn_bo,
           L0_ffn_W_r, L0_ffn_W_k, L0_ffn_W_v, L0_ffn_mu_r, L0_ffn_mu_k):
    x = np.ascontiguousarray(np.asarray(L0_attn_x, dtype=np.float32))
    tab, ident, maskt = _host_consts()
    common = dict(
        wq=np.ascontiguousarray(np.asarray(L0_attn_Wq, np.float32)),
        wk=np.ascontiguousarray(np.asarray(L0_attn_Wk, np.float32)),
        wv=np.ascontiguousarray(np.asarray(L0_attn_Wv, np.float32)),
        wo=np.ascontiguousarray(np.asarray(L0_attn_Wo, np.float32)),
        wr=np.ascontiguousarray(np.asarray(L0_ffn_W_r, np.float32)),
        wfk=np.ascontiguousarray(np.asarray(L0_ffn_W_k, np.float32)),
        wfv=np.ascontiguousarray(np.asarray(L0_ffn_W_v, np.float32)),
        gam=np.ascontiguousarray(np.asarray(L0_attn_ln_gamma, np.float32)),
        bet=np.ascontiguousarray(np.asarray(L0_attn_ln_beta, np.float32)),
        bq=np.ascontiguousarray(np.asarray(L0_attn_bq, np.float32).reshape(1, D)),
        bk=np.ascontiguousarray(np.asarray(L0_attn_bk, np.float32).reshape(1, D)),
        bv=np.ascontiguousarray(np.asarray(L0_attn_bv, np.float32).reshape(1, D)),
        bo=np.ascontiguousarray(np.asarray(L0_attn_bo, np.float32).reshape(1, D)),
        mur=np.ascontiguousarray(np.asarray(L0_ffn_mu_r, np.float32)),
        muk=np.ascontiguousarray(np.asarray(L0_ffn_mu_k, np.float32)),
        tab=tab, ident=ident, maskt=maskt,
    )
    in_maps = [dict(common, x=np.ascontiguousarray(x[b])) for b in range(8)]
    nc = _get_prog()
    res = run_bass_kernel_spmd(nc, in_maps, core_ids=list(range(8)))
    return np.stack([res.results[b]["out"] for b in range(8)], axis=0)


# version 33
# speedup vs baseline: 1.1810x; 1.0008x over previous
import numpy as np
import ml_dtypes
import contextlib

DEBUG_C = False
DEBUG_IO = False
import concourse.bass as bass
import concourse.mybir as mybir
from concourse.bass_utils import run_bass_kernel_spmd

F32 = mybir.dt.float32
BF16 = mybir.dt.bfloat16
AF = mybir.ActivationFunctionType
ALU = mybir.AluOpType
AX = mybir.AxisListType

L = 4096
D = 1024
NT = 32
NG = 8
GT = 4
H = 16
DH = 64
EPS = 1e-5

ENGS = ("sync", "scalar", "vector", "tensor", "gpsimd")


class Chan:
    __slots__ = ("sem", "n", "name")

    def __init__(self, sem, name):
        self.sem = sem
        self.n = 0
        self.name = name


class PBank:
    def __init__(self, t):
        self.t = t
        self.last = None

    def acquire(self):
        return [self.last] if self.last else []

    def consumed(self, chan, val):
        self.last = (chan, val)


class Sched:
    def __init__(self, nc, stack):
        self.nc = nc
        self.stack = stack
        self.ops = {e: [] for e in ENGS}
        self.hwm = {e: {} for e in ENGS}

    def chan(self, name):
        sem = self.stack.enter_context(self.nc.semaphore(name))
        return Chan(sem, name)

    def op(self, eng, fn, waits=(), inc=None, inc_n=1):
        rw = []
        for w in waits:
            if w is None:
                continue
            ch, v = w
            if v is None or v <= 0:
                continue
            if self.hwm[eng].get(id(ch), 0) >= v:
                continue
            self.hwm[eng][id(ch)] = v
            rw.append((ch, v))
        tgt = None
        if inc is not None:
            inc.n += inc_n
            tgt = inc.n
        self.ops[eng].append((rw, fn, inc, inc_n))
        return tgt

    def emit(self):
        with self.nc.Block() as block:
            for name in ENGS:
                ops = self.ops[name]

                def mk(ops=ops):
                    def run(e):
                        for (rw, fn, inc, inc_n) in ops:
                            for ch, v in rw:
                                e.wait_ge(ch.sem, v)
                            inst = fn(e)
                            if inc is not None:
                                inst.then_inc(inc.sem, inc_n)
                    return run
                getattr(block, name)(mk())


def build_program():
    nc = bass.Bass("TRN2", target_bir_lowering=False)
    stack = contextlib.ExitStack()

    x_d = nc.dram_tensor("x", [L, D], F32, kind="ExternalInput").ap()
    wq_d = nc.dram_tensor("wq", [D, D], F32, kind="ExternalInput").ap()
    wk_d = nc.dram_tensor("wk", [D, D], F32, kind="ExternalInput").ap()
    wv_d = nc.dram_tensor("wv", [D, D], F32, kind="ExternalInput").ap()
    wo_d = nc.dram_tensor("wo", [D, D], F32, kind="ExternalInput").ap()
    wr_d = nc.dram_tensor("wr", [D, D], F32, kind="ExternalInput").ap()
    wfk_d = nc.dram_tensor("wfk", [D, D], F32, kind="ExternalInput").ap()
    wfv_d = nc.dram_tensor("wfv", [D, D], F32, kind="ExternalInput").ap()
    gam_d = nc.dram_tensor("gam", [D], F32, kind="ExternalInput").ap()
    bet_d = nc.dram_tensor("bet", [D], F32, kind="ExternalInput").ap()
    bq_d = nc.dram_tensor("bq", [1, D], F32, kind="ExternalInput").ap()
    bk_d = nc.dram_tensor("bk", [1, D], F32, kind="ExternalInput").ap()
    bv_d = nc.dram_tensor("bv", [1, D], F32, kind="ExternalInput").ap()
    bo_d = nc.dram_tensor("bo", [1, D], F32, kind="ExternalInput").ap()
    mur_d = nc.dram_tensor("mur", [D], F32, kind="ExternalInput").ap()
    muk_d = nc.dram_tensor("muk", [D], F32, kind="ExternalInput").ap()
    tab_d = nc.dram_tensor("tab", [L, 1536], F32, kind="ExternalInput").ap()
    id_d = nc.dram_tensor("ident", [128, 128], BF16, kind="ExternalInput").ap()
    mask_d = nc.dram_tensor("maskt", [128, 896], BF16, kind="ExternalInput").ap()
    out_d = nc.dram_tensor("out", [L, D], F32, kind="ExternalOutput").ap()

    _sk = "ExternalOutput" if DEBUG_IO else "Internal"
    qtr_d = nc.dram_tensor("qtr_s", [128, 8 * L], BF16, kind=_sk).ap()
    ktr_d = nc.dram_tensor("ktr_s", [128, 8 * L], BF16, kind=_sk).ap()
    vau_d = nc.dram_tensor("vau_s", [128, NT * 1040], BF16, kind=_sk).ap()
    atn_d = nc.dram_tensor("atn_s", [128, 8 * L], BF16, kind=_sk).ap()
    wrT_d = nc.dram_tensor("wrT_s", [128, 8192], BF16, kind="Internal").ap()
    ydbg_d = nc.dram_tensor("ydbg_s", [128, NT * 1024], F32, kind="Internal").ap()
    xndbg_d = nc.dram_tensor("xndbg_s", [128, NT * 1024], BF16,
                             kind=("ExternalOutput" if DEBUG_IO else "Internal")).ap()
    rsdbg_d = nc.dram_tensor("rsdbg_s", [128, NT * 4], F32,
                             kind=("ExternalOutput" if DEBUG_IO else "Internal")).ap()
    zdbg_d = nc.dram_tensor("zdbg_s", [128, NT * 1024], F32, kind="Internal").ap()
    wkT_d = nc.dram_tensor("wkT_s", [128, 8192], BF16, kind="Internal").ap()
    wvT_d = nc.dram_tensor("wvT_s", [128, 8192], BF16, kind="Internal").ap()

    DT_SZ = {F32: 4, BF16: 2}

    def mk_region(base):
        cur = [base]

        def alloc(name, shape, dtype):
            off = cur[0]
            nbytes = shape[1] * DT_SZ[dtype]
            cur[0] = (off + nbytes + 31) // 32 * 32
            return nc.alloc_sbuf_tensor_at(name, shape, dtype, offset=off)
        return alloc, cur

    SBUF_LO = 16512
    SBUF_HI = 229344
    sh, sh_cur = mk_region(SBUF_LO)
    id_sb = sh("id_sb", [128, 128], BF16)
    mask_sb = sh("mask_sb", [128, 896], BF16)
    negc_sb = sh("negc_sb", [128, 1], F32)
    gam_sb = sh("gam_sb", [128, 8], F32)
    bet_sb = sh("bet_sb", [128, 8], F32)
    betbf_sb = sh("betbf_sb", [128, 8], BF16)
    mur_sb = sh("mur_sb", [128, 8], F32)
    muk_sb = sh("muk_sb", [128, 8], F32)
    omur_sb = sh("omur_sb", [128, 8], F32)
    omuk_sb = sh("omuk_sb", [128, 8], F32)
    onec_sb = sh("onec_sb", [128, 8], F32)
    ones_sb = sh("ones_sb", [128, 128], BF16)
    qkmax_sb = sh("qkmax_sb", [128, 32], F32)
    nrm1_sb = sh("nrm1_sb", [128, 32], F32)
    nrm2_sb = sh("nrm2_sb", [128, 128], BF16)
    nrm3_sb = sh("nrm3_sb", [128, 4], F32)
    mzc_sb = sh("mzc_sb", [128, 2], BF16)
    eps_sb = sh("eps_sb", [128, 1], F32)
    boh_row = sh("boh_row", [1, 1024], BF16)
    bol_row = sh("bol_row", [1, 1024], BF16)
    bqh_row = sh("bqh_row", [1, 1024], BF16)
    bql_row = sh("bql_row", [1, 1024], BF16)
    bkh_row = sh("bkh_row", [1, 1024], BF16)
    bkl_row = sh("bkl_row", [1, 1024], BF16)
    bvh_row = sh("bvh_row", [1, 1024], BF16)
    bvl_row = sh("bvl_row", [1, 1024], BF16)
    st6a2 = [sh(f"st6a_{i}", [128, 8], F32) for i in range(2)]
    st6b2 = [sh(f"st6b_{i}", [128, 8], F32) for i in range(2)]
    mv1_2 = [sh(f"mv1_{i}", [128, 2], F32) for i in range(2)]
    mv2_2 = [sh(f"mv2_{i}", [128, 2], F32) for i in range(2)]
    mvc_2 = [sh(f"mvc_{i}", [128, 2], F32) for i in range(2)]
    lntmp_2 = [sh(f"lntmp_{i}", [128, 2], F32) for i in range(2)]
    rs_2 = [sh(f"rs_{i}", [128, 1], F32) for i in range(2)]
    row_a = sh("row_a", [1, 1024], F32)
    row_b = sh("row_b", [1, 1024], F32)
    row_c = sh("row_c", [1, 1024], F32)
    R0 = (sh_cur[0] + 31) // 32 * 32

    pa, pa_cur = mk_region(R0)
    wq_sb = pa("wq_sb", [128, 8192], BF16)
    wk_sb = pa("wk_sb", [128, 8192], BF16)
    wv_sb = pa("wv_sb", [128, 8192], BF16)
    wpm_sb = pa("wpm_sb", [128, 8192], BF16)
    xbuf = [pa(f"xbuf{i}", [128, 1024], F32) for i in range(2)]
    tabbuf = [pa(f"tabbuf{i}", [128, 1536], F32) for i in range(2)]
    xnbf = [pa(f"xnbf{i}", [128, 1024], BF16) for i in range(2)]
    xnT = [pa(f"xnT{i}", [128, 1024], BF16) for i in range(2)]
    qkf_q = pa("qkf_q", [128, 1024], F32)
    qkf_k = pa("qkf_k", [128, 1024], F32)
    ropet = {('q', i): pa(f"ropetq{i}", [128, 512], F32) for i in range(4)}
    ropet.update({('k', i): pa(f"ropetk{i}", [128, 512], F32) for i in range(4)})
    qbf_sb = pa("qbf_sb", [128, 1024], BF16)
    kbf_sb = pa("kbf_sb", [128, 1024], BF16)
    sq_sb = pa("sq_sb", [128, 1024], F32)
    sq2_sb = pa("sq2_sb", [128, 1024], F32)
    trev = [pa(f"trev{i}", [128, 1024], BF16) for i in range(2)]
    vab = [pa(f"vab{i}", [128, 1040], BF16) for i in range(2)]
    wtmp_sb = pa("wtmp_sb", [128, 8192], BF16)
    hi_row = pa("hi_row", [1, 1024], BF16)
    lo_row = pa("lo_row", [1, 1024], BF16)
    hif_row = pa("hif_row", [1, 1024], F32)
    assert pa_cur[0] <= SBUF_HI, pa_cur[0]

    pb, pb_cur = mk_region(R0)
    ktr_sb = pb("ktr_sb", [128, 32768], BF16)
    vau_sb = pb("vau_sb", [128, 33280], BF16)
    qtr_sb = [pb(f"qtr_sb{i}", [128, 4096], BF16) for i in range(2)]
    ptb_bufs = [pb(f"ptb{i}", [128, 1024], BF16) for i in range(4)]
    repev_sb = pb("repev_sb", [128, 512], F32)
    recip_sb = pb("recip_sb", [128, 512], F32)
    rcbf_sb = pb("rcbf_sb", [128, 512], BF16)
    atn_sb = pb("atn_sb", [128, 8192], BF16)
    assert pb_cur[0] <= SBUF_HI, pb_cur[0]

    pc, pc_cur = mk_region(R0)
    wo_sb = pc("wo_sb", [128, 8192], BF16)
    wrT_sb = pc("wrT_sb", [128, 8192], BF16)
    wkT_sb = pc("wkT_sb", [128, 8192], BF16)
    wvT_sb = pc("wvT_sb", [128, 8192], BF16)
    atc_sb = [pc(f"atc_sb{i}", [128, 4096], BF16) for i in range(2)]
    xcb = [pc(f"xcb{i}", [128, 1024], F32) for i in range(2)]
    ysb = pc("ysb", [128, 1024], F32)
    z1_sb = pc("z1_sb", [128, 4096], F32)
    z1bf_off = pc_cur[0]
    z1bf_sb = pc("z1bf_sb", [128, 1024], BF16)
    z1bf2_off = pc_cur[0]
    z1bf_b2 = pc("z1bf_b2", [128, 1024], BF16)
    z1T_sb = pc("z1T_sb", [128, 4104], BF16)
    wkTm_sb = pc("wkTm_sb", [128, 8192], BF16)
    wrTm_sb = pc("wrTm_sb", [128, 8192], BF16)
    d1_sb = pc("d1_sb", [128, 4096], BF16)
    kactT_sb = pc("kactT_sb", [128, 4096], BF16)
    kact_t = nc.alloc_sbuf_tensor_at("kact_t", [128, 512], F32, offset=z1bf_off)
    sgt_sb = nc.alloc_sbuf_tensor_at("sgt_sb", [128, 512], F32, offset=z1bf2_off)
    r_sb = pc("r_sb", [128, 1024], F32)
    y2_sb = ysb
    outb0 = pc("outb0", [128, 1024], F32)
    outb = [outb0, outb0]
    assert pc_cur[0] <= SBUF_HI, pc_cur[0]

    ps_s = [nc.alloc_psum_tensor("ps_s0", [128, 1024], F32),
            nc.alloc_psum_tensor("ps_s1", [128, 1024], F32)]
    ps_a = [nc.alloc_psum_tensor("ps_a0", [128, 512], F32),
            nc.alloc_psum_tensor("ps_a1", [128, 512], F32)]
    ps_rp = nc.alloc_psum_tensor("ps_rp", [128, 512], F32)
    ps_tr = nc.alloc_psum_tensor("ps_tr", [128, 1024], BF16)

    sch = Sched(nc, stack)
    cSc = sch.chan("cSc")
    cV = sch.chan("cV")
    cT = sch.chan("cT")
    cSyStart = sch.chan("cSyStart")
    cBo = sch.chan("cBo")
    cWq = sch.chan("cWq"); cWk = sch.chan("cWk"); cWv = sch.chan("cWv")
    cWtmp = sch.chan("cWtmp")
    cWTw = sch.chan("cWTw")
    cX = [sch.chan("cX0"), sch.chan("cX1")]
    cTab = [sch.chan("cTab0"), sch.chan("cTab1")]
    cQw = sch.chan("cQw"); cKw = sch.chan("cKw")
    cVw = [sch.chan("cVw0"), sch.chan("cVw1")]
    cKV = sch.chan("cKV")
    cQtr = [sch.chan("cQtr0"), sch.chan("cQtr1")]
    cAtnW = sch.chan("cAtnW")
    cWoG = sch.chan("cWoG")
    cWc = sch.chan("cWc")
    cAtc = [sch.chan("cAtc0"), sch.chan("cAtc1")]
    cXc = [sch.chan("cXc0"), sch.chan("cXc1")]
    cOut = [sch.chan("cOut0"), sch.chan("cOut1")]
    cP = sch.chan("cP")

    bank_s = [PBank(ps_s[0]), PBank(ps_s[1])]
    bank_a = [PBank(ps_a[0]), PBank(ps_a[1])]
    bank_r = PBank(ps_rp)
    bank_t = PBank(ps_tr)
    ps_tr_f32 = ps_tr[:].bitcast(F32)

    def vself():
        return sch.op("vector", lambda e: e.engine_nop(), inc=cV)

    def vw(val):
        return (cV, val) if val else None

    ld_q = sch.op("gpsimd", lambda e: e.dma_start(
        out=wpm_sb[:].rearrange("p (ci d) -> p ci d", d=1024), in_=wq_d.rearrange("(ci p) d -> p ci d", p=128)), inc=cWq, inc_n=16)
    ld_k = sch.op("gpsimd", lambda e: e.dma_start(
        out=wtmp_sb[:].rearrange("p (ci d) -> p ci d", d=1024), in_=wk_d.rearrange("(ci p) d -> p ci d", p=128)),
        inc=cWk, inc_n=16)
    sch.op("gpsimd", lambda e: e.dma_start(
        out=wv_sb[:].rearrange("p (ci d) -> p ci d", d=1024), in_=wv_d.rearrange("(ci p) d -> p ci d", p=128)), inc=cWv, inc_n=16)
    pq_vals = []
    for ci in range(8):
        pq_vals.append(sch.op("scalar", lambda e, ci=ci: e.copy(
            wq_sb[:, 1024 * ci:1024 * ci + 1024].rearrange(
                "p (h z dd) -> p h z dd", z=2, dd=32),
            wpm_sb[:, 1024 * ci:1024 * ci + 1024].rearrange(
                "p (h dd z) -> p h z dd", z=2, dd=32)),
            waits=[(cWq, ld_q)], inc=cSc))
    perm_q = pq_vals[-1]
    pk_vals = []
    for ci in range(8):
        pk_vals.append(sch.op("scalar", lambda e, ci=ci: e.copy(
            wk_sb[:, 1024 * ci:1024 * ci + 1024].rearrange(
                "p (h z dd) -> p h z dd", z=2, dd=32),
            wtmp_sb[:, 1024 * ci:1024 * ci + 1024].rearrange(
                "p (h dd z) -> p h z dd", z=2, dd=32)),
            waits=[(cWk, ld_k)], inc=cSc))
    perm_k = pk_vals[-1]

    sch.op("sync", lambda e: e.dma_start(out=id_sb[:], in_=id_d), inc=cSyStart, inc_n=16)
    sch.op("sync", lambda e: e.dma_start(out=mask_sb[:], in_=mask_d), inc=cSyStart, inc_n=16)
    for k in range(8):
        for dd, dst in ((gam_d, gam_sb), (bet_d, bet_sb), (mur_d, mur_sb), (muk_d, muk_sb)):
            sch.op("scalar", lambda e, k=k, dd=dd, dst=dst: e.dma_start(
                out=dst[:, k:k + 1], in_=dd[128 * k:128 * (k + 1)].unsqueeze(1)),
                inc=cSyStart, inc_n=16)
    for dd, dst in ((bq_d, row_a), (bk_d, row_b), (bv_d, row_c)):
        sch.op("sync", lambda e, dd=dd, dst=dst: e.dma_start(out=dst[:], in_=dd),
               inc=cSyStart, inc_n=16)
    startup_sy = cSyStart.n

    sch.op("vector", lambda e: e.memset(onec_sb[:], 1.0))
    sch.op("vector", lambda e: e.memset(ones_sb[:], 1.0))
    sch.op("vector", lambda e: e.memzero(rcbf_sb[64:128, :]))
    sch.op("vector", lambda e: e.memzero(mzc_sb[:]))
    sch.op("vector", lambda e: e.memset(eps_sb[:], EPS))
    sch.op("gpsimd", lambda e: e.memzero(qkmax_sb[:]))
    sch.op("vector", lambda e: e.tensor_sub(omur_sb[:], onec_sb[:], mur_sb[:]),
           waits=[(cSyStart, startup_sy)])
    sch.op("vector", lambda e: e.tensor_sub(omuk_sb[:], onec_sb[:], muk_sb[:]))
    v_betbf = sch.op("vector", lambda e: e.tensor_copy(betbf_sb[:], bet_sb[:]), inc=cV)

    prow = None
    for rr in (row_a, row_b):
        p1 = sch.op("vector", lambda e, rr=rr: e.tensor_copy(hif_row[:], rr[:]),
                    waits=[(cSyStart, startup_sy), vw(prow)], inc=cV)
        prow = sch.op("vector", lambda e, rr=rr: e.tensor_copy(
            rr[:].rearrange("o (h z dd) -> o h z dd", z=2, dd=32),
            hif_row[:].rearrange("o (h dd z) -> o h z dd", z=2, dd=32)),
            waits=[(cV, p1)], inc=cV)

    bmm_last = {}
    beta_done = {}
    for mi, (w_sb, dst_row) in enumerate(((wq_sb, row_a), (wk_sb, row_b), (wv_sb, row_c))):
        for half in range(2):
            bank = bank_a[half]

            def f(e, w_sb=w_sb, half=half, bank=bank):
                last = None
                for ci in range(8):
                    last = e.matmul(bank.t[0:1, :], betbf_sb[:, ci:ci + 1],
                                    w_sb[:, 1024 * ci + 512 * half:1024 * ci + 512 * half + 512],
                                    start=(ci == 0), stop=(ci == 7), skip_group_check=True)
                return last
            wld = ((cSc, perm_q), (cSc, perm_k), (cWv, 16))[mi]
            t = sch.op("tensor", f, waits=[wld, vw(v_betbf)] + bank.acquire(),
                       inc=cT)
            bmm_last[mi] = t
            vv = sch.op("vector", lambda e, dst_row=dst_row, half=half, bank=bank: e.tensor_add(
                dst_row[:, 512 * half:512 * half + 512], dst_row[:, 512 * half:512 * half + 512],
                bank.t[0:1, 0:512]), waits=[(cT, t), (cSyStart, startup_sy), vw(prow)], inc=cV)
            bank.consumed(cV, vv)
            beta_done[mi] = vv

    gscale_done = None
    for mi, w_sb in enumerate((wq_sb, wk_sb, wv_sb)):
        for ci in range(8):
            gscale_done = sch.op("vector", lambda e, w_sb=w_sb, ci=ci: e.tensor_scalar(
                w_sb[:, 1024 * ci:1024 * ci + 1024], w_sb[:, 1024 * ci:1024 * ci + 1024],
                gam_sb[:, ci:ci + 1], None, ALU.mult),
                waits=[(cT, bmm_last[mi])], inc=cV)

    def split_row(src_row, dst_hi, dst_lo, extra_wait=None, prev=None):
        v1 = sch.op("vector", lambda e: e.tensor_copy(dst_hi[:], src_row[:]),
                    waits=[extra_wait, vw(prev)], inc=cV)
        v2 = sch.op("vector", lambda e: e.tensor_copy(hif_row[:], dst_hi[:]),
                    waits=[(cV, v1)], inc=cV)
        v3 = sch.op("vector", lambda e: e.tensor_sub(hif_row[:], src_row[:], hif_row[:]),
                    waits=[(cV, v2)], inc=cV)
        v4 = sch.op("vector", lambda e: e.tensor_copy(dst_lo[:], hif_row[:]),
                    waits=[(cV, v3)], inc=cV)
        return v4

    vq_rep = split_row(row_a, bqh_row, bql_row, extra_wait=(cV, beta_done[0]))
    vk_rep = split_row(row_b, bkh_row, bkl_row, extra_wait=(cV, beta_done[1]), prev=vq_rep)
    vv_rep = split_row(row_c, bvh_row, bvl_row, extra_wait=(cV, beta_done[2]), prev=vk_rep)
    bo_ld = sch.op("sync", lambda e: e.dma_start(out=row_a[:], in_=bo_d),
                   waits=[(cV, vq_rep)], inc=cBo, inc_n=16)
    vbo_rep = split_row(row_a, boh_row, bol_row, extra_wait=(cBo, bo_ld), prev=vv_rep)

    wtr_prev_t = None
    last_wT_dma = None
    for mi, (wd, wTd) in enumerate(((wr_d, wrT_d), (wfk_d, wkT_d), (wfv_d, wvT_d))):
        g_ld = sch.op("gpsimd", lambda e, wd=wd: e.dma_start(
            out=wtmp_sb[:].rearrange("p (di c) -> p di c", c=1024), in_=wd.rearrange("(di p) c -> p di c", p=128)),
            waits=[((cSc, perm_k) if wtr_prev_t is None else (cT, wtr_prev_t))], inc=cWtmp, inc_n=16)
        for ci in range(8):
            def f(e, ci=ci):
                last = None
                for di in range(8):
                    last = e.transpose(bank_t.t[:, 128 * di:128 * di + 128],
                                       wtmp_sb[:, 1024 * di + 128 * ci:1024 * di + 128 * ci + 128],
                                       id_sb[:])
                return last
            t = sch.op("tensor", f, waits=[(cWtmp, g_ld), (cSyStart, 16)] + bank_t.acquire(), inc=cT)
            wtr_prev_t = t
            s = sch.op("scalar", lambda e: e.copy(trev[0][:], bank_t.t[:]),
                       waits=[(cT, t), (None if last_wT_dma is None else (cWTw, last_wT_dma))],
                       inc=cSc)
            bank_t.consumed(cSc, s)
            last_wT_dma = sch.op("gpsimd", lambda e, wTd=wTd, ci=ci: e.dma_start(
                out=wTd[:, 1024 * ci:1024 * ci + 1024], in_=trev[0][:]),
                waits=[(cSc, s)], inc=cWTw, inc_n=16)
    wT_writes_done = cWTw.n

    ln_prev_s1 = [None, None]
    ln_prev_xn = [None, None]

    def emit_ln(sl, src, extra_wait=None):
        vst1 = sch.op("vector", lambda e, sl=sl, src=src: e.bn_stats(
            st6a2[sl][:, 0:6], src[:, 0:512]), waits=[extra_wait], inc=cV)
        vst2 = sch.op("vector", lambda e, sl=sl, src=src: e.bn_stats(
            st6b2[sl][:, 0:6], src[:, 512:1024]), inc=cV)
        sch.op("vector", lambda e, sl=sl: e.bn_aggr(mv1_2[sl][:], st6a2[sl][:, 0:6]),
               waits=[(cV, vst1)])
        vag = sch.op("vector", lambda e, sl=sl: e.bn_aggr(mv2_2[sl][:], st6b2[sl][:, 0:6]),
                     waits=[(cV, vst2)], inc=cV)
        i1 = sch.op("vector", lambda e, sl=sl: e.tensor_add(
            mvc_2[sl][:], mv1_2[sl][:], mv2_2[sl][:]),
            waits=[(cV, vag),
                   (None if ln_prev_s1[sl] is None else (cSc, ln_prev_s1[sl]))], inc=cV)
        i2 = sch.op("vector", lambda e, sl=sl: e.tensor_scalar(
            mvc_2[sl][:], mvc_2[sl][:], 0.5, None, ALU.mult), waits=[(cV, i1)], inc=cV)
        i3 = sch.op("vector", lambda e, sl=sl: e.tensor_sub(
            lntmp_2[sl][:, 0:1], mv1_2[sl][:, 0:1], mv2_2[sl][:, 0:1]), inc=cV)
        i4 = sch.op("vector", lambda e, sl=sl: e.tensor_mul(
            lntmp_2[sl][:, 0:1], lntmp_2[sl][:, 0:1], lntmp_2[sl][:, 0:1]),
            waits=[(cV, i3)], inc=cV)
        i5 = sch.op("vector", lambda e, sl=sl: e.tensor_scalar(
            lntmp_2[sl][:, 0:1], lntmp_2[sl][:, 0:1], 0.25, None, ALU.mult),
            waits=[(cV, i4)], inc=cV)
        vstat = sch.op("vector", lambda e, sl=sl: e.tensor_add(
            mvc_2[sl][:, 1:2], mvc_2[sl][:, 1:2], lntmp_2[sl][:, 0:1]),
            waits=[(cV, i5), (cV, i2)], inc=cV)
        s1 = sch.op("scalar", lambda e, sl=sl: e.activation(
            rs_2[sl][:], mvc_2[sl][:, 1:2], AF.Ln, bias=eps_sb[:]),
            waits=[(cV, vstat),
                   (None if ln_prev_xn[sl] is None else (cV, ln_prev_xn[sl]))], inc=cSc)
        s2 = sch.op("scalar", lambda e, sl=sl: e.activation(
            rs_2[sl][:], rs_2[sl][:], AF.Exp, scale=-0.5), inc=cSc)
        ln_prev_s1[sl] = s1
        return s1, s2

    def ln_used(sl, xn_v):
        ln_prev_xn[sl] = xn_v

    last_xn_v = {}
    last_tab_use = {}
    last_tr = {}
    tx_vals = {}
    trev_cons = {}
    vab_cons = {}
    xnT_cons = {}
    rope_cons = {"v": None}
    sq_done = {}
    red_done = {}
    qk_vv_t = {}
    tab_t = {}
    x_t = {}
    rope_qkf_done = {}

    def evens(ap):
        return ap.rearrange("p (h z d) -> p h z d", z=2, d=32)[:, :, 0, :]

    def odds(ap):
        return ap.rearrange("p (h z d) -> p h z d", z=2, d=32)[:, :, 1, :]

    rope_p = {}
    rope_vr = {}

    def emit_rope_muls(t):
        sl = t % 2
        tabin = tab_t[t]
        tabc = tabbuf[sl][:, 0:512].rearrange("p (h d) -> p h d", d=32)
        tabs = tabbuf[sl][:, 512:1024].rearrange("p (h d) -> p h d", d=32)
        tabns = tabbuf[sl][:, 1024:1536].rearrange("p (h d) -> p h d", d=32)
        for nm in ('q', 'k'):
            qkf = qkf_q if nm == 'q' else qkf_k
            x1 = qkf[:].rearrange("p (h z dd) -> p h z dd", z=2, dd=32)[:, :, 0, :]
            x2 = qkf[:].rearrange("p (h z dd) -> p h z dd", z=2, dd=32)[:, :, 1, :]
            rset = [ropet[(nm, i)][:].rearrange("p (h d) -> p h d", d=32) for i in range(4)]
            w_ws = rope_vr.get((t - 2, nm))
            sch.op("gpsimd", lambda e, x1=x1, tabc=tabc, r0=rset[0]: e.tensor_mul(
                r0, x1, tabc),
                waits=[(cTab[sl], tabin), (cSc, qk_vv_t[(t, nm)]), vw(w_ws)])
            p2 = sch.op("gpsimd", lambda e, x2=x2, tabns=tabns, r1=rset[1]: e.tensor_mul(
                r1, x2, tabns), inc=cP)
            sch.op("gpsimd", lambda e, x1=x1, tabs=tabs, r2=rset[2]: e.tensor_mul(
                r2, x1, tabs))
            p4 = sch.op("gpsimd", lambda e, x2=x2, tabc=tabc, r3=rset[3]: e.tensor_mul(
                r3, x2, tabc), inc=cP)
            rope_p[(t, nm)] = (p2, p4)
            rope_qkf_done[(t, nm)] = p4
            if nm == 'k':
                last_tab_use[t] = p4

    def emit_rope_dve(t, nm):
        dst_bf = qbf_sb if nm == 'q' else kbf_sb
        p2, p4 = rope_p[(t, nm)]
        rset = [ropet[(nm, i)][:].rearrange("p (h d) -> p h d", d=32) for i in range(4)]
        sch.op("vector", lambda e, dst_bf=dst_bf, r0=rset[0], r1=rset[1]: e.tensor_add(
            evens(dst_bf), r0, r1),
            waits=[(cP, p2),
                   (None if (t - 1, nm) not in last_tr else (cT, last_tr[(t - 1, nm)])),
                   (None if (t - 1, nm) not in sq_done else (cSc, sq_done[(t - 1, nm)]))])
        vr = sch.op("vector", lambda e, dst_bf=dst_bf, r2=rset[2], r3=rset[3]: e.tensor_add(
            odds(dst_bf), r2, r3), waits=[(cP, p4)], inc=cV)
        rope_vr[(t, nm)] = vr
        noff = 0 if nm == 'q' else 16
        sqbuf = sq_sb if nm == 'q' else sq2_sb
        sqv = sch.op("scalar", lambda e, dst_bf=dst_bf, sqbuf=sqbuf: e.activation(
            sqbuf[:], dst_bf[:], AF.Square),
            waits=[(cV, vr),
                   (None if (t - 1, nm) not in red_done else (cV, red_done[(t - 1, nm)]))],
            inc=cSc)
        sq_done[(t, nm)] = sqv
        prd = sch.op("vector", lambda e, noff=noff, sqbuf=sqbuf: e.reduce_sum(
            nrm1_sb[:, noff:noff + 16], sqbuf[:].rearrange("p (h d) -> p h d", d=64),
            axis=AX.X), waits=[(cSc, sqv)], inc=cV)
        red_done[(t, nm)] = sch.op("vector", lambda e, noff=noff: e.tensor_max(
            qkmax_sb[:, noff:noff + 16], qkmax_sb[:, noff:noff + 16],
            nrm1_sb[:, noff:noff + 16]), waits=[(cV, prd)], inc=cV)

    def emit_rope_tr(t, nm):
        dst_bf = qbf_sb if nm == 'q' else kbf_sb
        dram = qtr_d if nm == 'q' else ktr_d
        slot = 0 if nm == 'q' else 1
        vr = rope_vr[(t, nm)]

        def f_tr(e, dst_bf=dst_bf):
            last = None
            for j in range(8):
                last = e.transpose(bank_t.t[:, 128 * j:128 * j + 128],
                                   dst_bf[:, 128 * j:128 * j + 128], id_sb[:])
            return last
        tt = sch.op("tensor", f_tr, waits=[vw(vr)] + bank_t.acquire(), inc=cT)
        last_tr[(t, nm)] = tt
        w_slot = trev_cons.get((t - 1, slot))
        seed_w = (cWTw, last_wT_dma) if (t == 0 and slot == 0) else None
        se = sch.op("scalar", lambda e, slot=slot: e.copy(trev[slot][:], bank_t.t[:]),
                    waits=[(cT, tt), seed_w,
                           (None if w_slot is None else ((cQw if slot == 0 else cKw), w_slot))],
                    inc=cSc)
        bank_t.consumed(cSc, se)
        trev_cons[(t, slot)] = sch.op("sync", lambda e, dram=dram, slot=slot, t=t: e.dma_start(
            out=dram.rearrange("p (di c) -> p di c", c=L)[:, :, 128 * t:128 * t + 128],
            in_=trev[slot][:].rearrange("p (di j) -> p di j", j=128)),
            waits=[(cSc, se)], inc=(cQw if slot == 0 else cKw), inc_n=16)

    qk_aps = [ps_s[0][:][:, 0:512], ps_s[0][:][:, 512:1024],
              ps_s[1][:][:, 0:512], ps_s[1][:][:, 512:1024]]

    def emit_xdma_x(t):
        sl = t % 2
        x_t[t] = sch.op("sync", lambda e, t=t, sl=sl: e.dma_start(
            out=xbuf[sl][:], in_=x_d[128 * t:128 * (t + 1), :]),
            waits=[vw(last_xn_v.get(t - 2))], inc=cX[sl], inc_n=16)

    def emit_xdma_tab(t):
        sl = t % 2
        tab_t[t] = sch.op("sync", lambda e, t=t, sl=sl: e.dma_start(
            out=tabbuf[sl][:], in_=tab_d[128 * t:128 * (t + 1), :]),
            waits=[(None if (t - 2) not in last_tab_use else (cP, last_tab_use[t - 2]))],
            inc=cTab[sl], inc_n=16)

    def emit_ln_norm(t):
        sl = t % 2
        s1, s2 = emit_ln(sl, xbuf[sl][:], extra_wait=(cX[sl], x_t[t]))
        last_xn_v[t] = sch.op("vector", lambda e, t=t, sl=sl: e.tensor_scalar(
            xnbf[sl][:], xbuf[sl][:], mvc_2[sl][:, 0:1], rs_2[sl][:], ALU.subtract, ALU.mult),
            waits=[(cSc, s2), (None if (t - 2) not in tx_vals else (cT, tx_vals[t - 2]))],
            inc=cV)
        ln_used(sl, last_xn_v[t])

    emit_xdma_x(0)
    emit_xdma_x(1)
    emit_xdma_tab(0)
    for t in range(NT):
        sl = t % 2
        if t == 0:
            emit_ln_norm(0)
        if t >= 1:
            emit_rope_muls(t - 1)
        if t + 1 < NT:
            emit_xdma_tab(t + 1)
        if t + 2 < NT:
            emit_xdma_x(t + 2)

        if DEBUG_IO:
            dxn = sch.op("gpsimd", lambda e, sl=sl, t=t: e.dma_start(
                out=xndbg_d[:, 1024 * t:1024 * (t + 1)], in_=xnbf[sl][:]),
                waits=[(cV, last_xn_v[t])], inc=cVw[0], inc_n=16)
            sch.op("vector", lambda e: e.engine_nop(), waits=[(cVw[0], dxn)])

        def f_trxn(e, sl=sl):
            last = None
            for j in range(8):
                last = e.transpose(bank_t.t[:, 128 * j:128 * j + 128],
                                   xnbf[sl][:, 128 * j:128 * j + 128], id_sb[:])
            return last
        tx = sch.op("tensor", f_trxn, waits=[vw(last_xn_v[t])] + bank_t.acquire(), inc=cT)
        tx_vals[t] = tx
        sxn = sch.op("scalar", lambda e, sl=sl: e.copy(xnT[sl][:], bank_t.t[:]),
                     waits=[(cT, tx),
                            (None if (t - 2) not in xnT_cons else (cT, xnT_cons[t - 2]))],
                     inc=cSc)
        bank_t.consumed(cSc, sxn)

        if t >= 1:
            emit_rope_dve(t - 1, 'q')
            emit_rope_dve(t - 1, 'k')

        for nm, w_sb, bs, aps, bh, bl in (
                ('q', wq_sb, bank_s[0], (qk_aps[0], qk_aps[1]), bqh_row, bql_row),
                ('k', wk_sb, bank_s[1], (qk_aps[2], qk_aps[3]), bkh_row, bkl_row)):
            def f_p(e, sl=sl, w_sb=w_sb, aps=aps, bh=bh, bl=bl):
                last = None
                for ci in range(8):
                    for half in range(2):
                        e.matmul(aps[half], xnT[sl][:, 128 * ci:128 * ci + 128],
                                 w_sb[:, 1024 * ci + 512 * half:1024 * ci + 512 * half + 512],
                                 start=(ci == 0), stop=False, skip_group_check=True)
                for half in range(2):
                    e.matmul(aps[half], ones_sb[0:1, :],
                             bh[:, 512 * half:512 * half + 512],
                             start=False, stop=False, skip_group_check=True)
                    last = e.matmul(aps[half], ones_sb[0:1, :],
                                    bl[:, 512 * half:512 * half + 512],
                                    start=False, stop=True, skip_group_check=True)
                return last
            waits = [(cSc, sxn)] + bs.acquire()
            if t == 0:
                waits += [vw(gscale_done), vw(vbo_rep)]
            tp_ = sch.op("tensor", f_p, waits=waits, inc=cT)
            dst_f = qkf_q if nm == 'q' else qkf_k
            w_rp = rope_qkf_done.get((t - 1, nm))
            for half in range(2):
                vv = sch.op("scalar", lambda e, half=half, dst_f=dst_f,
                            ap=aps[half]: e.copy(
                    dst_f[:, 512 * half:512 * half + 512], ap),
                    waits=[(cT, tp_),
                           (None if w_rp is None else (cP, w_rp))], inc=cSc)
                if half == 1:
                    qk_vv_t[(t, nm)] = vv
                    bs.consumed(cSc, vv)
        if t + 1 < NT:
            emit_ln_norm(t + 1)
        if t >= 1:
            emit_rope_tr(t - 1, 'q')
            emit_rope_tr(t - 1, 'k')

        w_slot_v = vab_cons.get(t - 2)

        def f_v(e, sl=sl):
            last = None
            for ci in range(8):
                for half in range(2):
                    e.matmul(bank_a[half].t[:], xnT[sl][:, 128 * ci:128 * ci + 128],
                             wv_sb[:, 1024 * ci + 512 * half:1024 * ci + 512 * half + 512],
                             start=(ci == 0), stop=False, skip_group_check=True)
            for half in range(2):
                e.matmul(bank_a[half].t[:], ones_sb[0:1, :],
                         bvh_row[:, 512 * half:512 * half + 512],
                         start=False, stop=False, skip_group_check=True)
                last = e.matmul(bank_a[half].t[:], ones_sb[0:1, :],
                                bvl_row[:, 512 * half:512 * half + 512],
                                start=False, stop=True, skip_group_check=True)
            return last
        tv = sch.op("tensor", f_v,
                    waits=[(cSc, sxn)] + bank_a[0].acquire() + bank_a[1].acquire(), inc=cT)
        xnT_cons[t] = tv
        for half in range(2):
            vv = sch.op("scalar", lambda e, half=half, sl=sl: e.copy(
                vab[sl][:].rearrange("p (h c) -> p h c", c=65)[:, 8 * half:8 * half + 8, 0:64],
                bank_a[half].t[:].rearrange("p (h d) -> p h d", d=64)),
                waits=[(cT, tv),
                       (None if w_slot_v is None else (cVw[sl], w_slot_v))], inc=cSc)
            bank_a[half].consumed(cSc, vv)
        vab_ev = vv
        if t < 2:
            vones = sch.op("vector", lambda e, sl=sl: e.memset(
                vab[sl][:].rearrange("p (h c) -> p h c", c=65)[:, :, 64:65], 1.0), inc=cV)
            vab_cons[t] = sch.op("sync", lambda e, sl=sl, t=t: e.dma_start(
                out=vau_d[:, 1040 * t:1040 * (t + 1)], in_=vab[sl][:]),
                waits=[(cV, vones), (cSc, vab_ev)], inc=cVw[sl], inc_n=16)
        else:
            vab_cons[t] = sch.op("sync", lambda e, sl=sl, t=t: e.dma_start(
                out=vau_d[:, 1040 * t:1040 * (t + 1)], in_=vab[sl][:]),
                waits=[(cSc, vab_ev)], inc=cVw[sl], inc_n=16)

    emit_rope_muls(NT - 1)
    emit_rope_dve(NT - 1, 'q')
    emit_rope_dve(NT - 1, 'k')
    emit_rope_tr(NT - 1, 'q')
    emit_rope_tr(NT - 1, 'k')

    e1 = sch.op("vector", lambda e: e.reduce_max(nrm3_sb[:, 0:1], qkmax_sb[:], axis=AX.X),
                waits=[(cV, red_done[(NT - 1, 'k')])], inc=cV)
    e2 = sch.op("vector", lambda e: e.memzero(nrm2_sb[:]), waits=[(cV, e1)], inc=cV)
    e3 = sch.op("vector", lambda e: e.tensor_copy(nrm2_sb[:, 0:1], nrm3_sb[:, 0:1]),
                waits=[(cV, e2), (cV, e1)], inc=cV)
    te = sch.op("tensor", lambda e: e.transpose(bank_t.t[:, 0:128], nrm2_sb[:], id_sb[:]),
                waits=[vw(e3)] + bank_t.acquire(), inc=cT)
    se = sch.op("scalar", lambda e: e.copy(nrm2_sb[:], bank_t.t[:, 0:128]),
                waits=[(cT, te)], inc=cSc)
    bank_t.consumed(cSc, se)
    e4 = sch.op("vector", lambda e: e.tensor_copy(sq_sb[0:1, 0:128], nrm2_sb[0:1, :]),
                waits=[(cSc, se)], inc=cV)
    e5 = sch.op("vector", lambda e: e.reduce_max(nrm3_sb[0:1, 1:2], sq_sb[0:1, 0:128], axis=AX.X),
                waits=[(cV, e4)], inc=cV)
    e6 = sch.op("vector", lambda e: e.tensor_scalar(nrm3_sb[0:1, 2:3], nrm3_sb[0:1, 1:2],
                                                    -0.1275, None, ALU.mult),
                waits=[(cV, e5)], inc=cV)
    e7 = sch.op("vector", lambda e: e.tensor_copy(mzc_sb[0:1, 0:1], nrm3_sb[0:1, 2:3]),
                waits=[(cV, e6)], inc=cV)
    tneg = sch.op("tensor", lambda e: e.matmul(
        ps_rp[:, 0:1], ones_sb[0:64, 0:128], mzc_sb[0:64, 0:1], start=True, stop=True,
        skip_group_check=True),
        waits=[vw(e7)] + bank_r.acquire(), inc=cT)
    negc_done = sch.op("vector", lambda e: e.tensor_copy(negc_sb[:], ps_rp[:, 0:1]),
                       waits=[(cT, tneg)], inc=cV)
    bank_r.consumed(cV, negc_done)
    if DEBUG_IO:
        dbg_d = nc.dram_tensor("dbg_s", [128, 40], F32, kind="ExternalOutput").ap()
        sch.op("vector", lambda e: e.tensor_copy(sq_sb[:, 200:201], negc_sb[:]),
               waits=[(cV, negc_done)])
        dv = sch.op("vector", lambda e: e.tensor_copy(sq_sb[:, 201:233], qkmax_sb[:]), inc=cV)
        sch.op("gpsimd", lambda e: e.dma_start(out=dbg_d[:, 0:1], in_=sq_sb[:, 200:201], allow_slow_non_contiguous=True),
               waits=[(cV, dv)], inc=cVw[0], inc_n=16)
        sch.op("gpsimd", lambda e: e.dma_start(out=dbg_d[:, 1:33], in_=sq_sb[:, 201:233], allow_slow_non_contiguous=True),
               inc=cVw[0], inc_n=16)

    dma_chansA = [cSyStart, cBo, cWq, cWk, cWv, cWtmp, cWTw, cX[0], cX[1],
                  cTab[0], cTab[1], cQw, cKw, cVw[0], cVw[1]]
    barA = [(c0, c0.n) for c0 in dma_chansA] + [(cSc, cSc.n), (cV, cV.n), (cT, cT.n)]

    cVau = sch.chan("cVau")
    for i in range(4):
        sch.op("sync", lambda e, i=i: e.dma_start(
            out=ktr_sb[:, 8192 * i:8192 * (i + 1)], in_=ktr_d[:, 8192 * i:8192 * (i + 1)]),
            waits=barA, inc=cKV, inc_n=16)
    for i in range(4):
        sch.op("scalar", lambda e, i=i: e.dma_start(
            out=vau_sb[:, 8320 * i:8320 * (i + 1)], in_=vau_d[:, 8320 * i:8320 * (i + 1)]),
            waits=(barA if i == 0 else []), inc=cVau, inc_n=16)
    kv_done = cKV.n

    qtr_ld = {}
    S_last_of_group = {}
    div_vals = []
    atw_vals = {}
    u = 0
    av_pair = []
    av_banks = [bank_a[0], bank_a[1], bank_t]
    av_aps = [ps_a[0][:], ps_a[1][:], ps_tr_f32]
    pend_epi = []
    epi_state = {"rc_last_cons": None, "rcbf_last_T": None}

    def emit_qtr_ld(g):
        w_slot = (cT, S_last_of_group[g - 2]) if g - 2 in S_last_of_group else None
        qtr_ld[g] = sch.op("sync", lambda e, g=g: e.dma_start(
            out=qtr_sb[g % 2][:].rearrange("p (di i) -> p di i", i=512),
            in_=qtr_d.rearrange("p (di c) -> p di c", c=L)[:, :, 512 * g:512 * g + 512]),
            waits=[w_slot], inc=cQtr[g % 2], inc_n=16)

    for g in range(NG):
        if g == 0:
            emit_qtr_ld(0)
        if g + 1 < NG:
            emit_qtr_ld(g + 1)

        nkt = 4 * g + 4
        kt_order = list(range(4 * g, 4 * g + 4)) + list(range(0, 4 * g))
        for hp in range(8):
            pend_av = []
            for ki, kt in enumerate(kt_order):
                sb = bank_s[u % 2]

                def f_s(e, hp=hp, kt=kt, g=g, sb=sb):
                    e.matmul(sb.t[:, 0:512],
                             ktr_sb[0:64, 4096 * hp + 128 * kt:4096 * hp + 128 * kt + 128],
                             qtr_sb[g % 2][0:64, 512 * hp:512 * hp + 512],
                             start=True, stop=True, skip_group_check=True, tile_position=(0, 0))
                    return e.matmul(sb.t[:, 512:1024],
                                    ktr_sb[64:128, 4096 * hp + 128 * kt:4096 * hp + 128 * kt + 128],
                                    qtr_sb[g % 2][64:128, 512 * hp:512 * hp + 512],
                                    start=True, stop=True, skip_group_check=True,
                                    tile_position=(64, 0))
                waits = [(cQtr[g % 2], qtr_ld[g]), (cKV, 16 * (hp // 2 + 1))] + sb.acquire()
                if u == 0:
                    waits += [vw(negc_done)]
                ts_ = sch.op("tensor", f_s, waits=waits, inc=cT)
                if ki == nkt - 1 and hp == 7:
                    S_last_of_group[g] = ts_

                pslot = u % 4
                w_pt = (cT, av_pair[u - 4][1]) if u >= 4 else None
                ex = sch.op("scalar", lambda e, sb=sb, pslot=pslot: e.activation(
                    ptb_bufs[pslot][:], sb.t[:], AF.Exp, bias=negc_sb[:, 0:1], scale=0.125),
                    waits=[(cT, ts_), w_pt], inc=cSc)
                sb.consumed(cSc, ex)

                m = kt - 4 * g
                use_w_a = use_w_b = (cSc, ex)
                if 0 <= m <= 3:
                    mo = 384 - 128 * m
                    vma = sch.op("vector", lambda e, pslot=pslot, mo=mo: e.tensor_mul(
                        ptb_bufs[pslot][:, 0:512], ptb_bufs[pslot][:, 0:512],
                        mask_sb[:, mo:mo + 512]), waits=[(cSc, ex)], inc=cV)
                    vmb = sch.op("vector", lambda e, pslot=pslot, mo=mo: e.tensor_mul(
                        ptb_bufs[pslot][:, 512:1024], ptb_bufs[pslot][:, 512:1024],
                        mask_sb[:, mo:mo + 512]), waits=[(cSc, ex)], inc=cV)
                    use_w_a = (cV, vma)
                    use_w_b = (cV, vmb)

                def emit_av(ki2, kt2, pslot2, uwa, uwb, hp=hp, g=g, nkt=nkt):
                    tas = []
                    for a in range(2):
                        h = 2 * hp + a
                        ab = av_banks[h % 3]
                        abt = av_aps[h % 3]

                        def f_av(e, kt2=kt2, h=h, pslot2=pslot2, a=a, abt=abt,
                                 ki2=ki2, nkt=nkt):
                            return e.matmul(abt[0:65, :],
                                            vau_sb[:, 1040 * kt2 + 65 * h:1040 * kt2 + 65 * h + 65],
                                            ptb_bufs[pslot2][:, 512 * a:512 * a + 512],
                                            start=(ki2 == 0), stop=(ki2 == nkt - 1),
                                            skip_group_check=True)
                        aw = [uwa if a == 0 else uwb, (cVau, 16 * (kt2 // 8 + 1))]
                        if ki2 == 0:
                            hg = 16 * g + h
                            if hg >= 3:
                                aw.append((cV, div_vals[hg - 3]))
                            aw += ab.acquire()
                        ta = sch.op("tensor", f_av, waits=aw, inc=cT)
                        tas.append(ta)
                    av_pair.append(tuple(tas))

                if ki == 2 and pend_epi:
                    for f_ in pend_epi:
                        f_()
                    pend_epi.clear()
                if len(pend_av) == 2:
                    emit_av(*pend_av.pop(0))
                pend_av.append((ki, kt, pslot, use_w_a, use_w_b))
                u += 1
            for p_ in pend_av:
                emit_av(*p_)
            pend_av = []

            for a in range(2):
                h = 2 * hp + a
                hg = 16 * g + h
                ab = av_banks[h % 3]
                abt = av_aps[h % 3]
                av_done = av_pair[-1][a]

                def epi(a=a, h=h, hg=hg, ab=ab, abt=abt, av_done=av_done, g=g):
                    sr1 = sch.op("scalar", lambda e, abt=abt: e.activation(
                        recip_sb[64:65, :], abt[64:65, :], AF.Ln),
                        waits=[(cT, av_done), (None if epi_state["rc_last_cons"] is None
                                               else (cSc, epi_state["rc_last_cons"]))], inc=cSc)
                    sr2 = sch.op("scalar", lambda e: e.activation(
                        rcbf_sb[64:65, :], recip_sb[64:65, :], AF.Exp, scale=-1.0),
                        waits=[(None if epi_state["rcbf_last_T"] is None
                                else (cT, epi_state["rcbf_last_T"]))], inc=cSc)
                    epi_state["rc_last_cons"] = sr2
                    tr_ = sch.op("tensor", lambda e: e.matmul(
                        bank_r.t[0:64, :], ones_sb[64:128, 0:64], rcbf_sb[64:128, :],
                        start=True, stop=True, skip_group_check=True),
                        waits=[(cSc, sr2)] + bank_r.acquire(), inc=cT)
                    epi_state["rcbf_last_T"] = tr_
                    ve = sch.op("vector", lambda e: e.tensor_copy(
                        repev_sb[0:64, :], bank_r.t[0:64, :]),
                        waits=[(cT, tr_)], inc=cV)
                    bank_r.consumed(cV, ve)
                    w_at = (cAtnW, atw_vals[g - 1]) if (h == 0 and g - 1 in atw_vals) else None
                    vd_ = sch.op("vector", lambda e, h=h, abt=abt: e.tensor_mul(
                        atn_sb[0:64, 512 * h:512 * h + 512], abt[0:64, :], repev_sb[0:64, :]),
                        waits=[w_at], inc=cV)
                    ab.consumed(cV, vd_)
                    div_vals.append(vd_)
                pend_epi.append(epi)
            if hp == 7:
                for f_ in pend_epi:
                    f_()
                pend_epi.clear()

        sch.op("gpsimd", lambda e, g=g: e.dma_start(
            out=atn_d.rearrange("(par p) (g2 h2 i) -> p par g2 h2 i",
                                par=2, h2=8, i=512)[:, 0, g, :, :],
            in_=atn_sb[0:64, :].rearrange("p (h2 par i) -> p par h2 i",
                                          par=2, i=512)[:, 0, :, :]),
            waits=[(cV, div_vals[16 * g + 15])], inc=cAtnW, inc_n=16)
        atw_vals[g] = sch.op("gpsimd", lambda e, g=g: e.dma_start(
            out=atn_d.rearrange("(par p) (g2 h2 i) -> p par g2 h2 i",
                                par=2, h2=8, i=512)[:, 1, g, :, :],
            in_=atn_sb[0:64, :].rearrange("p (h2 par i) -> p par h2 i",
                                          par=2, i=512)[:, 1, :, :]),
            inc=cAtnW, inc_n=16)

    dma_chansB = dma_chansA + [cKV, cVau, cQtr[0], cQtr[1], cAtnW]
    barB = [(c0, c0.n) for c0 in dma_chansB] + [(cSc, cSc.n), (cV, cV.n), (cT, cT.n)]

    sch.op("gpsimd", lambda e: e.dma_start(
        out=wo_sb[:].rearrange("p (ci d) -> p ci d", d=1024), in_=wo_d.rearrange("(ci p) d -> p ci d", p=128)),
        waits=barB, inc=cWoG, inc_n=16)
    for wsb, wdd in ((wrT_sb, wrT_d), (wkT_sb, wkT_d), (wvT_sb, wvT_d)):
        sch.op("sync", lambda e, wsb=wsb, wdd=wdd: e.dma_start(out=wsb[:], in_=wdd),
               waits=barB, inc=cWc, inc_n=16)
    wC_sy = cWc.n
    wC_g = cWoG.n

    wscale_done = None
    for wsrc, wdst, musb in ((wkT_sb, wkTm_sb, muk_sb), (wrT_sb, wrTm_sb, mur_sb)):
        for ci2 in range(8):
            wscale_done = sch.op("vector", lambda e, wsrc=wsrc, wdst=wdst, musb=musb,
                                 ci2=ci2: e.tensor_scalar(
                wdst[:, 1024 * ci2:1024 * ci2 + 1024],
                wsrc[:, 1024 * ci2:1024 * ci2 + 1024],
                musb[:, ci2:ci2 + 1], None, ALU.mult),
                waits=[(cWc, wC_sy)] + (barB if ci2 == 0 else []), inc=cV)

    atc_ld = {}
    xc_ld = {}
    xcb_cons = {}
    bank_rC = [PBank(None), PBank(None)]
    bank_kvC = [PBank(None), PBank(None)]
    pend_trz = []
    trz_T = {}
    z1bf2 = [z1bf_sb, z1bf_b2]
    bank_rC[0].last = bank_s[0].last
    bank_rC[1].last = bank_s[0].last
    bank_kvC[0].last = bank_s[1].last
    bank_kvC[1].last = bank_s[1].last
    r_aps = [ps_s[0][:][:, 0:512], ps_s[0][:][:, 512:1024]]
    kv_aps = [ps_s[1][:][:, 0:512], ps_s[1][:][:, 512:1024]]
    out_st = {}
    y_done = {}
    z1T_evac = {}
    wo_last = {}
    grp_end_T = {}
    r_cons = {}
    kact_cons = None
    z1padd = {}
    d1_done = {}

    def emit_atc_ld(g):
        w_slot = (cT, wo_last[g - 2]) if g - 2 in wo_last else None
        atc_ld[g] = sch.op("sync", lambda e, g=g: e.dma_start(
            out=atc_sb[g % 2][:], in_=atn_d[:, 4096 * g:4096 * (g + 1)]),
            waits=[w_slot, (cWc, wC_sy)], inc=cAtc[g % 2], inc_n=16)

    for g in range(NG):
        if g == 0:
            emit_atc_ld(0)
        if g + 1 < NG:
            emit_atc_ld(g + 1)

        if g == 0:
            vh = sch.op("vector", lambda e: e.memset(
                z1T_sb[:].rearrange("p (ci w) -> p ci w", w=513)[:, :, 0:1], 0.0),
                waits=barB, inc=cV)
        else:
            vh = sch.op("vector", lambda e: e.tensor_copy(
                z1T_sb[:].rearrange("p (ci w) -> p ci w", w=513)[:, :, 0:1],
                z1T_sb[:].rearrange("p (ci w) -> p ci w", w=513)[:, :, 512:513]),
                waits=[(cSc, z1T_evac[4 * g - 1]), (cT, grp_end_T[g - 1])], inc=cV)
        vh_g = vh

        def emit_d1(tt, g=g, vh_g=vh_g):
            w = [(cSc, z1T_evac[4 * g + tt])]
            if tt == 0:
                w.append(vw(vh_g))
                if g >= 1:
                    w.append((cT, grp_end_T[g - 1]))
            d1_done[(g, tt)] = sch.op("vector", lambda e, tt=tt: e.tensor_sub(
                d1_sb[:].rearrange("p (ci w) -> p ci w", w=512)[:, :, 128 * tt:128 * tt + 128],
                z1T_sb[:].rearrange("p (ci w) -> p ci w", w=513)[:, :, 1 + 128 * tt:129 + 128 * tt],
                z1T_sb[:].rearrange("p (ci w) -> p ci w", w=513)[:, :, 128 * tt:128 * tt + 128]),
                waits=w, inc=cV)

        for tt in range(GT):
            ct = 4 * g + tt
            xc_ld[ct] = sch.op("scalar", lambda e, ct=ct: e.dma_start(
                out=xcb[ct % 2][:], in_=x_d[128 * ct:128 * (ct + 1), :]),
                waits=[vw(xcb_cons.get(ct - 2))], inc=cXc[ct % 2], inc_n=16)
            for half in range(2):
                bank = bank_a[half]

                def f(e, g=g, tt=tt, half=half, bank=bank):
                    for ci2 in range(8):
                        e.matmul(
                            bank.t[:],
                            atc_sb[g % 2][:, 512 * ci2 + 128 * tt:512 * ci2 + 128 * tt + 128],
                            wo_sb[:, 1024 * ci2 + 512 * half:1024 * ci2 + 512 * half + 512],
                            start=(ci2 == 0), stop=False, skip_group_check=True)
                    e.matmul(bank.t[:], ones_sb[0:1, :],
                             boh_row[:, 512 * half:512 * half + 512],
                             start=False, stop=False, skip_group_check=True)
                    return e.matmul(bank.t[:], ones_sb[0:1, :],
                                    bol_row[:, 512 * half:512 * half + 512],
                                    start=False, stop=True, skip_group_check=True)
                two = sch.op("tensor", f,
                             waits=[(cAtc[g % 2], atc_ld[g]), (cWoG, wC_g)] + bank.acquire(), inc=cT)
                if half == 1:
                    wo_last[g] = two
                vvx = sch.op("vector", lambda e, half=half, ct=ct, bank=bank: e.scalar_tensor_tensor(
                    ysb[:, 512 * half:512 * half + 512],
                    xcb[ct % 2][:, 512 * half:512 * half + 512], 2.0, bank.t[:],
                    ALU.mult, ALU.add),
                    waits=[(cT, two), (cXc[ct % 2], xc_ld[ct])], inc=cV)
                if half == 1:
                    xcb_cons[ct] = vvx
                bank.consumed(cV, vvx)
            sl2 = ct % 2
            s1, s2 = emit_ln(sl2, ysb[:], extra_wait=(cV, vvx))
            vz = sch.op("vector", lambda e, tt=tt, sl2=sl2: e.tensor_scalar(
                z1_sb[:, 1024 * tt:1024 * tt + 1024], ysb[:], mvc_2[sl2][:, 0:1], rs_2[sl2][:],
                ALU.subtract, ALU.mult),
                waits=[(cSc, s2),
                       (None if tt not in z1padd else (cP, z1padd[tt]))], inc=cV)
            ln_used(sl2, vz)
            zb = sch.op("scalar", lambda e, tt=tt: e.copy(
                z1bf2[tt % 2][:], z1_sb[:, 1024 * tt:1024 * tt + 1024]),
                waits=[(cV, vz),
                       (None if (ct - 2) not in trz_T else (cT, trz_T[ct - 2]))], inc=cSc)

            def mk_trz(tt=tt, ct=ct, zb=zb):
                def doit():
                    def f_trz(e, sl2=tt % 2):
                        last = None
                        for j in range(8):
                            last = e.transpose(bank_t.t[:, 128 * j:128 * j + 128],
                                               z1bf2[sl2][:, 128 * j:128 * j + 128], id_sb[:])
                        return last
                    tz = sch.op("tensor", f_trz, waits=[(cSc, zb)] + bank_t.acquire(), inc=cT)
                    trz_T[ct] = tz
                    sz = sch.op("scalar", lambda e, tt=tt: e.copy(
                        z1T_sb[:].rearrange("p (ci w) -> p ci w", w=513)[:, :, 1 + 128 * tt:129 + 128 * tt],
                        bank_t.t[:].rearrange("p (ci j) -> p ci j", j=128)),
                        waits=[(cT, tz), vw(vh)], inc=cSc)
                    bank_t.consumed(cSc, sz)
                    z1T_evac[ct] = sz
                    emit_d1(tt)
                return doit
            pend_trz.append(mk_trz())
            if len(pend_trz) > 1:
                pend_trz.pop(0)()

        for f_ in pend_trz:
            f_()
        pend_trz.clear()
        d1_last = d1_done[(g, 3)]

        z1Tv_g = z1T_sb[:].rearrange("p (ci w) -> p ci w", w=513)
        d1v_g = d1_sb[:].rearrange("p (ci w) -> p ci w", w=512)
        sq_vals = {}
        for di in range(8):
            bank = bank_a[di % 2]

            def f(e, di=di, bank=bank):
                last = None
                for ci2 in range(8):
                    e.matmul(bank.t[:],
                             wkT_sb[:, 1024 * ci2 + 128 * di:1024 * ci2 + 128 * di + 128],
                             z1Tv_g[:, ci2, 0:512],
                             start=(ci2 == 0), stop=False, skip_group_check=True)
                    last = e.matmul(bank.t[:],
                                    wkTm_sb[:, 1024 * ci2 + 128 * di:1024 * ci2 + 128 * di + 128],
                                    d1v_g[:, ci2, 0:512],
                                    start=False, stop=(ci2 == 7), skip_group_check=True)
                return last
            kw = [(cV, d1_last)] + bank.acquire()
            if g == 0:
                kw.append((cV, wscale_done))
            tk = sch.op("tensor", f, waits=kw, inc=cT)
            sk = sch.op("scalar", lambda e, bank=bank: e.activation(kact_t[:], bank.t[:], AF.Relu),
                        waits=[(cT, tk)], inc=cSc)
            bank.consumed(cSc, sk)
            sq_vals[di] = sch.op("scalar", lambda e, di=di: e.activation(
                kactT_sb[:, 512 * di:512 * di + 512], kact_t[:], AF.Square),
                waits=[(None if g == 0 else (cT, grp_end_T[g - 1]))], inc=cSc)
        kact_done = sq_vals[7]

        for tt in range(GT):
            ct = 4 * g + tt
            for half in range(2):

                def f_r(e, tt=tt, half=half, rap=r_aps[half]):
                    last = None
                    for ci2 in range(8):
                        e.matmul(
                            rap,
                            z1Tv_g[:, ci2, 128 * tt:128 * tt + 128],
                            wrT_sb[:, 1024 * ci2 + 512 * half:1024 * ci2 + 512 * half + 512],
                            start=(ci2 == 0), stop=False, skip_group_check=True)
                        last = e.matmul(
                            rap,
                            d1v_g[:, ci2, 128 * tt:128 * tt + 128],
                            wrTm_sb[:, 1024 * ci2 + 512 * half:1024 * ci2 + 512 * half + 512],
                            start=False, stop=(ci2 == 7), skip_group_check=True)
                    return last
                rb = bank_rC[half]
                rap = r_aps[half]
                rw = [(cV, d1_done[(g, tt)])] + rb.acquire()
                if g == 0 and tt == 0:
                    rw.append((cV, wscale_done))
                tr_ = sch.op("tensor", f_r, waits=rw, inc=cT)
                sa = sch.op("scalar", lambda e, rap=rap: e.activation(
                    sgt_sb[:], rap, AF.Exp, scale=-1.0),
                    waits=[(cT, tr_), vw(r_cons.get((ct - 1, half)) if tt > 0 or g > 0 else None)],
                    inc=cSc)
                rb.consumed(cSc, sa)
                sch.op("scalar", lambda e: e.activation(sgt_sb[:], sgt_sb[:], AF.Ln, bias=onec_sb[:, 0:1]))
                sc_ = sch.op("scalar", lambda e, half=half: e.activation(
                    r_sb[:, 512 * half:512 * half + 512], sgt_sb[:], AF.Exp, scale=-1.0), inc=cSc)

                def f_kv(e, tt=tt, half=half, kap=kv_aps[half]):
                    last = None
                    for ci2 in range(8):
                        last = e.matmul(
                            kap,
                            kactT_sb[:, 512 * ci2 + 128 * tt:512 * ci2 + 128 * tt + 128],
                            wvT_sb[:, 1024 * ci2 + 512 * half:1024 * ci2 + 512 * half + 512],
                            start=(ci2 == 0), stop=(ci2 == 7), skip_group_check=True)
                    return last
                kvb = bank_kvC[half]
                tkv = sch.op("tensor", f_kv,
                             waits=[(cSc, kact_done)] + kvb.acquire(), inc=cT)
                vp = sch.op("vector", lambda e, half=half, kap=kv_aps[half]: e.tensor_mul(
                    y2_sb[:, 512 * half:512 * half + 512], kap,
                    r_sb[:, 512 * half:512 * half + 512]),
                    waits=[(cT, tkv), (cSc, sc_)], inc=cV)
                kvb.consumed(cV, vp)
                r_cons[(ct, half)] = vp
                pad = sch.op("gpsimd", lambda e, half=half, tt=tt: e.tensor_add(
                    y2_sb[:, 512 * half:512 * half + 512], y2_sb[:, 512 * half:512 * half + 512],
                    z1_sb[:, 1024 * tt + 512 * half:1024 * tt + 512 * half + 512]),
                    waits=[(cV, vp)], inc=cP)
                z1padd[tt] = pad
            sl2 = ct % 2
            s1, s2 = emit_ln(sl2, y2_sb[:], extra_wait=(cP, pad))
            w_ob = (cOut[(ct - 1) % 2], out_st[ct - 1]) if (ct - 1) in out_st else None
            vo = sch.op("vector", lambda e, ct=ct, sl2=sl2: e.tensor_scalar(
                outb[ct % 2][:], y2_sb[:], mvc_2[sl2][:, 0:1], rs_2[sl2][:],
                ALU.subtract, ALU.mult),
                waits=[(cSc, s2), w_ob], inc=cV)
            ln_used(sl2, vo)
            y_done[ct] = vo
            out_st[ct] = sch.op("gpsimd", lambda e, ct=ct: e.dma_start(
                out=out_d[128 * ct:128 * (ct + 1), :], in_=outb[ct % 2][:]),
                waits=[(cV, vo)], inc=cOut[ct % 2], inc_n=16)
        grp_end_T[g] = cT.n
        kact_cons = r_cons[(4 * g + 3, 1)]

    sch.op("gpsimd", lambda e: e.nop(), waits=[(cOut[0], cOut[0].n), (cOut[1], cOut[1].n)])

    sch.emit()
    stack.close()
    return nc


_prog_cache = {}


def _get_prog():
    if "nc" not in _prog_cache:
        _prog_cache["nc"] = build_program()
    return _prog_cache["nc"]


def _host_consts():
    inv_freq = 1.0 / (10000.0 ** (np.arange(0, DH, 2, dtype=np.float64) / DH))
    tpos = np.arange(L, dtype=np.float64)[:, None] * inv_freq[None, :]
    cos_rep = np.tile(np.cos(tpos), (1, H)).astype(np.float32)
    sin_rep = np.tile(np.sin(tpos), (1, H)).astype(np.float32)
    tab = np.concatenate([cos_rep, sin_rep, -sin_rep], axis=1)
    ident = np.eye(128, dtype=ml_dtypes.bfloat16)
    jj = np.arange(128)[:, None]
    cc = np.arange(896)[None, :]
    maskt = (jj <= cc - 384).astype(ml_dtypes.bfloat16)
    return np.ascontiguousarray(tab), ident, np.ascontiguousarray(maskt)


def kernel(L0_attn_x, L0_attn_ln_gamma, L0_attn_ln_beta, L0_attn_Wq, L0_attn_bq,
           L0_attn_Wk, L0_attn_bk, L0_attn_Wv, L0_attn_bv, L0_attn_Wo, L0_attn_bo,
           L0_ffn_W_r, L0_ffn_W_k, L0_ffn_W_v, L0_ffn_mu_r, L0_ffn_mu_k):
    x = np.ascontiguousarray(np.asarray(L0_attn_x, dtype=np.float32))
    tab, ident, maskt = _host_consts()
    common = dict(
        wq=np.ascontiguousarray(np.asarray(L0_attn_Wq, np.float32)),
        wk=np.ascontiguousarray(np.asarray(L0_attn_Wk, np.float32)),
        wv=np.ascontiguousarray(np.asarray(L0_attn_Wv, np.float32)),
        wo=np.ascontiguousarray(np.asarray(L0_attn_Wo, np.float32)),
        wr=np.ascontiguousarray(np.asarray(L0_ffn_W_r, np.float32)),
        wfk=np.ascontiguousarray(np.asarray(L0_ffn_W_k, np.float32)),
        wfv=np.ascontiguousarray(np.asarray(L0_ffn_W_v, np.float32)),
        gam=np.ascontiguousarray(np.asarray(L0_attn_ln_gamma, np.float32)),
        bet=np.ascontiguousarray(np.asarray(L0_attn_ln_beta, np.float32)),
        bq=np.ascontiguousarray(np.asarray(L0_attn_bq, np.float32).reshape(1, D)),
        bk=np.ascontiguousarray(np.asarray(L0_attn_bk, np.float32).reshape(1, D)),
        bv=np.ascontiguousarray(np.asarray(L0_attn_bv, np.float32).reshape(1, D)),
        bo=np.ascontiguousarray(np.asarray(L0_attn_bo, np.float32).reshape(1, D)),
        mur=np.ascontiguousarray(np.asarray(L0_ffn_mu_r, np.float32)),
        muk=np.ascontiguousarray(np.asarray(L0_ffn_mu_k, np.float32)),
        tab=tab, ident=ident, maskt=maskt,
    )
    in_maps = [dict(common, x=np.ascontiguousarray(x[b])) for b in range(8)]
    nc = _get_prog()
    res = run_bass_kernel_spmd(nc, in_maps, core_ids=list(range(8)))
    return np.stack([res.results[b]["out"] for b in range(8)], axis=0)
